# version 04
# baseline (speedup 1.0000x reference)
import contextlib
import numpy as np
import ml_dtypes
import concourse.bass as bass
import concourse.mybir as mybir
from concourse.bass_utils import run_bass_kernel_spmd

F32 = mybir.dt.float32
BF16 = mybir.dt.bfloat16
AF = mybir.ActivationFunctionType
ALU = mybir.AluOpType
AX = mybir.AxisListType

T = 2048
NT = T // 128
D = 1024
NDK = D // 128
PI = float(np.pi)
ENGINES = ("sync", "vector", "scalar", "tensor", "gpsimd")
GROUPS = [[0, 1, 2, 3], [4, 5, 6, 7]]
BF = ml_dtypes.bfloat16


class Prog:

    def __init__(self, nc, es):
        self.nc = nc
        self.es = es
        self.cmds = {e: [] for e in ENGINES}
        self.sem = {}
        self.cnt = {}
        self.tiles = {}
        for s in ("dma", "st", "mm", "v", "s", "g", "cc"):
            self.ensure_sem(s)

    def ensure_sem(self, name):
        if name not in self.sem:
            self.sem[name] = self.es.enter_context(self.nc.semaphore(name))
            self.cnt[name] = 0
        return self.sem[name]

    def bump(self, name, n):
        self.cnt[name] += n
        return self.cnt[name]

    def op(self, eng, fn, waits=(), incs=()):
        totals = [(self.ensure_sem(s), self.bump(s, n)) for (s, n) in incs]
        wl = [(self.ensure_sem(s), v) for (s, v) in waits if v > 0]

        def cmd(e):
            for (s, v) in wl:
                e.wait_ge(s, v)
            inst = fn(e)
            for i, (s, _tot) in enumerate(totals):
                inst = inst.then_inc(s, incs[i][1])
        self.cmds[eng].append(cmd)
        return [t for (_s, t) in totals]

    def wait(self, eng, waits):
        wl = [(self.ensure_sem(s), v) for (s, v) in waits if v > 0]

        def cmd(e):
            for (s, v) in wl:
                e.wait_ge(s, v)
        self.cmds[eng].append(cmd)

    def sbuf(self, name, shape, dtype):
        t = self.es.enter_context(self.nc.sbuf_tensor(name, list(shape), dtype))
        self.tiles[name] = t
        return t

    def psum(self, name, shape=(128, 512), dtype=F32):
        t = self.es.enter_context(self.nc.psum_tensor(name, list(shape), dtype))
        self.tiles[name] = t
        return t

    def finalize(self):
        with self.nc.Block() as block:
            for eng in ENGINES:
                cmds = self.cmds[eng]
                if not cmds:
                    continue
                getattr(block, eng)(lambda e, cmds=cmds: [c(e) for c in cmds])


CONSTS = {"eps": (0, 1e-5), "halfpi": (1, PI / 2), "neg1": (2, -1.0), "one": (3, 1.0)}


def alloc_common(P):
    P.sbuf("const_sb", (128, 8), F32)
    P.sbuf("x_sb", (128, NT * D), F32)
    P.sbuf("xln_sb", (128, NT * D), BF16)
    P.sbuf("xf_sb", (128, NDK * T), BF16)
    P.sbuf("stat_sb", (128, NT * 8), F32)
    P.sbuf("ident_sb", (128, 128), BF16)
    P.sbuf("id32_sb", (128, 128), F32)
    P.sbuf("onesbf_sb", (1, 128), BF16)
    for i in range(4):
        P.psum(f"ps{i}")
    P.psum("pt0", (128, 1024), BF16)
    P.psum("pt1", (128, 1024), BF16)
    P.psum("psA")
    P.psum("psB")


def cst(P, name):
    return P.tiles["const_sb"][:, CONSTS[name][0]:CONSTS[name][0] + 1]


def emit_load_consts(P, ident_ap, onesbf_ap):
    P.op("sync", lambda e: e.dma_start(out=P.tiles["ident_sb"][:], in_=ident_ap),
         incs=[("dma", 16)])
    P.op("sync", lambda e: e.dma_start(out=P.tiles["onesbf_sb"][:], in_=onesbf_ap),
         incs=[("dma", 16)])
    P.op("sync", lambda e: e.dma_start(out=P.tiles["id32_sb"][:], in_=P.ident32_ap),
         incs=[("dma", 16)])
    cs = P.tiles["const_sb"]
    for (_nm, (slot, val)) in CONSTS.items():
        P.op("vector", lambda e, slot=slot, val=val: e.memset(cs[:, slot:slot + 1], val))
    P.op("vector", lambda e: e.memset(cs[:, 7:8], 0.0), incs=[("v", 1)])
    P.wait("scalar", [("v", P.cnt["v"])])
    P.wait("gpsimd", [("v", P.cnt["v"])])


def emit_load_x(P, x_hbm):
    x_sb = P.tiles["x_sb"]
    P.op("sync", lambda e: e.dma_start(
        out=x_sb[:].rearrange("p (t d) -> p t d", t=NT),
        in_=x_hbm.rearrange("(t p) d -> p t d", p=128)),
        incs=[("dma", 16)])


def emit_store_x(P, x_hbm_out):
    x_sb = P.tiles["x_sb"]
    P.op("sync", lambda e: e.dma_start(
        out=x_hbm_out.rearrange("(t p) d -> p t d", p=128),
        in_=x_sb[:].rearrange("p (t d) -> p t d", t=NT)),
        waits=[("v", P.cnt["v"]), ("s", P.cnt["s"])],
        incs=[("st", 16)])
    P.wait("sync", [("st", P.cnt["st"])])


def emit_ln(P, x_ready=(), identity=False, gain=None, mamba_double=False,
            gain_ready=(), inplace=False):
    x_sb, xln, xf = P.tiles["x_sb"], P.tiles["xln_sb"], P.tiles["xf_sb"]
    st = P.tiles["stat_sb"]
    trash = xln
    sums = st[:, 0 * NT:1 * NT]
    sq = st[:, 1 * NT:2 * NT]
    mean = st[:, 2 * NT:3 * NT]
    var = st[:, 3 * NT:4 * NT]
    rstd = st[:, 4 * NT:5 * NT]
    nmr = st[:, 5 * NT:6 * NT]

    if not identity:
        P.wait("vector", x_ready)
        P.wait("scalar", x_ready)
        sq_base = P.cnt["s"]
        for t in range(NT):
            xt = x_sb[:, t * D:(t + 1) * D]
            P.op("vector", lambda e, xt=xt, t=t: e.tensor_reduce(
                sums[:, t:t + 1], xt, AX.X, ALU.add),
                incs=([("v", 1)] if t == NT - 1 else []))
            w = [("s", sq_base + t - 1)] if t >= 2 else []
            P.op("scalar", lambda e, xt=xt, t=t: e.activation(
                trash[:, (t % 2) * D:(t % 2) * D + D], xt, AF.Square,
                accum_out=sq[:, t:t + 1]), waits=w, incs=[("s", 1)])
        s_sq = P.cnt["s"]
        def vedge():
            P.wait("vector", [("v", P.cnt["v"])])
        P.op("vector", lambda e: e.tensor_scalar_mul(mean[:], sums[:], 1.0 / D),
             waits=[("v", P.cnt["v"])], incs=[("v", 1)])
        vedge()
        P.op("vector", lambda e: e.tensor_scalar_mul(var[:], sq[:], 1.0 / D),
             waits=[("s", s_sq)], incs=[("v", 1)])
        vedge()
        P.op("vector", lambda e: e.tensor_mul(sums[:], mean[:], mean[:]), incs=[("v", 1)])
        vedge()
        v_var = P.op("vector", lambda e: e.tensor_sub(var[:], var[:], sums[:]),
                     incs=[("v", 1)])[0]
        s_lnop = P.op("scalar", lambda e: e.activation(rstd[:], var[:], AF.Ln, bias=cst(P, "eps")),
             waits=[("v", v_var)], incs=[("s", 1)])[0]
        P.wait("scalar", [("s", s_lnop)])
        s_ln = P.op("scalar", lambda e: e.activation(rstd[:], rstd[:], AF.Exp, scale=-0.5),
                    incs=[("s", 1)])[0]
        P.op("vector", lambda e: e.scalar_tensor_tensor(
            nmr[:], mean[:], -1.0, rstd[:], ALU.mult, ALU.mult),
            waits=[("s", s_ln), ("v", P.cnt["v"])], incs=[("v", 1)])
        v_nmr = P.cnt["v"]
        rstd_a, nmr_a = rstd, nmr
        if mamba_double:
            rc = st[:, 6 * NT:7 * NT]
            nc2 = st[:, 7 * NT:8 * NT]

            def ved():
                P.wait("vector", [("v", P.cnt["v"])])
            P.op("vector", lambda e: e.tensor_mul(rc[:], rstd[:], rstd[:]),
                 waits=[("v", P.cnt["v"])], incs=[("v", 1)])
            ved()
            P.op("vector", lambda e: e.tensor_mul(rc[:], rc[:], var[:]), incs=[("v", 1)])
            v_t = P.cnt["v"]
            s_a = P.op("scalar", lambda e: e.activation(rc[:], rc[:], AF.Ln, bias=cst(P, "eps")),
                       waits=[("v", v_t)], incs=[("s", 1)])[0]
            P.wait("scalar", [("s", s_a)])
            s_b = P.op("scalar", lambda e: e.activation(rc[:], rc[:], AF.Exp, scale=-0.5),
                       incs=[("s", 1)])[0]
            P.op("vector", lambda e: e.tensor_mul(rc[:], rc[:], rstd[:]),
                 waits=[("s", s_b), ("v", P.cnt["v"])], incs=[("v", 1)])
            ved()
            P.op("vector", lambda e: e.scalar_tensor_tensor(
                nc2[:], mean[:], -1.0, rc[:], ALU.mult, ALU.mult), incs=[("v", 1)])
            v_nmr = P.cnt["v"]
            rstd_a, nmr_a = rc, nc2
        P.wait("scalar", [("v", v_nmr)])
        if inplace:
            for t in range(NT):
                P.op("scalar", lambda e, t=t: e.activation(
                    x_sb[:, t * D:(t + 1) * D], x_sb[:, t * D:(t + 1) * D],
                    AF.Identity, scale=rstd[:, t:t + 1], bias=nmr[:, t:t + 1]),
                    incs=[("s", 1)])
            return P.cnt["v"], P.cnt["s"]
        for t in range(NT):
            ic = [("s", 1)] if t == NT - 1 else []
            P.op("scalar", lambda e, t=t, rstd_a=rstd_a, nmr_a=nmr_a: e.activation(
                xln[:, t * D:(t + 1) * D], x_sb[:, t * D:(t + 1) * D], AF.Identity,
                scale=rstd_a[:, t:t + 1], bias=nmr_a[:, t:t + 1]), incs=ic)
        if mamba_double:
            s_apply = P.cnt["s"]
            u_t = P.tiles["mscr"][:, 0:D]
            P.wait("vector", [("s", s_apply)])
            for t in range(NT):
                P.op("scalar", lambda e, t=t, u_t=u_t: e.activation(
                    u_t, x_sb[:, t * D:(t + 1) * D], AF.Identity,
                    scale=rstd[:, t:t + 1], bias=nmr[:, t:t + 1]),
                    waits=[("v", P.cnt["v"])], incs=[("s", 1)])
                P.op("vector", lambda e, t=t, u_t=u_t: e.tensor_add(
                    x_sb[:, t * D:(t + 1) * D], x_sb[:, t * D:(t + 1) * D], u_t),
                    waits=[("s", P.cnt["s"]), ("v", P.cnt["v"])], incs=[("v", 1)])
    else:
        P.wait("scalar", x_ready)
        for t in range(NT):
            ic = [("s", 1)] if t == NT - 1 else []
            P.op("scalar", lambda e, t=t: e.activation(
                xln[:, t * D:(t + 1) * D], x_sb[:, t * D:(t + 1) * D], AF.Copy), incs=ic)
    s_xln = P.cnt["s"]

    ident = P.tiles["ident_sb"]
    pt = [P.tiles["pt0"], P.tiles["pt1"]]
    P.wait("tensor", [("s", s_xln)])
    evac_tot = []
    g = 0
    for dk in range(NDK):
        for t0 in range(0, NT, 4):
            par = g % 2
            if g >= 2:
                P.wait("tensor", [("v", evac_tot[g - 2])])
            for j in range(4):
                t = t0 + j
                ic = [("mm", 1)] if j == 3 else []
                P.op("tensor", lambda e, dk=dk, t=t, par=par, j=j: e.transpose(
                    pt[par][:, j * 128:(j + 1) * 128],
                    xln[:, t * D + dk * 128: t * D + dk * 128 + 128],
                    ident[:]), incs=ic)
            mm_tot = P.cnt["mm"]
            P.op("vector", lambda e, dk=dk, t0=t0, par=par: e.tensor_copy(
                xf[:, dk * T + t0 * 128: dk * T + t0 * 128 + 512],
                pt[par][:, 0:512]),
                waits=[("mm", mm_tot)], incs=[("v", 1)])
            evac_tot.append(P.cnt["v"])
            g += 1
    v_done = P.cnt["v"]
    s_done = P.cnt["s"]
    if gain is not None:
        w_sb, b_sb = gain
        P.wait("scalar", [("v", v_done)] + list(gain_ready))
        for dk in range(NDK):
            ic = [("s", 1)] if dk == NDK - 1 else []
            P.op("scalar", lambda e, dk=dk: e.activation(
                xf[:, dk * T:(dk + 1) * T], xf[:, dk * T:(dk + 1) * T], AF.Identity,
                scale=w_sb[:, dk:dk + 1], bias=b_sb[:, dk:dk + 1]), incs=ic)
        s_done = P.cnt["s"]
    return v_done, s_done



def alloc_l1ffn(P):
    P.sbuf("cm_sb", (128, NDK * D), BF16)
    P.sbuf("sm_sb", (128, NDK * D), BF16)
    if "scr" not in P.tiles:
        P.sbuf("scr", (128, 4096), F32)


def emit_l1ffn_prep(P, theta, theta_noise, phi, phi_noise):
    scr = P.tiles["scr"]
    for (base, noise, out_name, bias) in ((theta, theta_noise, "cm_sb", PI / 2),
                                          (phi, phi_noise, "sm_sb", 0.0)):
        for half in range(2):
            hb = base.rearrange("(k p) o -> p k o", p=128)[:, 4 * half:4 * half + 4, :]
            hn = noise.rearrange("(k p) o -> p k o", p=128)[:, 4 * half:4 * half + 4, :]
            s_prev = P.cnt["s"]
            d0 = P.op("sync", lambda e, hb=hb: e.dma_start(
                out=scr[:].rearrange("p (k o) -> p k o", k=4), in_=hb),
                waits=[("s", s_prev)], incs=[("dprep", 16)])[0]
            d1 = P.op("gpsimd", lambda e, hn=hn: e.dma_start(
                out=scr[:].rearrange("p (k o) -> p k o", k=4), in_=hn,
                accum_op=ALU.add),
                waits=[("dprep", d0)], incs=[("g", 16)])[0]
            P.op("scalar", lambda e, out_name=out_name, bias=bias, half=half: e.activation(
                P.tiles[out_name][:, half * 4096:(half + 1) * 4096], scr[:],
                AF.Sin, bias=(cst(P, "halfpi") if bias else 0.0)),
                waits=[("g", d1)], incs=[("s", 1)])


def emit_l1ffn_main(P, xf_ready):
    xf, x_sb = P.tiles["xf_sb"], P.tiles["x_sb"]
    cm, sm = P.tiles["cm_sb"], P.tiles["sm_sb"]
    scr = P.tiles["scr"]
    ps = [P.tiles["ps0"], P.tiles["ps1"], P.tiles["ps2"], P.tiles["ps3"]]
    P.wait("tensor", list(xf_ready))
    u = 0
    sq_done = []
    add_done = []
    stt_done = []
    for tc in range(NT):
        for oh in range(2):
            par = u % 2
            pr, pi_ = ps[par], ps[2 + par]
            if u >= 2:
                P.wait("tensor", [("s", sq_done[u - 2])])
            for ik in range(NDK):
                ic = [("mm", 1)] if ik == NDK - 1 else []
                P.op("tensor", lambda e, ik=ik, tc=tc, oh=oh, pr=pr: e.matmul(
                    pr[:], xf[:, ik * T + tc * 128: ik * T + tc * 128 + 128],
                    cm[:, ik * D + oh * 512: ik * D + oh * 512 + 512],
                    start=(ik == 0), stop=(ik == NDK - 1)), incs=ic)
            for ik in range(NDK):
                ic = [("mm", 1)] if ik == NDK - 1 else []
                P.op("tensor", lambda e, ik=ik, tc=tc, oh=oh, pi_=pi_: e.matmul(
                    pi_[:], xf[:, ik * T + tc * 128: ik * T + tc * 128 + 128],
                    sm[:, ik * D + oh * 512: ik * D + oh * 512 + 512],
                    start=(ik == 0), stop=(ik == NDK - 1)), incs=ic)
            mm_tot = P.cnt["mm"]
            r2 = scr[:, par * 2048 + 0:par * 2048 + 512]
            i2 = scr[:, par * 2048 + 512:par * 2048 + 1024]
            mag = scr[:, par * 2048 + 1024:par * 2048 + 1536]
            sqw = [("mm", mm_tot - 1)]
            if u >= 2:
                sqw.append(("v", add_done[u - 2]))
            P.op("scalar", lambda e, pr=pr, r2=r2: e.activation(r2, pr[:], AF.Square),
                 waits=sqw, incs=[("s", 1)])
            P.op("scalar", lambda e, pi_=pi_, i2=i2: e.activation(i2, pi_[:], AF.Square),
                 waits=[("mm", mm_tot)], incs=[("s", 1)])
            sq_done.append(P.cnt["s"])
            P.op("vector", lambda e, r2=r2, i2=i2: e.tensor_add(r2, r2, i2),
                 waits=[("s", sq_done[u])], incs=[("v", 1)])
            add_done.append(P.cnt["v"])
            sqrw = [("v", add_done[u])]
            if u >= 2:
                sqrw.append(("v", stt_done[u - 2]))
            P.op("scalar", lambda e, mag=mag, r2=r2: e.activation(mag, r2, AF.Sqrt),
                 waits=sqrw, incs=[("s", 1)])
            s_sqrt = P.cnt["s"]
            xs = x_sb[:, tc * D + oh * 512: tc * D + oh * 512 + 512]
            P.op("vector", lambda e, mag=mag, xs=xs: e.scalar_tensor_tensor(
                xs, mag, 0.5, xs, ALU.mult, ALU.add),
                waits=[("s", s_sqrt)], incs=[("v", 1)])
            stt_done.append(P.cnt["v"])
            u += 1



NV = 32
FV = 1024
NE = 8


def alloc_moe(P):
    P.sbuf("rw_sb", (8, D), BF16)
    P.sbuf("rwT_sb", (128, NDK * NE), BF16)
    P.sbuf("b1stg", (8, D), F32)
    P.sbuf("b1f", (128, 32 * NE), F32)
    P.sbuf("b2_sb", (8, D), BF16)
    P.sbuf("w_tm", (128, NT * NE), F32)
    P.sbuf("w_fm", (8, T), BF16)
    P.sbuf("exps_sb", (128, 2 * NE), F32)
    P.sbuf("msc", (128, 16), F32)
    for p in range(2):
        P.sbuf(f"w1_buf{p}", (128, NDK * FV), BF16)
        P.sbuf(f"w2_buf{p}", (128, NDK * FV), BF16)


def emit_moe_prep(P, rw_ap, b1_ap, b2_ap):
    rw, rwT = P.tiles["rw_sb"], P.tiles["rwT_sb"]
    b1stg = P.tiles["b1stg"]
    b1f, b2 = P.tiles["b1f"], P.tiles["b2_sb"]
    pt0, psB = P.tiles["pt0"], P.tiles["psB"]
    ident, id32 = P.tiles["ident_sb"], P.tiles["id32_sb"]
    d_rw = P.op("gpsimd", lambda e: e.dma_start(out=rw[:], in_=rw_ap),
                incs=[("dmoe_g", 16)])[0]
    d_b2 = P.op("gpsimd", lambda e: e.dma_start(out=b2[:], in_=b2_ap),
                incs=[("dmoe_g", 16)])[0]

    P.wait("tensor", [("dmoe_g", d_rw + 16), ("dma", P.cnt["dma"])])
    for dk in range(NDK):
        ic = [("mm", 1)] if dk == NDK - 1 else []
        P.op("tensor", lambda e, dk=dk: e.transpose(
            pt0[:, dk * 8:dk * 8 + 8], rw[:, dk * 128:(dk + 1) * 128], ident[0:8, 0:8]),
            incs=ic)
    mm_rw = P.cnt["mm"]
    v_rw = P.op("vector", lambda e: e.tensor_copy(rwT[:], pt0[:, 0:NDK * 8]),
                waits=[("mm", mm_rw)], incs=[("v", 1)])[0]
    b1r = b1_ap.rearrange("e (bat c) -> bat e c", bat=4)
    for bat in range(4):
        d_b1 = P.op("sync", lambda e, bat=bat: e.dma_start(out=b1stg[:], in_=b1r[bat]),
                    waits=[("mm", P.cnt["mm"])], incs=[("dmoe", 16)])[0]
        P.wait("tensor", [("dmoe", d_b1), ("v", P.cnt["v"])])
        for j8 in range(8):
            j = bat * 8 + j8
            ic = [("mm", 1)] if j8 == 7 else []
            P.op("tensor", lambda e, j=j, j8=j8: e.transpose(
                psB[:, j8 * 8:j8 * 8 + 8], b1stg[:, j8 * 128:(j8 + 1) * 128], id32[0:8, 0:8]),
                incs=ic)
        mm_b = P.cnt["mm"]
        P.op("vector", lambda e, bat=bat: e.tensor_copy(
            b1f[:, bat * 64:(bat + 1) * 64], psB[:, 0:64]),
            waits=[("mm", mm_b)], incs=[("v", 1)])
    return P.cnt["v"], P.cnt["s"]


def emit_moe_router(P, xf_ready):
    xf, rwT = P.tiles["xf_sb"], P.tiles["rwT_sb"]
    w_tm, w_fm = P.tiles["w_tm"], P.tiles["w_fm"]
    exps2, msc = P.tiles["exps_sb"], P.tiles["msc"]
    psA, psB = P.tiles["psA"], P.tiles["psB"]
    ps_l = [psA, psB]

    def vedge():
        P.wait("vector", [("v", P.cnt["v"])])

    P.wait("tensor", list(xf_ready))
    exp_done = []
    wsel_done = []
    for t in range(NT):
        par = t % 2
        pl = ps_l[par]
        if t >= 2:
            P.wait("tensor", [("s", exp_done[t - 2])])
        for dk in range(NDK):
            ic = [("mm", 1)] if dk == NDK - 1 else []
            P.op("tensor", lambda e, t=t, dk=dk, pl=pl: e.matmul(
                pl[:, 0:8], xf[:, dk * T + t * 128: dk * T + t * 128 + 128],
                rwT[:, dk * 8:(dk + 1) * 8],
                start=(dk == 0), stop=(dk == NDK - 1)), incs=ic)
        mm_t = P.cnt["mm"]
        exps = exps2[:, par * 8:par * 8 + 8]
        sw = [("mm", mm_t)]
        if t >= 2:
            sw.append(("v", wsel_done[t - 2]))
        P.op("scalar", lambda e, pl=pl, exps=exps: e.activation(exps, pl[:, 0:8], AF.Exp),
             waits=sw, incs=[("s", 1)])
        exp_done.append(P.cnt["s"])
        rsum, m1, m2, dn = (msc[:, 0:1], msc[:, 1:2], msc[:, 2:3], msc[:, 3:4])
        mask = msc[:, 8:16]
        P.op("vector", lambda e, exps=exps: e.tensor_reduce(rsum, exps, AX.X, ALU.add),
             waits=[("s", exp_done[t]), ("v", P.cnt["v"])], incs=[("v", 1)])
        vedge()
        P.op("vector", lambda e, exps=exps: e.tensor_reduce(m1, exps, AX.X, ALU.max),
             incs=[("v", 1)])
        vedge()
        P.op("vector", lambda e, exps=exps: e.tensor_single_scalar(
            mask, exps, m1, ALU.is_ge), incs=[("v", 1)])
        vedge()
        P.op("vector", lambda e, exps=exps: e.scalar_tensor_tensor(
            mask, mask, -1e30, exps, ALU.mult, ALU.add), incs=[("v", 1)])
        vedge()
        P.op("vector", lambda e: e.tensor_reduce(m2, mask, AX.X, ALU.max),
             incs=[("v", 1)])
        vedge()
        P.op("vector", lambda e: e.tensor_add(dn, m1, m2), incs=[("v", 1)])
        vedge()
        P.op("vector", lambda e: e.scalar_tensor_tensor(
            dn, rsum, 1e-8, dn, ALU.mult, ALU.add), incs=[("v", 1)])
        vedge()
        P.op("vector", lambda e: e.reciprocal(dn, dn), incs=[("v", 1)])
        vedge()
        P.op("vector", lambda e, exps=exps: e.tensor_single_scalar(
            mask, exps, m2, ALU.is_ge), incs=[("v", 1)])
        vedge()
        P.op("vector", lambda e: e.tensor_scalar_mul(dn, dn, 0.5), incs=[("v", 1)])
        vedge()
        P.op("vector", lambda e, exps=exps, t=t: e.tensor_single_scalar(
            exps, exps, dn, ALU.mult), incs=[("v", 1)])
        vedge()
        P.op("vector", lambda e, exps=exps, t=t: e.tensor_mul(
            w_tm[:, t * 8:(t + 1) * 8], exps, mask), incs=[("v", 1)])
        wsel_done.append(P.cnt["v"])
    v_wtm = P.cnt["v"]
    id32 = P.tiles["id32_sb"]
    P.wait("tensor", [("v", v_wtm)])
    tr_evac = []
    for t in range(NT):
        par = t % 2
        if t >= 2:
            P.wait("tensor", [("v", tr_evac[t - 2])])
        mm_t = P.op("tensor", lambda e, t=t, par=par: e.transpose(
            psB[0:8, par * 128:par * 128 + 128],
            w_tm[:, t * 8:(t + 1) * 8], id32[:]), incs=[("mm", 1)])[0]
        P.op("vector", lambda e, t=t, par=par: e.tensor_copy(
            w_fm[:, t * 128:(t + 1) * 128], psB[0:8, par * 128:par * 128 + 128]),
            waits=[("mm", mm_t)], incs=[("v", 1)])
        tr_evac.append(P.cnt["v"])
    return P.cnt["v"]



def alloc_l1attn(P):
    for nm in ("wbufA", "wbufB"):
        if nm not in P.tiles:
            P.sbuf(nm, (128, NDK * D), BF16)
    for nm in ("wbufC", "wbufD"):
        if nm not in P.tiles:
            P.sbuf(nm, (128, NDK * D), BF16)
    xln = P.tiles["xln_sb"]
    P.tiles["ekv_sb"] = xln[:, D:3 * D]
    P.tiles["qf_sb"] = xln[:, 3 * D:4 * D]
    P.tiles["af_sb"] = xln[:, 4 * D:5 * D]
    P.sbuf("ctx_sb", (128, 8 * 264), F32)
    P.sbuf("ctxn_sb", (128, 8 * 256), BF16)
    P.tiles["qt_sb"] = P.tiles["ctx_sb"][:, 0:D]
    P.tiles["arow_sb"] = xln[0:1, 11 * D:13 * D]
    P.tiles["bdrow_sb"] = xln[0:1, 13 * D:14 * D]
    P.sbuf("g1_sb", (128, 2 * NDK), F32)
    P.sbuf("g2_sb", (128, 2 * NDK), F32)
    P.sbuf("bf1_sb", (128, 32), F32)
    P.sbuf("asc_sb", (128, 8), F32)
    P.sbuf("onecol_sb", (128, 1), BF16)


ATTN_STOP = None


def emit_l1attn(P, aps, x_ready):
    xf, x_sb, xln = P.tiles["xf_sb"], P.tiles["x_sb"], P.tiles["xln_sb"]
    wA, wB = P.tiles["wbufA"], P.tiles["wbufB"]
    wC, wD = P.tiles["wbufC"], P.tiles["wbufD"]
    ekv, ctx, ctxn = P.tiles["ekv_sb"], P.tiles["ctx_sb"], P.tiles["ctxn_sb"]
    qt, qf, af = P.tiles["qt_sb"], P.tiles["qf_sb"], P.tiles["af_sb"]
    arow, bdrow = P.tiles["arow_sb"], P.tiles["bdrow_sb"]
    g1, g2, bf1, asc = (P.tiles["g1_sb"], P.tiles["g2_sb"], P.tiles["bf1_sb"],
                        P.tiles["asc_sb"])
    ps0, ps1, ps2, ps3 = (P.tiles["ps0"], P.tiles["ps1"], P.tiles["ps2"],
                          P.tiles["ps3"])
    psA, psB = P.tiles["psA"], P.tiles["psB"]
    onesbf = P.tiles["onesbf_sb"]
    onecol = P.tiles["onecol_sb"]
    ident = P.tiles["ident_sb"]
    SC4 = 0.25
    P.op("vector", lambda e: e.memset(onecol[:], 1.0),
         waits=[("v", P.cnt["v"])], incs=[("v", 1)])
    P.op("vector", lambda e: e.memset(ctx[:], 0.0), incs=[("v", 1)])
    v_one = P.cnt["v"]

    d_xo = P.op("sync", lambda e: e.dma_start(
        out=aps["x_old"].rearrange("(t p) d -> p t d", p=128),
        in_=x_sb[:].rearrange("p (t d) -> p t d", t=NT)),
        waits=list(x_ready), incs=[("dxold", 16)])[0]
    v_ou, s_ou = emit_ln(P, x_ready=list(x_ready) + [("dxold", d_xo)], inplace=True)
    x_ready = [("v", v_ou), ("s", s_ou)]

    def wload(dst, key, sem):
        return P.op("gpsimd", lambda e, dst=dst, key=key: e.dma_start(
            out=dst[:].rearrange("p (k c) -> p k c", k=NDK),
            in_=aps[key].rearrange("(k p) c -> p k c", p=128)),
            incs=[(sem, 16)])[0]

    d_wk = wload(wA, "L1_attn_wk", "dat_a")
    d_wv = wload(wB, "L1_attn_wv", "dat_a")
    d_small = P.cnt["dat_a"]
    for (dst, key) in ((g1[:, 0:NDK], "a_ln1g"), (g1[:, NDK:2 * NDK], "a_ln1b"),
                       (g2[:, 0:NDK], "a_ln2g"), (g2[:, NDK:2 * NDK], "a_ln2b"),
                       (bf1[:], "a_bff1")):
        P.op("sync", lambda e, dst=dst, key=key: e.dma_start(
            out=dst, in_=aps[key]), incs=[("dat_f", 16)])
    d_f32 = P.cnt["dat_f"]

    def load_bias_rows(sem, wait):
        P.op("gpsimd", lambda e: e.dma_start(
            out=arow[:, 0:D], in_=aps["L1_attn_bo"].rearrange("(a c) -> a c", a=1)),
            waits=wait, incs=[(sem, 16)])
        P.op("gpsimd", lambda e: e.dma_start(
            out=arow[:, D:2 * D], in_=aps["L1_attn_b_ff2"].rearrange("(a c) -> a c", a=1)),
            incs=[(sem, 16)])
        P.op("gpsimd", lambda e: e.dma_start(
            out=bdrow[:, :], in_=aps["L1_attn_b_d"].rearrange("(a c) -> a c", a=1)),
            incs=[(sem, 16)])
        return P.cnt[sem]

    v1, s1 = emit_ln(P, x_ready=x_ready, gain=(g1[:, 0:NDK], g1[:, NDK:2 * NDK]),
                     gain_ready=[("dat_f", d_f32)])
    d_rows = load_bias_rows("dat_r", [("v", v1), ("s", s1)])

    P.wait("tensor", [("v", v1), ("s", s1), ("dat_a", d_small), ("v", v_one), ("dat_r", d_rows)])
    kv_evac = []
    ctxadd_v = []
    last_k_evac = 0
    last_v_evac = 0
    for tc in range(NT):
        for dh in range(2):
            if last_k_evac:
                P.wait("tensor", [("s", last_k_evac)])
            for dk in range(NDK):
                ic = [("mm", 1)] if dk == NDK - 1 else []
                P.op("tensor", lambda e, tc=tc, dh=dh, dk=dk: e.matmul(
                    ps0[:], xf[:, dk * T + tc * 128: dk * T + tc * 128 + 128],
                    wA[:, dk * D + dh * 512: dk * D + dh * 512 + 512],
                    start=(dk == 0), stop=(dk == NDK - 1)), incs=ic)
            mmk = P.cnt["mm"]
            P.op("scalar", lambda e, dh=dh: e.activation(
                ekv[:, dh * 512:dh * 512 + 512], ps0[:], AF.Exp, scale=SC4),
                waits=[("mm", mmk)], incs=[("s", 1)])
            last_k_evac = P.cnt["s"]
            if last_v_evac:
                P.wait("tensor", [("s", last_v_evac)])
            for dk in range(NDK):
                ic = [("mm", 1)] if dk == NDK - 1 else []
                P.op("tensor", lambda e, tc=tc, dh=dh, dk=dk: e.matmul(
                    ps1[:], xf[:, dk * T + tc * 128: dk * T + tc * 128 + 128],
                    wB[:, dk * D + dh * 512: dk * D + dh * 512 + 512],
                    start=(dk == 0), stop=(dk == NDK - 1)), incs=ic)
            mmv = P.cnt["mm"]
            P.op("scalar", lambda e, dh=dh: e.activation(
                ekv[:, D + dh * 512:D + dh * 512 + 512], ps1[:], AF.Copy),
                waits=[("mm", mmv)], incs=[("s", 1)])
            last_v_evac = P.cnt["s"]
        kv_evac.append(P.cnt["s"])
        P.wait("tensor", [("s", kv_evac[tc])])
        for u in range(8):
            h, hh = u // 2, u % 2
            pp = [ps2, ps3][u % 2]
            if tc * 8 + u >= 2:
                P.wait("tensor", [("v", ctxadd_v[tc * 8 + u - 2])])
            P.op("tensor", lambda e, h=h, hh=hh, pp=pp: e.matmul(
                pp[:, 0:256], ekv[:, h * 256 + hh * 128: h * 256 + hh * 128 + 128],
                ekv[:, D + h * 256: D + h * 256 + 256], start=True, stop=True))
            P.op("tensor", lambda e, h=h, hh=hh, pp=pp: e.matmul(
                pp[:, 256:257], ekv[:, h * 256 + hh * 128: h * 256 + hh * 128 + 128],
                onecol[:], start=True, stop=True), incs=[("mm", 1)])
            mmc = P.cnt["mm"]
            P.op("vector", lambda e, u=u, pp=pp, tc=tc: e.tensor_tensor(
                ctx[:, u * 264:u * 264 + 257], ctx[:, u * 264:u * 264 + 257],
                pp[:, 0:257], ALU.add) if tc > 0 else e.tensor_copy(
                ctx[:, u * 264:u * 264 + 257], pp[:, 0:257]),
                waits=[("mm", mmc), ("v", P.cnt["v"])], incs=[("v", 1)])
            ctxadd_v.append(P.cnt["v"])
    v_ctx = P.cnt["v"]
    d_cp = P.op("sync", lambda e: e.dma_start(out=aps["ctx_part"], in_=ctx[:]),
                waits=[("v", v_ctx)], incs=[("dctx", 16)])[0]
    P.op("gpsimd", lambda e: e.collective_compute(
        "AllReduce", ALU.add, replica_groups=GROUPS,
        ins=[aps["ctx_part"].opt()], outs=[aps["ctx_sum"].opt()]),
        waits=[("dctx", d_cp)], incs=[("cc", 1)])
    cc_ctx = P.cnt["cc"]
    d_cl = P.op("sync", lambda e: e.dma_start(out=ctx[:], in_=aps["ctx_sum"]),
                waits=[("cc", cc_ctx)], incs=[("dctx", 16)])[0]
    P.wait("vector", [("dctx", d_cl)])
    for u in range(8):
        P.op("vector", lambda e, u=u: e.reciprocal(
            asc[:, u:u + 1], ctx[:, u * 264 + 256:u * 264 + 257]),
            waits=[("v", P.cnt["v"])], incs=[("v", 1)])
        P.wait("vector", [("v", P.cnt["v"])])
        P.op("vector", lambda e, u=u: e.tensor_single_scalar(
            ctxn[:, u * 256:(u + 1) * 256], ctx[:, u * 264:u * 264 + 256],
            asc[:, u:u + 1], ALU.mult), incs=[("v", 1)])
        P.wait("vector", [("v", P.cnt["v"])])
    v_ctxn = P.cnt["v"]

    d_wq = wload(wC, "L1_attn_wq", "dat_b")
    d_wo = wload(wD, "L1_attn_wo", "dat_b")
    d_b = P.cnt["dat_b"]

    P.wait("tensor", [("v", v_ctxn), ("dat_b", d_b)])
    q_evac = []
    pb_v = []
    for tc in range(NT):
        if tc >= 1:
            P.wait("tensor", [("v", pb_v[-1])])
        for dh in range(2):
            pp = [ps0, ps1][dh]
            for dk in range(NDK):
                ic = [("mm", 1)] if dk == NDK - 1 else []
                P.op("tensor", lambda e, tc=tc, dh=dh, dk=dk, pp=pp: e.matmul(
                    pp[:], xf[:, dk * T + tc * 128: dk * T + tc * 128 + 128],
                    wC[:, dk * D + dh * 512: dk * D + dh * 512 + 512],
                    start=(dk == 0), stop=(dk == NDK - 1)), incs=ic)
            mmq = P.cnt["mm"]
            sw = [("mm", mmq)]
            if tc >= 1:
                sw.append(("v", pb_v[-1]))
            P.op("scalar", lambda e, dh=dh, pp=pp: e.activation(
                qt[:, dh * 512:dh * 512 + 512], pp[:], AF.Copy),
                waits=sw, incs=[("s", 1)])
        s_q = P.cnt["s"]
        P.wait("vector", [("s", s_q)])
        eq = xln[:, 0:D]
        for h in range(4):
            P.op("vector", lambda e, h=h: e.tensor_reduce(
                asc[:, h:h + 1], qt[:, h * 256:(h + 1) * 256], AX.X, ALU.max),
                waits=[("v", P.cnt["v"])], incs=[("v", 1)])
            P.wait("vector", [("v", P.cnt["v"])])
            P.op("vector", lambda e, h=h: e.tensor_single_scalar(
                asc[:, 4 + h:5 + h], asc[:, h:h + 1], -SC4, ALU.mult), incs=[("v", 1)])
            P.wait("vector", [("v", P.cnt["v"])])
        v_m = P.cnt["v"]
        P.wait("scalar", [("v", v_m)])
        mmprev = P.cnt["mm"]
        for h in range(4):
            P.op("scalar", lambda e, h=h: e.activation(
                eq[:, h * 256:(h + 1) * 256], qt[:, h * 256:(h + 1) * 256], AF.Exp,
                scale=SC4, bias=asc[:, 4 + h:5 + h],
                accum_out=asc[:, h:h + 1]),
                waits=([("mm", mmprev)] if h == 0 else []), incs=[("s", 1)])
        s_eq = P.cnt["s"]
        P.wait("vector", [("s", s_eq)])
        for h in range(4):
            P.op("vector", lambda e, h=h: e.reciprocal(
                asc[:, 4 + h:5 + h], asc[:, h:h + 1]),
                waits=[("v", P.cnt["v"])], incs=[("v", 1)])
            P.wait("vector", [("v", P.cnt["v"])])
            P.op("vector", lambda e, h=h: e.tensor_single_scalar(
                eq[:, h * 256:(h + 1) * 256], eq[:, h * 256:(h + 1) * 256],
                asc[:, 4 + h:5 + h], ALU.mult), incs=[("v", 1)])
            P.wait("vector", [("v", P.cnt["v"])])
        v_eq = P.cnt["v"]
        P.wait("tensor", [("v", v_eq)])
        qf_ev = []
        for dk in range(NDK):
            par = dk % 2
            pt = [P.tiles["pt0"], P.tiles["pt1"]][par]
            if dk >= 2:
                P.wait("tensor", [("v", qf_ev[dk - 2])])
            mmt = P.op("tensor", lambda e, dk=dk, pt=pt: e.transpose(
                pt[:, 0:128], eq[:, dk * 128:(dk + 1) * 128], ident[:]),
                incs=[("mm", 1)])[0]
            P.op("vector", lambda e, dk=dk, pt=pt: e.tensor_copy(
                qf[:, dk * 128:(dk + 1) * 128], pt[:, 0:128]),
                waits=[("mm", mmt), ("v", P.cnt["v"])], incs=[("v", 1)])
            qf_ev.append(P.cnt["v"])
        v_qf = P.cnt["v"]
        P.wait("tensor", [("v", v_qf)])
        af_ev = []
        for half in range(2):
            pp = [ps2, ps3][half]
            if tc >= 1 and half == 0:
                P.wait("tensor", [("v", pb_v[-1])])
            for j in range(4):
                a = half * 4 + j
                h, eh = a // 2, a % 2
                for hh in range(2):
                    ic = [("mm", 1)] if hh == 1 else []
                    P.op("tensor", lambda e, h=h, eh=eh, hh=hh, pp=pp, j=j: e.matmul(
                        pp[:, j * 128:(j + 1) * 128],
                        ctxn[:, (h * 2 + hh) * 256 + eh * 128: (h * 2 + hh) * 256 + eh * 128 + 128],
                        qf[:, (h * 2 + hh) * 128: (h * 2 + hh) * 128 + 128],
                        start=(hh == 0), stop=(hh == 1)), incs=ic)
            mmaf = P.cnt["mm"]
            P.op("vector", lambda e, half=half, pp=pp: e.tensor_copy(
                af[:, half * 512:(half + 1) * 512], pp[:, 0:512]),
                waits=[("mm", mmaf), ("v", P.cnt["v"])], incs=[("v", 1)])
            af_ev.append(P.cnt["v"])
        P.wait("tensor", [("v", af_ev[-1])])
        for dh in range(2):
            pp = [ps0, ps1][dh]
            for a in range(NDK):
                P.op("tensor", lambda e, a=a, dh=dh, pp=pp: e.matmul(
                    pp[:], af[:, a * 128:(a + 1) * 128],
                    wD[:, a * D + dh * 512: a * D + dh * 512 + 512],
                    start=(a == 0), stop=False))
            P.op("tensor", lambda e, dh=dh, pp=pp: e.matmul(
                pp[:], onesbf[:], arow[:, dh * 512:dh * 512 + 512],
                start=False, stop=True), incs=[("mm", 1)])
            mmo = P.cnt["mm"]
            xs = x_sb[:, tc * D + dh * 512: tc * D + dh * 512 + 512]
            P.op("vector", lambda e, xs=xs, pp=pp: e.tensor_add(xs, xs, pp[:]),
                 waits=[("mm", mmo), ("v", P.cnt["v"])], incs=[("v", 1)])
        pb_v.append(P.cnt["v"])

    if ATTN_STOP == "wo":
        return
    x_ready2 = [("v", P.cnt["v"]), ("s", P.cnt["s"])]
    v2, s2 = emit_ln(P, x_ready=x_ready2, gain=(g2[:, 0:NDK], g2[:, NDK:2 * NDK]),
                     gain_ready=[("dat_f", d_f32)])
    d_rows2 = load_bias_rows("dat_r2", [("v", v2), ("s", s2)])
    h = xln
    wf_sem = ["dff0", "dff1"]
    W1r = aps["L1_attn_w_ff1"].rearrange("(dk p) (vf f) -> vf p dk f", p=128, f=512)
    W2r = aps["L1_attn_w_ff2"].rearrange("(vf fk p) d -> vf p fk d", p=128, fk=4)
    ff1_mm = []
    ff2_mm = []
    gelu_s = []
    yst_v = []
    hev = []
    P.wait("tensor", [("v", v2), ("s", s2)])
    for vf in range(8):
        par = vf % 2
        wts = [("mm", ff1_mm[vf - 2])] if vf >= 2 else [("mm", P.cnt["mm"])]
        P.op("gpsimd", lambda e, vf=vf, par=par: e.dma_start(
            out=[wA, wB][par][:, 0:NDK * 512].rearrange("p (dk f) -> p dk f", f=512),
            in_=W1r[vf]), waits=wts, incs=[(wf_sem[par], 16)])
        w1r = P.cnt[wf_sem[par]]
        wts2 = [("mm", ff2_mm[vf - 2])] if vf >= 2 else [("mm", P.cnt["mm"])]
        P.op("gpsimd", lambda e, vf=vf, par=par: e.dma_start(
            out=[wC, wD][par][:, 0:4 * 1024].rearrange("p (fk d) -> p fk d", d=1024),
            in_=W2r[vf]), waits=wts2, incs=[(wf_sem[par] + "b", 16)])
        w2r = P.cnt[wf_sem[par] + "b"]
        P.wait("tensor", [(wf_sem[par], w1r)])
        if vf >= 1:
            P.wait("scalar", [("mm", ff2_mm[vf - 1])])
        for fm in range(4):
            for tg in range(4):
                gg = len(gelu_s)
                hp = [ps0, ps1][gg % 2]
                if gg >= 2:
                    P.wait("tensor", [("s", gelu_s[gg - 2])])
                for dk in range(NDK):
                    ic = [("mm", 1)] if dk == NDK - 1 else []
                    P.op("tensor", lambda e, fm=fm, tg=tg, dk=dk, par=par, hp=hp: e.matmul(
                        hp[:], [wA, wB][par][:, dk * 512 + fm * 128: dk * 512 + fm * 128 + 128],
                        xf[:, dk * T + tg * 512: dk * T + tg * 512 + 512],
                        start=(dk == 0), stop=(dk == NDK - 1)), incs=ic)
                mmh = P.cnt["mm"]
                jcol = vf * 4 + fm
                P.op("scalar", lambda e, fm=fm, tg=tg, hp=hp, jcol=jcol: e.activation(
                    h[:, fm * T + tg * 512: fm * T + tg * 512 + 512], hp[:],
                    AF.Gelu_apprx_tanh, bias=bf1[:, jcol:jcol + 1]),
                    waits=[("mm", mmh)], incs=[("s", 1)])
                gelu_s.append(P.cnt["s"])
        ff1_mm.append(P.cnt["mm"])
        s_h = P.cnt["s"]
        P.wait("tensor", [(wf_sem[par] + "b", w2r), ("s", s_h)])
        if vf == 0:
            P.wait("tensor", [("dat_r2", d_rows2)])
        for tc in range(NT):
            for dh in range(2):
                uu = len(yst_v)
                yp = [ps2, ps3][uu % 2]
                if uu >= 2:
                    P.wait("tensor", [("v", yst_v[uu - 2])])
                for fk in range(4):
                    last = (fk == 3 and vf > 0)
                    ic = [("mm", 1)] if last else []
                    P.op("tensor", lambda e, tc=tc, dh=dh, fk=fk, par=par, yp=yp, last=last: e.matmul(
                        yp[:], h[:, fk * T + tc * 128: fk * T + tc * 128 + 128],
                        [wC, wD][par][:, fk * 1024 + dh * 512: fk * 1024 + dh * 512 + 512],
                        start=(fk == 0), stop=last), incs=ic)
                if vf == 0:
                    P.op("tensor", lambda e, dh=dh, yp=yp: e.matmul(
                        yp[:], onesbf[:], arow[:, D + dh * 512: D + dh * 512 + 512],
                        start=False, stop=True), incs=[("mm", 1)])
                mmy = P.cnt["mm"]
                xs = x_sb[:, tc * D + dh * 512: tc * D + dh * 512 + 512]
                P.op("vector", lambda e, xs=xs, yp=yp: e.tensor_add(xs, xs, yp[:]),
                     waits=[("mm", mmy), ("v", P.cnt["v"])], incs=[("v", 1)])
                yst_v.append(P.cnt["v"])
        ff2_mm.append(P.cnt["mm"])

    if ATTN_STOP == "ff":
        return
    x_ready3 = [("v", P.cnt["v"]), ("s", P.cnt["s"])]
    v3, s3 = emit_ln(P, x_ready=x_ready3, identity=True)
    d_rows3 = load_bias_rows("dat_r3", [("v", v3), ("s", s3)])
    d_wd = P.op("gpsimd", lambda e: e.dma_start(
        out=wA[:].rearrange("p (k c) -> p k c", k=NDK),
        in_=aps["L1_attn_w_d"].rearrange("(k p) c -> p k c", p=128)),
        waits=[("mm", P.cnt["mm"])], incs=[("dat_d", 16)])[0]
    P.wait("tensor", [("v", v3), ("s", s3), ("dat_d", d_wd), ("dat_r3", d_rows3)])
    wd_s = []
    xob = P.tiles["ctx_sb"][:, 0:1024]
    xor_ap = aps["x_old"].rearrange("(t p) d -> p t d", p=128)
    ox_sem = ["dox0", "dox1"]
    for tc in range(NT):
        for dh in range(2):
            uu = tc * 2 + dh
            pp = [ps0, ps1][uu % 2]
            par = uu % 2
            wts = [("v", wd_s[uu - 2])] if uu >= 2 else [("v", P.cnt["v"])]
            P.op("sync", lambda e, tc=tc, dh=dh, par=par: e.dma_start(
                out=xob[:, par * 512:par * 512 + 512],
                in_=xor_ap[:, tc, dh * 512:dh * 512 + 512]),
                waits=wts, incs=[(ox_sem[par], 16)])
            d_ox = P.cnt[ox_sem[par]]
            if uu >= 2:
                P.wait("tensor", [("v", wd_s[uu - 2])])
            for dk in range(NDK):
                P.op("tensor", lambda e, tc=tc, dh=dh, dk=dk, pp=pp: e.matmul(
                    pp[:], xf[:, dk * T + tc * 128: dk * T + tc * 128 + 128],
                    wA[:, dk * D + dh * 512: dk * D + dh * 512 + 512],
                    start=(dk == 0), stop=False))
            P.op("tensor", lambda e, dh=dh, pp=pp: e.matmul(
                pp[:], onesbf[:], bdrow[:, dh * 512:dh * 512 + 512],
                start=False, stop=True), incs=[("mm", 1)])
            mmw = P.cnt["mm"]
            xs = x_sb[:, tc * D + dh * 512: tc * D + dh * 512 + 512]
            P.op("vector", lambda e, xs=xs, pp=pp, par=par: e.tensor_tensor(
                xs, xob[:, par * 512:par * 512 + 512], pp[:], ALU.add),
                waits=[("mm", mmw), (ox_sem[par], d_ox), ("v", P.cnt["v"])],
                incs=[("v", 1)])
            wd_s.append(P.cnt["v"])


def emit_moe_bias(P, router_v_done):
    w_fm, b2, x_sb = P.tiles["w_fm"], P.tiles["b2_sb"], P.tiles["x_sb"]
    psA, psB = P.tiles["psA"], P.tiles["psB"]
    ps = [psA, psB]
    P.wait("tensor", [("v", router_v_done)])
    add_done = []
    u = 0
    for tc in range(NT):
        for dh in range(2):
            par = u % 2
            if u >= 2:
                P.wait("tensor", [("v", add_done[u - 2])])
            mm_u = P.op("tensor", lambda e, tc=tc, dh=dh, par=par: e.matmul(
                ps[par][:], w_fm[:, tc * 128:(tc + 1) * 128],
                b2[:, dh * 512:(dh + 1) * 512], start=True, stop=True),
                incs=[("mm", 1)])[0]
            xs = x_sb[:, tc * D + dh * 512: tc * D + dh * 512 + 512]
            P.op("vector", lambda e, xs=xs, par=par: e.tensor_add(xs, xs, ps[par][:]),
                 waits=[("mm", mm_u), ("v", P.cnt["v"])], incs=[("v", 1)])
            add_done.append(P.cnt["v"])
            u += 1


def emit_moe_experts(P, W1_ap, W2_ap, xf_ready):
    xf, x_sb = P.tiles["xf_sb"], P.tiles["x_sb"]
    h = P.tiles["xln_sb"]
    b1f, w_tm = P.tiles["b1f"], P.tiles["w_tm"]
    ps_h = [P.tiles["ps0"], P.tiles["ps1"]]
    ps_y = [P.tiles["ps2"], P.tiles["ps3"]]
    w1b = [P.tiles["w1_buf0"], P.tiles["w1_buf1"]]
    w2b = [P.tiles["w2_buf0"], P.tiles["w2_buf1"]]
    W1r = W1_ap.rearrange("e (dk p) (vf f) -> e vf p dk f", p=128, f=FV)
    W2r = W2_ap.rearrange("e (vf fk p) d -> e vf p fk d", p=128, fk=NDK)

    P.wait("tensor", list(xf_ready))
    w1_sem = ["dw1a", "dw1b"]
    w2_sem = ["dw2a", "dw2b"]
    w1_mm_done = []
    w2_mm_done = []
    hevac_done = []
    gelu_done = []
    ystt_done = []
    ghp = 0
    gyp = 0
    for v in range(NV):
        e_idx, vf = v // 4, v % 4
        par = v % 2
        wts = [("mm", w1_mm_done[v - 2])] if v >= 2 else []
        P.op("gpsimd", lambda e, e_idx=e_idx, vf=vf, par=par: e.dma_start(
            out=w1b[par][:].rearrange("p (dk f) -> p dk f", f=FV),
            in_=W1r[e_idx, vf]),
            waits=wts, incs=[(w1_sem[par], 16)])
        w1_rdy = P.cnt[w1_sem[par]]
        wts2 = [("mm", w2_mm_done[v - 2])] if v >= 2 else []
        P.op("gpsimd", lambda e, e_idx=e_idx, vf=vf, par=par: e.dma_start(
            out=w2b[par][:].rearrange("p (fk d) -> p fk d", d=D),
            in_=W2r[e_idx, vf]),
            waits=wts2, incs=[(w2_sem[par], 16)])
        w2_rdy = P.cnt[w2_sem[par]]

        P.wait("tensor", [(w1_sem[par], w1_rdy)])
        if v >= 1:
            P.wait("scalar", [("mm", w2_mm_done[v - 1])])
        for fm in range(NDK):
            for tg in range(4):
                hpar = ghp % 2
                if ghp >= 2:
                    P.wait("tensor", [("s", gelu_done[ghp - 2])])
                for dk in range(NDK):
                    ic = [("mm", 1)] if dk == NDK - 1 else []
                    P.op("tensor", lambda e, fm=fm, tg=tg, dk=dk, par=par, hpar=hpar: e.matmul(
                        ps_h[hpar][:],
                        w1b[par][:, dk * FV + fm * 128: dk * FV + fm * 128 + 128],
                        xf[:, dk * T + tg * 512: dk * T + tg * 512 + 512],
                        start=(dk == 0), stop=(dk == NDK - 1)), incs=ic)
                mm_g = P.cnt["mm"]
                j8 = (vf * 8 + fm) * 8 + e_idx
                b1col = b1f[:, j8:j8 + 1]
                P.op("scalar", lambda e, fm=fm, tg=tg, hpar=hpar, b1col=b1col: e.activation(
                    h[:, fm * T + tg * 512: fm * T + tg * 512 + 512],
                    ps_h[hpar][:], AF.Gelu_apprx_tanh, bias=b1col),
                    waits=[("mm", mm_g)], incs=[("s", 1)])
                gelu_done.append(P.cnt["s"])
                ghp += 1
        w1_mm_done.append(P.cnt["mm"])
        hevac_done.append(P.cnt["s"])

        P.wait("tensor", [(w2_sem[par], w2_rdy), ("s", hevac_done[v])])
        for tc in range(NT):
            for dh in range(2):
                ypar = gyp % 2
                if gyp >= 2:
                    P.wait("tensor", [("v", ystt_done[gyp - 2])])
                for fk in range(NDK):
                    ic = [("mm", 1)] if fk == NDK - 1 else []
                    P.op("tensor", lambda e, tc=tc, dh=dh, fk=fk, par=par, ypar=ypar: e.matmul(
                        ps_y[ypar][:],
                        h[:, fk * T + tc * 128: fk * T + tc * 128 + 128],
                        w2b[par][:, fk * FV + dh * 512: fk * FV + dh * 512 + 512],
                        start=(fk == 0), stop=(fk == NDK - 1)), incs=ic)
                mm_g = P.cnt["mm"]
                wcol = w_tm[:, tc * 8 + e_idx: tc * 8 + e_idx + 1]
                xs = x_sb[:, tc * D + dh * 512: tc * D + dh * 512 + 512]
                P.op("vector", lambda e, xs=xs, ypar=ypar, wcol=wcol: e.scalar_tensor_tensor(
                    xs, ps_y[ypar][:], wcol, xs, ALU.mult, ALU.add),
                    waits=[("mm", mm_g), ("v", P.cnt["v"])], incs=[("v", 1)])
                ystt_done.append(P.cnt["v"])
                gyp += 1
        w2_mm_done.append(P.cnt["mm"])



DI = 512
NDC = DI // 128
SEQ = 8192
NSC = SEQ // 512
DST = 16


def alloc_mamba(P):
    P.sbuf("winT_sb", (128, NDK * 1024), BF16)
    P.sbuf("opT_sb", (128, NDC * D), BF16)
    P.sbuf("xpwT_sb", (128, NDC * 96), BF16)
    P.sbuf("dtwT_sb", (64, DI), BF16)
    P.sbuf("basis_sb", (16, 16 * 128), BF16)
    P.sbuf("A_sb", (128, NDC * DST), F32)
    P.sbuf("cw_sb", (128, NDC * 4), F32)
    P.sbuf("cb_sb", (128, NDC), F32)
    P.sbuf("D_sb", (128, NDC), F32)
    P.sbuf("dtb_sb", (128, NDC), F32)
    P.sbuf("lnw_sb", (128, NDK), F32)
    P.sbuf("lnb_sb", (128, NDK), F32)
    for p in range(2):
        P.sbuf(f"xin_buf{p}", (128, NDK * 512), BF16)
    xf = P.tiles["xf_sb"]
    off = 0
    for (nm, cols) in (("xcin_sb", NDC * 515), ("zc_sb", NDC * 512),
                       ("xcv_sb", NDC * 512), ("dts_sb", NDC * 512),
                       ("dtu_sb", NDC * 512), ("yg_sb", NDC * 512),
                       ("scanA", 512), ("scanB", 512), ("states", 512)):
        P.tiles[nm] = xf[:, off:off + cols]
        off += cols
    assert off <= NDK * T
    P.sbuf("cacc_sb", (128, 512), F32)
    P.sbuf("xdbl_tm", (128, 4 * 96), F32)
    P.sbuf("dtraw_fm", (64, 512), BF16)
    P.sbuf("Bfm_sb", (16, 512), BF16)
    P.sbuf("Cfm_sb", (16, 512), BF16)
    P.sbuf("h_sb", (128, NDC * DST), F32)
    P.sbuf("yacc_sb", (128, NDC * 512), F32)
    P.sbuf("sz_sb", (128, 512), F32)
    P.sbuf("mscr", (128, 2 * 512), F32)
    P.sbuf("m4buf", (128, 2048), F32)


MAMBA_STOP = None


def emit_mamba(P, aps, x_ready):
    xf, x_sb = P.tiles["xf_sb"], P.tiles["x_sb"]
    winT, opT = P.tiles["winT_sb"], P.tiles["opT_sb"]
    xpwT, dtwT = P.tiles["xpwT_sb"], P.tiles["dtwT_sb"]
    basis = P.tiles["basis_sb"]
    A_sb, cw, cb, D_sb, dtb = (P.tiles["A_sb"], P.tiles["cw_sb"], P.tiles["cb_sb"],
                               P.tiles["D_sb"], P.tiles["dtb_sb"])
    xcin, zc, xcv = P.tiles["xcin_sb"], P.tiles["zc_sb"], P.tiles["xcv_sb"]
    cacc = P.tiles["cacc_sb"]
    xdbl_tm, dtraw = P.tiles["xdbl_tm"], P.tiles["dtraw_fm"]
    Bfm, Cfm = P.tiles["Bfm_sb"], P.tiles["Cfm_sb"]
    dts, dtu = P.tiles["dts_sb"], P.tiles["dtu_sb"]
    scanA, scanB, states = P.tiles["scanA"], P.tiles["scanB"], P.tiles["states"]
    h_sb, yacc, sz, yg = (P.tiles["h_sb"], P.tiles["yacc_sb"], P.tiles["sz_sb"],
                          P.tiles["yg_sb"])
    mscr = P.tiles["mscr"]
    xinb = [P.tiles["xin_buf0"], P.tiles["xin_buf1"]]
    ps0, ps1 = P.tiles["ps0"], P.tiles["ps1"]
    ps2, ps3 = P.tiles["ps2"], P.tiles["ps3"]
    psA, psB = P.tiles["psA"], P.tiles["psB"]
    id32 = P.tiles["id32_sb"]

    def vedge():
        P.wait("vector", [("v", P.cnt["v"])])

    if MAMBA_STOP == "now":
        return
    def gload(out_ap, in_ap):
        P.op("gpsimd", lambda e, out_ap=out_ap, in_ap=in_ap: e.dma_start(
            out=out_ap, in_=in_ap), incs=[("dmam_g", 16)])

    gload(winT[:].rearrange("p (k c) -> p k c", k=NDK),
          aps["m_winT"].rearrange("(k p) c -> p k c", p=128))
    gload(opT[:].rearrange("p (k c) -> p k c", k=NDC),
          aps["m_opT"].rearrange("(k p) c -> p k c", p=128))
    gload(xpwT[:].rearrange("p (k c) -> p k c", k=NDC),
          aps["m_xpwT"].rearrange("(k p) c -> p k c", p=128))
    gload(dtwT[:], aps["m_dtwT"])
    gload(basis[:], aps["basis16"])
    w_g = P.cnt["dmam_g"]
    for (dst, key) in ((A_sb, "m_Alog"), (cw, "m_convw"), (cb, "m_convb"),
                       (D_sb, "m_D"), (dtb, "m_dtb"),
                       (P.tiles["lnw_sb"], "m_lnw"), (P.tiles["lnb_sb"], "m_lnb")):
        P.op("sync", lambda e, dst=dst, key=key: e.dma_start(
            out=dst[:], in_=aps[key]), incs=[("dmam", 16)])
    w_f = P.cnt["dmam"]
    if MAMBA_STOP == "w":
        P.wait("tensor", [("dmam", w_f), ("dmam_g", w_g)])
        P.wait("vector", [("dmam", w_f), ("dmam_g", w_g)])
        P.wait("scalar", [("dmam", w_f), ("dmam_g", w_g)])
        return
    s_A = P.op("scalar", lambda e: e.activation(A_sb[:], A_sb[:], AF.Exp),
               waits=[("dmam", w_f)], incs=[("s", 1)])[0]
    P.op("vector", lambda e: e.tensor_scalar_mul(A_sb[:], A_sb[:], -1.0),
         waits=[("s", s_A), ("v", P.cnt["v"])], incs=[("v", 1)])
    if MAMBA_STOP == "A":
        return

    v_ln, s_ln = emit_ln(P, x_ready=x_ready, mamba_double=True,
                         gain=(P.tiles["lnw_sb"], P.tiles["lnb_sb"]),
                         gain_ready=[("dmam", w_f)])

    if MAMBA_STOP == "ln":
        return
    d_ag = P.op("sync", lambda e: e.dma_start(
        out=aps["ag_in"].rearrange("(k p) c -> p k c", p=128),
        in_=xf[:].rearrange("p (k c) -> p k c", k=NDK)),
        waits=[("v", v_ln), ("s", s_ln)], incs=[("dag", 16)])[0]
    P.op("gpsimd", lambda e: e.collective_compute(
        "AllGather", ALU.bypass, replica_groups=GROUPS,
        ins=[aps["ag_in"].opt()], outs=[aps["ag_out"].opt()]),
        waits=[("dag", d_ag), ("dmam_g", w_g)], incs=[("cc", 1)])
    cc_ag = P.cnt["cc"]

    if MAMBA_STOP == "ag":
        return
    xin_sem = ["dxi0", "dxi1"]
    evac_done = []
    conv_v = []
    xp_evac_v = []
    oc_evac = []
    xp_mm = []
    agr = aps["ag_out"].rearrange("(r k p) c -> r k p c", k=NDK, p=128)
    P.wait("sync", [("cc", cc_ag)])
    P.wait("tensor", [("dmam_g", w_g), ("dma", P.cnt["dma"])])
    for sc in range(NSC):
        par = sc % 2
        r0, off = (sc * 512) // 2048, (sc * 512) % 2048
        wts = [("s", evac_done[sc - 2])] if sc >= 2 else []
        P.op("sync", lambda e, par=par, r0=r0, off=off: e.dma_start(
            out=xinb[par][:].rearrange("p (k c) -> p k c", k=NDK),
            in_=agr[r0, :, :, off:off + 512].transpose([1, 0, 2])),
            waits=wts, incs=[(xin_sem[par], 16)])
        x_rdy = P.cnt[xin_sem[par]]
        P.wait("tensor", [(xin_sem[par], x_rdy)])
        xc3 = xcin[:].rearrange("p (dc c) -> p dc c", dc=NDC)
        if sc == 0:
            P.op("vector", lambda e, xc3=xc3: e.memset(xc3[:, :, 0:3], 0.0),
                 waits=[("v", P.cnt["v"]), ("dag", d_ag)], incs=[("v", 1)])
        else:
            P.op("vector", lambda e, xc3=xc3: e.tensor_copy(
                xc3[:, :, 0:3], xc3[:, :, 512:515]),
                waits=[("v", P.cnt["v"]), ("s", evac_done[sc - 1])], incs=[("v", 1)])
        v_halo = P.cnt["v"]
        for oc in range(8):
            g = sc * 8 + oc
            pp = [ps0, ps1][g % 2]
            if g >= 2:
                P.wait("tensor", [("s", oc_evac[g - 2])])
            for dmk in range(NDK):
                ic = [("mm", 1)] if dmk == NDK - 1 else []
                P.op("tensor", lambda e, oc=oc, dmk=dmk, par=par, pp=pp: e.matmul(
                    pp[:], winT[:, dmk * 1024 + oc * 128: dmk * 1024 + oc * 128 + 128],
                    xinb[par][:, dmk * 512:(dmk + 1) * 512],
                    start=(dmk == 0), stop=(dmk == NDK - 1)), incs=ic)
            mm_oc = P.cnt["mm"]
            if oc < 4:
                dst = xcin[:, oc * 515 + 3: oc * 515 + 515]
            else:
                dst = zc[:, (oc - 4) * 512: (oc - 4) * 512 + 512]
            sw = [("mm", mm_oc)]
            if oc < 4:
                sw.append(("v", v_halo))
            if sc >= 1 and oc >= 4:
                sw.append(("dzs", 32 * sc))
            P.op("scalar", lambda e, dst=dst, pp=pp: e.activation(dst, pp[:], AF.Copy),
                 waits=sw, incs=[("s", 1)])
            oc_evac.append(P.cnt["s"])
        evac_done.append(P.cnt["s"])
        P.op("scalar", lambda e, sc=sc: e.dma_start(
            out=aps["z_sp"][:, :, sc * 512:(sc + 1) * 512].transpose([1, 0, 2]),
            in_=zc[:].rearrange("p (dc c) -> p dc c", dc=NDC)),
            waits=[("s", P.cnt["s"])], incs=[("dzs", 16)])
        for dc in range(NDC):
            P.op("vector", lambda e, dc=dc: e.tensor_single_scalar(
                cacc[:], xcin[:, dc * 515 + 0: dc * 515 + 512],
                cw[:, dc * 4:dc * 4 + 1], ALU.mult),
                waits=[("s", P.cnt["s"]), ("v", P.cnt["v"])], incs=[("v", 1)])
            for j in range(1, 4):
                P.op("vector", lambda e, dc=dc, j=j: e.scalar_tensor_tensor(
                    cacc[:], xcin[:, dc * 515 + j: dc * 515 + j + 512],
                    cw[:, dc * 4 + j:dc * 4 + j + 1], cacc[:], ALU.mult, ALU.add),
                    waits=[("v", P.cnt["v"])], incs=[("v", 1)])
            v_c = P.cnt["v"]
            P.op("scalar", lambda e, dc=dc: e.activation(
                xcv[:, dc * 512:(dc + 1) * 512], cacc[:], AF.Silu,
                bias=cb[:, dc:dc + 1]),
                waits=[("v", v_c), ("dzs", 32 * sc + 16) if sc >= 1 else ("v", 0)],
                incs=[("s", 1)])
        s_conv = P.cnt["s"]
        conv_v.append(P.cnt["v"])
        P.op("scalar", lambda e, sc=sc: e.dma_start(
            out=aps["xcv_sp"][:, :, sc * 512:(sc + 1) * 512].transpose([1, 0, 2]),
            in_=xcv[:].rearrange("p (dc c) -> p dc c", dc=NDC)),
            waits=[("s", P.cnt["s"])], incs=[("dzs", 16)])
        P.wait("tensor", [("s", s_conv)])
        for tt in range(4):
            pp = [ps2, ps3][tt % 2]
            if sc * 4 + tt >= 2:
                P.wait("tensor", [("v", xp_evac_v[sc * 4 + tt - 2])])
            for dc in range(NDC):
                ic = [("mm", 1)] if dc == NDC - 1 else []
                P.op("tensor", lambda e, tt=tt, dc=dc, pp=pp: e.matmul(
                    pp[:, 0:96], xcv[:, dc * 512 + tt * 128: dc * 512 + tt * 128 + 128],
                    xpwT[:, dc * 96:(dc + 1) * 96],
                    start=(dc == 0), stop=(dc == NDC - 1)), incs=ic)
            mm_t = P.cnt["mm"]
            sl = mscr[:, (tt % 2) * 512:(tt % 2) * 512 + 96]
            P.op("vector", lambda e, sl=sl, pp=pp: e.tensor_copy(sl, pp[:, 0:96]),
                 waits=[("mm", mm_t), ("v", P.cnt["v"]),
                        ("dxdb", 16 * (sc * 4 + tt))],
                 incs=[("v", 1)])
            xp_evac_v.append(P.cnt["v"])
            P.op("sync", lambda e, sc=sc, tt=tt, sl=sl: e.dma_start(
                out=aps["xdbl_part"][sc * 512 + tt * 128: sc * 512 + tt * 128 + 128, :],
                in_=sl),
                waits=[("v", xp_evac_v[-1])], incs=[("dxdb", 16)])
    P.op("gpsimd", lambda e: e.collective_compute(
        "AllReduce", ALU.add, replica_groups=GROUPS,
        ins=[aps["xdbl_part"].opt()], outs=[aps["xdbl_sum"].opt()]),
        waits=[("dxdb", P.cnt["dxdb"])], incs=[("cc", 1)])
    cc_ar = P.cnt["cc"]
    P.wait("sync", [("cc", cc_ar)])
    P.wait("scalar", [("cc", cc_ar), ("dzs", P.cnt["dzs"])])
    if MAMBA_STOP == "ar":
        return

    P.op("vector", lambda e: e.memset(h_sb[:], 0.0),
         waits=[("v", P.cnt["v"])], incs=[("v", 1)])
    op_evac_v = []
    dt_s = []
    last_bc_mm = 0
    gate_v = []
    gate_s = []
    xdsum_r = aps["xdbl_sum"].rearrange("(t p) c -> p t c", p=128)
    for sc in range(NSC):
        wts = [("mm", last_bc_mm)] if sc >= 1 else [("cc", cc_ar)]
        d_xd = P.op("sync", lambda e, sc=sc: e.dma_start(
            out=xdbl_tm[:].rearrange("p (t c) -> p t c", t=4),
            in_=xdsum_r[:, sc * 4:sc * 4 + 4, :]),
            waits=wts, incs=[("dxd", 16)])[0]
        wts2 = [("v", gate_v[sc - 1]), ("s", gate_s[sc - 1])] if sc >= 1 else []
        d_xc = P.op("scalar", lambda e, sc=sc: e.dma_start(
            out=xcv[:].rearrange("p (dc c) -> p dc c", dc=NDC),
            in_=aps["xcv_sp"][:, :, sc * 512:(sc + 1) * 512].transpose([1, 0, 2])),
            waits=wts2, incs=[("dxc", 16)])[0]
        d_z = P.op("scalar", lambda e, sc=sc: e.dma_start(
            out=zc[:].rearrange("p (dc c) -> p dc c", dc=NDC),
            in_=aps["z_sp"][:, :, sc * 512:(sc + 1) * 512].transpose([1, 0, 2])),
            incs=[("dzr", 16)])[0]
        P.wait("tensor", [("dxd", d_xd), ("v", P.cnt["v"])])
        tr_mm = []
        tr_ev = []
        for tt in range(4):
            pp = [ps2, ps3][tt % 2]
            if tt >= 2:
                P.wait("tensor", [("v", tr_ev[tt - 2])])
            P.op("tensor", lambda e, tt=tt, pp=pp: e.transpose(
                pp[0:80, 0:128], xdbl_tm[:, tt * 96: tt * 96 + 80], id32[:]))
            mm1 = P.op("tensor", lambda e, tt=tt, pp=pp: e.transpose(
                pp[0:16, 128:256], xdbl_tm[:, tt * 96 + 80: tt * 96 + 96], id32[:]),
                incs=[("mm", 1)])[0]
            tr_mm.append(mm1)
            P.op("vector", lambda e, tt=tt, pp=pp: e.tensor_copy(
                dtraw[:, tt * 128:(tt + 1) * 128], pp[0:64, 0:128]),
                waits=[("mm", mm1), ("v", P.cnt["v"])], incs=[("v", 1)])
            P.op("vector", lambda e, tt=tt, pp=pp: e.tensor_copy(
                Bfm[:, tt * 128:(tt + 1) * 128], pp[64:80, 0:128]),
                waits=[("v", P.cnt["v"])], incs=[("v", 1)])
            P.op("vector", lambda e, tt=tt, pp=pp: e.tensor_copy(
                Cfm[:, tt * 128:(tt + 1) * 128], pp[0:16, 128:256]),
                waits=[("v", P.cnt["v"])], incs=[("v", 1)])
            tr_ev.append(P.cnt["v"])
        v_tr = P.cnt["v"]
        P.wait("tensor", [("v", v_tr)])
        for dc in range(NDC):
            pp = [ps0, ps1][dc % 2]
            if dc >= 2:
                P.wait("tensor", [("s", dt_s[-2])])
            mm_d = P.op("tensor", lambda e, dc=dc, pp=pp: e.matmul(
                pp[:], dtwT[:, dc * 128:(dc + 1) * 128], dtraw[:],
                start=True, stop=True), incs=[("mm", 1)])[0]
            t0 = mscr[:, 0:512]
            t1 = mscr[:, 512:1024]
            P.op("vector", lambda e, pp=pp, t0=t0, dc=dc: e.tensor_single_scalar(
                t0, pp[:], dtb[:, dc:dc + 1], ALU.add),
                waits=[("mm", mm_d), ("v", P.cnt["v"]), ("s", P.cnt["s"]),
                       ("dyp", P.cnt.get("dyp", 0))],
                incs=[("v", 1)])
            vedge()
            s_abs = P.op("scalar", lambda e, t0=t0, t1=t1: e.activation(t1, t0, AF.Abs),
                         waits=[("v", P.cnt["v"]), ("s", P.cnt["s"])], incs=[("s", 1)])[0]
            P.wait("scalar", [("s", s_abs)])
            s0 = P.op("scalar", lambda e, t1=t1: e.activation(t1, t1, AF.Exp, scale=-1.0),
                      incs=[("s", 1)])[0]
            P.wait("scalar", [("s", s0)])
            s1 = P.op("scalar", lambda e, t1=t1: e.activation(t1, t1, AF.Ln, bias=cst(P, "one")),
                      incs=[("s", 1)])[0]
            P.op("vector", lambda e, t0=t0: e.tensor_relu(t0, t0),
                 waits=[("v", P.cnt["v"]), ("s", s_abs)], incs=[("v", 1)])
            vedge()
            P.op("vector", lambda e, t0=t0, t1=t1, dc=dc: e.tensor_add(
                dts[:, dc * 512:(dc + 1) * 512], t0, t1),
                waits=[("s", s1)], incs=[("v", 1)])
            dt_s.append(P.cnt["s"])
        v_dt = P.cnt["v"]
        P.wait("vector", [("dxc", d_xc)])
        for dc in range(NDC):
            P.op("vector", lambda e, dc=dc: e.tensor_mul(
                dtu[:, dc * 512:(dc + 1) * 512], dts[:, dc * 512:(dc + 1) * 512],
                xcv[:, dc * 512:(dc + 1) * 512]),
                waits=[("v", P.cnt["v"])], incs=[("v", 1)])
            P.op("vector", lambda e, dc=dc: e.tensor_single_scalar(
                yacc[:, dc * 512:(dc + 1) * 512], xcv[:, dc * 512:(dc + 1) * 512],
                D_sb[:, dc:dc + 1], ALU.mult),
                waits=[("v", P.cnt["v"])], incs=[("v", 1)])
        v_du = P.cnt["v"]
        P.wait("tensor", [("v", v_du)])
        for s_i in range(DST):
            P.wait("tensor", [("v", P.cnt["v"])])
            mmB = P.op("tensor", lambda e, s_i=s_i: e.matmul(
                psA[:], basis[:, s_i * 128:(s_i + 1) * 128], Bfm[:],
                start=True, stop=True), incs=[("mm", 1)])[0]
            mmC = P.op("tensor", lambda e, s_i=s_i: e.matmul(
                psB[:], basis[:, s_i * 128:(s_i + 1) * 128], Cfm[:],
                start=True, stop=True), incs=[("mm", 1)])[0]
            last_bc_mm = P.cnt["mm"]
            for dc in range(NDC):
                col = dc * DST + s_i
                P.op("scalar", lambda e, dc=dc, col=col: e.activation(
                    scanA[:], dts[:, dc * 512:(dc + 1) * 512], AF.Exp,
                    scale=A_sb[:, col:col + 1]),
                    waits=[("v", P.cnt["v"]), ("s", P.cnt["s"])], incs=[("s", 1)])
                s_a = P.cnt["s"]
                P.op("vector", lambda e, dc=dc: e.tensor_mul(
                    scanB[:], dtu[:, dc * 512:(dc + 1) * 512], psA[:]),
                    waits=[("mm", mmB), ("v", P.cnt["v"])], incs=[("v", 1)])
                vedge()
                P.op("vector", lambda e, col=col: e.tensor_tensor_scan(
                    states[:], scanA[:], scanB[:], h_sb[:, col:col + 1],
                    ALU.mult, ALU.add),
                    waits=[("s", s_a)], incs=[("v", 1)])
                vedge()
                P.op("vector", lambda e, col=col: e.tensor_copy(
                    h_sb[:, col:col + 1], states[:, 511:512]), incs=[("v", 1)])
                vedge()
                P.op("vector", lambda e: e.tensor_mul(scanB[:], states[:], psB[:]),
                     waits=[("mm", mmC)], incs=[("v", 1)])
                vedge()
                P.op("vector", lambda e, dc=dc: e.tensor_add(
                    yacc[:, dc * 512:(dc + 1) * 512],
                    yacc[:, dc * 512:(dc + 1) * 512], scanB[:]), incs=[("v", 1)])
                vedge()
        for dc in range(NDC):
            P.op("scalar", lambda e, dc=dc: e.activation(
                sz[:], zc[:, dc * 512:(dc + 1) * 512], AF.Silu),
                waits=[("v", P.cnt["v"]), ("s", P.cnt["s"]), ("dzr", d_z)],
                incs=[("s", 1)])
            P.op("vector", lambda e, dc=dc: e.tensor_mul(
                yg[:, dc * 512:(dc + 1) * 512], yacc[:, dc * 512:(dc + 1) * 512],
                sz[:]), waits=[("s", P.cnt["s"]), ("v", P.cnt["v"])], incs=[("v", 1)])
        gate_v.append(P.cnt["v"])
        gate_s.append(P.cnt["s"])
        P.wait("tensor", [("v", gate_v[-1])])
        for tt in range(4):
            for dh in range(2):
                u = len(op_evac_v)
                pp = [ps2, ps3][u % 2]
                if u >= 2:
                    P.wait("tensor", [("v", op_evac_v[u - 2])])
                for dc in range(NDC):
                    ic = [("mm", 1)] if dc == NDC - 1 else []
                    P.op("tensor", lambda e, tt=tt, dh=dh, dc=dc, pp=pp: e.matmul(
                        pp[:], yg[:, dc * 512 + tt * 128: dc * 512 + tt * 128 + 128],
                        opT[:, dc * D + dh * 512: dc * D + dh * 512 + 512],
                        start=(dc == 0), stop=(dc == NDC - 1)), incs=ic)
                mm_o = P.cnt["mm"]
                sl = mscr[:, (u % 2) * 512:(u % 2) * 512 + 512]
                P.op("vector", lambda e, sl=sl, pp=pp: e.tensor_copy(sl, pp[:]),
                     waits=[("mm", mm_o), ("v", P.cnt["v"]),
                            ("dyp", 16 * u)],
                     incs=[("v", 1)])
                op_evac_v.append(P.cnt["v"])
                P.op("sync", lambda e, sc=sc, tt=tt, dh=dh, sl=sl: e.dma_start(
                    out=aps["yprt"][sc * 512 + tt * 128: sc * 512 + tt * 128 + 128,
                                    dh * 512:(dh + 1) * 512],
                    in_=sl), waits=[("v", op_evac_v[-1])], incs=[("dyp", 16)])
    if MAMBA_STOP == "rs":
        return
    P.op("gpsimd", lambda e: e.collective_compute(
        "ReduceScatter", ALU.add, replica_groups=GROUPS,
        ins=[aps["yprt"].opt()], outs=[aps["rs_out"].opt()]),
        waits=[("dyp", P.cnt["dyp"])], incs=[("cc", 1)])
    cc_rs = P.cnt["cc"]
    m4 = P.tiles["m4buf"]
    P.wait("sync", [("cc", cc_rs)])
    rsr = aps["rs_out"].rearrange("(t p) d -> p t d", p=128)
    for rnd in range(8):
        wts = [("v", P.cnt["v"])] if rnd else []
        d_r = P.op("sync", lambda e, rnd=rnd: e.dma_start(
            out=m4[:].rearrange("p (t d) -> p t d", t=2),
            in_=rsr[:, rnd * 2:rnd * 2 + 2, :]),
            waits=wts, incs=[("drs", 16)])[0]
        for j in range(2):
            t = rnd * 2 + j
            P.op("vector", lambda e, t=t, j=j: e.tensor_add(
                x_sb[:, t * D:(t + 1) * D], x_sb[:, t * D:(t + 1) * D],
                m4[:, j * D:(j + 1) * D]),
                waits=[("drs", d_r), ("v", P.cnt["v"])], incs=[("v", 1)])


def build_chain(parts, debug_dump=()):
    nc = bass.Bass("TRN2", target_bir_lowering=False)
    es = contextlib.ExitStack()
    inp = {}

    def di(name, shape, dtype=F32):
        if name not in inp:
            inp[name] = nc.dram_tensor(name, list(shape), dtype, kind="ExternalInput").ap()
        return inp[name]

    x_in = di("x", (T, D))
    ident = di("ident", (128, 128), BF16)
    onesbf = di("onesbf", (1, 128), BF16)
    ident32 = di("ident32", (128, 128), F32)
    xo = nc.dram_tensor("xo", [T, D], F32, kind="ExternalOutput").ap()

    P = Prog(nc, es)
    P.ident32_ap = ident32
    alloc_common(P)
    if "l1ffn" in parts:
        alloc_l1ffn(P)
    if "moe" in parts:
        alloc_moe(P)
    if "mamba" in parts:
        alloc_mamba(P)
    if "l1attn" in parts:
        alloc_l1attn(P)

    emit_load_consts(P, ident, onesbf)
    emit_load_x(P, x_in)
    x_ready = [("dma", P.cnt["dma"])]

    for part in parts:
        if part == "l1ffn":
            th = di("L1_ffn_theta", (D, D))
            tn = di("L1_ffn_theta_noise", (D, D))
            ph = di("L1_ffn_phi", (D, D))
            pn = di("L1_ffn_phi_noise", (D, D))
            emit_l1ffn_prep(P, th, tn, ph, pn)
            prep_s = P.cnt["s"]
            v_done, s_done = emit_ln(P, x_ready=x_ready)
            emit_l1ffn_main(P, xf_ready=[("v", v_done), ("s", max(s_done, prep_s))])
        elif part == "l1attn":
            aps = {}
            for nm in ("L1_attn_wq", "L1_attn_wk", "L1_attn_wv", "L1_attn_wo",
                       "L1_attn_w_d"):
                aps[nm] = di(nm, (D, D))
            aps["L1_attn_w_ff1"] = di("L1_attn_w_ff1", (D, 4 * D))
            aps["L1_attn_w_ff2"] = di("L1_attn_w_ff2", (4 * D, D))
            for nm in ("L1_attn_bo", "L1_attn_b_ff2", "L1_attn_b_d"):
                aps[nm] = di(nm, (D,))
            for nm in ("a_ln1g", "a_ln1b", "a_ln2g", "a_ln2b"):
                aps[nm] = di(nm, (128, NDK))
            aps["a_bff1"] = di("a_bff1", (128, 32))
            aps["x_old"] = nc.dram_tensor("x_old", [T, D], F32).ap()
            aps["ctx_part"] = nc.dram_tensor("ctx_part", [128, 8 * 264], F32).ap()
            aps["ctx_sum"] = nc.dram_tensor("ctx_sum", [128, 8 * 264], F32).ap()
            emit_l1attn(P, aps, x_ready)
        elif part == "mamba":
            aps = {}
            for (nm, shp, dt) in (
                    ("m_winT", (D, 1024), F32), ("m_opT", (DI, D), F32),
                    ("m_xpwT", (DI, 96), F32), ("m_dtwT", (64, DI), F32),
                    ("basis16", (16, 2048), BF16), ("m_Alog", (128, NDC * DST), F32),
                    ("m_convw", (128, NDC * 4), F32), ("m_convb", (128, NDC), F32),
                    ("m_D", (128, NDC), F32), ("m_dtb", (128, NDC), F32),
                    ("m_lnw", (128, NDK), F32), ("m_lnb", (128, NDK), F32)):
                aps[nm] = di(nm, shp, dt)
            for (nm, shp, dt) in (
                    ("ag_in", (D, T), BF16), ("ag_out", (4 * D, T), BF16),
                    ("xdbl_part", (SEQ, 96), F32), ("xdbl_sum", (SEQ, 96), F32),
                    ("z_sp", (NDC, 128, SEQ), BF16), ("xcv_sp", (NDC, 128, SEQ), BF16),
                    ("yprt", (SEQ, D), F32), ("rs_out", (T, D), F32)):
                aps[nm] = nc.dram_tensor(nm, list(shp), dt).ap()
            emit_mamba(P, aps, x_ready)
        elif part == "moe":
            rw = di("L0_ffn_router_w", (NE, D))
            W1 = di("L0_ffn_W1", (NE, D, 4096))
            b1 = di("L0_ffn_b1", (NE, 4096))
            W2 = di("L0_ffn_W2", (NE, 4096, D))
            b2 = di("L0_ffn_b2", (NE, D))
            emit_moe_prep(P, rw, b1, b2)
            v_done, s_done = emit_ln(P, x_ready=x_ready)
            xf_ready = [("v", v_done), ("s", s_done)]
            rv = emit_moe_router(P, xf_ready)
            emit_moe_bias(P, rv)
            emit_moe_experts(P, W1, W2, xf_ready)
        else:
            raise NotImplementedError(part)
        x_ready = [("v", P.cnt["v"]), ("s", P.cnt["s"])]

    emit_store_x(P, xo)
    if debug_dump:
        for name in debug_dump:
            tile = P.tiles[name]
            shp = [tile.shape[0], tile.shape[1]]
            dt = tile.dtype
            o = nc.dram_tensor("dbg_" + name, shp, dt, kind="ExternalOutput").ap()
            P.op("sync", lambda e, o=o, tile=tile: e.dma_start(out=o, in_=tile[:]),
                 incs=[("st", 16)])
        P.wait("sync", [("st", P.cnt["st"])])
    P.finalize()
    es.close()
    return nc


IDENT_BF = np.eye(128, dtype=np.float32).astype(BF)
ONES_BF = np.ones((1, 128), dtype=np.float32).astype(BF)
IDENT_F32 = np.eye(128, dtype=np.float32)
BASIS16 = np.zeros((16, 2048), np.float32)
for _s in range(16):
    BASIS16[_s, _s * 128:(_s + 1) * 128] = 1.0
BASIS16 = BASIS16.astype(BF)


def _col128(v):
    return np.ascontiguousarray(v.reshape(-1, 128).T)


def shard_inputs(inputs, parts=()):
    x = np.ascontiguousarray(np.asarray(inputs["L0_attn_x"]))
    maps = []
    if "mamba" in parts:
        winT = np.asarray(inputs["L0_attn_in_proj_w"]).T
        opT = np.asarray(inputs["L0_attn_out_proj_w"]).T
        xpwT = np.asarray(inputs["L0_attn_x_proj_w"]).T
        dtwT = np.asarray(inputs["L0_attn_dt_proj_w"]).T
        alog = np.asarray(inputs["L0_attn_A_log"])
        cwv = np.asarray(inputs["L0_attn_conv_w"])[:, 0, :]
        cbv = np.asarray(inputs["L0_attn_conv_b"])
        Dv = np.asarray(inputs["L0_attn_D"])
        dtbv = np.asarray(inputs["L0_attn_dt_proj_b"])
        lnw = np.asarray(inputs["L0_attn_ln_w"])
        lnb = np.asarray(inputs["L0_attn_ln_b"])
    for c in range(8):
        b, s = c // 4, c % 4
        m = {"x": np.ascontiguousarray(x[b, s * T:(s + 1) * T]),
             "ident": IDENT_BF, "onesbf": ONES_BF, "ident32": IDENT_F32}
        if "moe" in parts or "l1ffn" in parts or "l1attn" in parts:
            for kk in inputs:
                if kk.startswith("L0_ffn_") or kk.startswith("L1_"):
                    m[kk] = np.ascontiguousarray(np.asarray(inputs[kk]))
        if "l1attn" in parts:
            m["a_ln1g"] = _col128(np.asarray(inputs["L1_attn_ln1_g"]))
            m["a_ln1b"] = _col128(np.asarray(inputs["L1_attn_ln1_b"]))
            m["a_ln2g"] = _col128(np.asarray(inputs["L1_attn_ln2_g"]))
            m["a_ln2b"] = _col128(np.asarray(inputs["L1_attn_ln2_b"]))
            m["a_bff1"] = _col128(np.asarray(inputs["L1_attn_b_ff1"]))
        if "mamba" in parts:
            r = s
            sl = slice(r * DI, (r + 1) * DI)
            m["m_winT"] = np.ascontiguousarray(
                np.concatenate([winT[:, sl], winT[:, 2048 + r * DI:2048 + (r + 1) * DI]], axis=1))
            m["m_opT"] = np.ascontiguousarray(opT[sl])
            m["m_xpwT"] = np.ascontiguousarray(xpwT[sl])
            m["m_dtwT"] = np.ascontiguousarray(dtwT[:, sl])
            m["basis16"] = BASIS16
            m["m_Alog"] = np.ascontiguousarray(
                alog[sl].reshape(NDC, 128, DST).transpose(1, 0, 2).reshape(128, NDC * DST))
            m["m_convw"] = np.ascontiguousarray(
                cwv[sl].reshape(NDC, 128, 4).transpose(1, 0, 2).reshape(128, NDC * 4))
            m["m_convb"] = _col128(cbv[sl])
            m["m_D"] = _col128(Dv[sl])
            m["m_dtb"] = _col128(dtbv[sl])
            m["m_lnw"] = _col128(lnw)
            m["m_lnb"] = _col128(lnb)
        maps.append(m)
    return maps


_PROGS = {}


def _prog(part):
    if part not in _PROGS:
        _PROGS[part] = build_chain([part])
    return _PROGS[part]


def _input_names(nc):
    import concourse.mybir as mb
    names = set()
    for alloc in nc.m.functions[0].allocations:
        if isinstance(alloc, mb.MemoryLocationSet) and alloc.kind == "ExternalInput":
            names.add(alloc.memorylocations[0].name)
    return names


def kernel(**inputs):
    x = np.asarray(inputs["L0_attn_x"])
    chain = ["moe", "mamba", "moe", "l1ffn", "l1attn", "l1ffn"]
    cur = {"L0_attn_x": x}
    cur.update({kk: v for kk, v in inputs.items() if kk != "L0_attn_x"})
    for part in chain:
        nc = _prog(part)
        names = _input_names(nc)
        maps = shard_inputs(cur, parts=[part])
        maps = [{kk: v for kk, v in m.items() if kk in names} for m in maps]
        res = run_bass_kernel_spmd(nc, maps, core_ids=list(range(8)))
        xo = np.zeros((2, 8192, 1024), np.float32)
        for c in range(8):
            b, s = c // 4, c % 4
            xo[b, s * T:(s + 1) * T] = res.results[c]["xo"]
        cur["L0_attn_x"] = xo
    return cur["L0_attn_x"]



# version 05
# speedup vs baseline: 1.0146x; 1.0146x over previous
import contextlib
import numpy as np
import ml_dtypes
import concourse.bass as bass
import concourse.mybir as mybir
from concourse.bass_utils import run_bass_kernel_spmd

F32 = mybir.dt.float32
BF16 = mybir.dt.bfloat16
AF = mybir.ActivationFunctionType
ALU = mybir.AluOpType
AX = mybir.AxisListType

T = 2048
NT = T // 128
D = 1024
NDK = D // 128
PI = float(np.pi)
ENGINES = ("sync", "vector", "scalar", "tensor", "gpsimd")
GROUPS = [[0, 1, 2, 3], [4, 5, 6, 7]]
BF = ml_dtypes.bfloat16


class Prog:

    def __init__(self, nc, es):
        self.nc = nc
        self.es = es
        self.cmds = {e: [] for e in ENGINES}
        self.sem = {}
        self.cnt = {}
        self.tiles = {}
        for s in ("dma", "st", "mm", "v", "s", "g", "cc"):
            self.ensure_sem(s)

    def ensure_sem(self, name):
        if name not in self.sem:
            self.sem[name] = self.es.enter_context(self.nc.semaphore(name))
            self.cnt[name] = 0
        return self.sem[name]

    def bump(self, name, n):
        self.cnt[name] += n
        return self.cnt[name]

    def op(self, eng, fn, waits=(), incs=()):
        totals = [(self.ensure_sem(s), self.bump(s, n)) for (s, n) in incs]
        wl = [(self.ensure_sem(s), v) for (s, v) in waits if v > 0]

        def cmd(e):
            for (s, v) in wl:
                e.wait_ge(s, v)
            inst = fn(e)
            for i, (s, _tot) in enumerate(totals):
                inst = inst.then_inc(s, incs[i][1])
        self.cmds[eng].append(cmd)
        return [t for (_s, t) in totals]

    def wait(self, eng, waits):
        wl = [(self.ensure_sem(s), v) for (s, v) in waits if v > 0]

        def cmd(e):
            for (s, v) in wl:
                e.wait_ge(s, v)
        self.cmds[eng].append(cmd)

    def sbuf(self, name, shape, dtype):
        t = self.es.enter_context(self.nc.sbuf_tensor(name, list(shape), dtype))
        self.tiles[name] = t
        return t

    def psum(self, name, shape=(128, 512), dtype=F32):
        t = self.es.enter_context(self.nc.psum_tensor(name, list(shape), dtype))
        self.tiles[name] = t
        return t

    def finalize(self):
        with self.nc.Block() as block:
            for eng in ENGINES:
                cmds = self.cmds[eng]
                if not cmds:
                    continue
                getattr(block, eng)(lambda e, cmds=cmds: [c(e) for c in cmds])


CONSTS = {"eps": (0, 1e-5), "halfpi": (1, PI / 2), "neg1": (2, -1.0), "one": (3, 1.0)}


def alloc_common(P):
    P.sbuf("const_sb", (128, 8), F32)
    P.sbuf("x_sb", (128, NT * D), F32)
    P.sbuf("xln_sb", (128, NT * D), BF16)
    P.sbuf("xf_sb", (128, NDK * T), BF16)
    P.sbuf("stat_sb", (128, NT * 8), F32)
    P.sbuf("ident_sb", (128, 128), BF16)
    P.sbuf("id32_sb", (128, 128), F32)
    P.sbuf("onesbf_sb", (1, 128), BF16)
    for i in range(4):
        P.psum(f"ps{i}")
    P.psum("pt0", (128, 1024), BF16)
    P.psum("pt1", (128, 1024), BF16)
    P.psum("psA")
    P.psum("psB")


def cst(P, name):
    return P.tiles["const_sb"][:, CONSTS[name][0]:CONSTS[name][0] + 1]


def emit_load_consts(P, ident_ap, onesbf_ap):
    P.op("sync", lambda e: e.dma_start(out=P.tiles["ident_sb"][:], in_=ident_ap),
         incs=[("dma", 16)])
    P.op("sync", lambda e: e.dma_start(out=P.tiles["onesbf_sb"][:], in_=onesbf_ap),
         incs=[("dma", 16)])
    P.op("sync", lambda e: e.dma_start(out=P.tiles["id32_sb"][:], in_=P.ident32_ap),
         incs=[("dma", 16)])
    cs = P.tiles["const_sb"]
    for (_nm, (slot, val)) in CONSTS.items():
        P.op("vector", lambda e, slot=slot, val=val: e.memset(cs[:, slot:slot + 1], val))
    P.op("vector", lambda e: e.memset(cs[:, 7:8], 0.0), incs=[("v", 1)])
    P.wait("scalar", [("v", P.cnt["v"])])
    P.wait("gpsimd", [("v", P.cnt["v"])])


def emit_load_x(P, x_hbm):
    x_sb = P.tiles["x_sb"]
    P.op("sync", lambda e: e.dma_start(
        out=x_sb[:].rearrange("p (t d) -> p t d", t=NT),
        in_=x_hbm.rearrange("(t p) d -> p t d", p=128)),
        incs=[("dma", 16)])


def emit_store_x(P, x_hbm_out):
    x_sb = P.tiles["x_sb"]
    P.op("sync", lambda e: e.dma_start(
        out=x_hbm_out.rearrange("(t p) d -> p t d", p=128),
        in_=x_sb[:].rearrange("p (t d) -> p t d", t=NT)),
        waits=[("v", P.cnt["v"]), ("s", P.cnt["s"])],
        incs=[("st", 16)])
    P.wait("sync", [("st", P.cnt["st"])])


def emit_ln(P, x_ready=(), identity=False, gain=None, mamba_double=False,
            gain_ready=(), inplace=False):
    x_sb, xln, xf = P.tiles["x_sb"], P.tiles["xln_sb"], P.tiles["xf_sb"]
    st = P.tiles["stat_sb"]
    trash = xln
    sums = st[:, 0 * NT:1 * NT]
    sq = st[:, 1 * NT:2 * NT]
    mean = st[:, 2 * NT:3 * NT]
    var = st[:, 3 * NT:4 * NT]
    rstd = st[:, 4 * NT:5 * NT]
    nmr = st[:, 5 * NT:6 * NT]

    if not identity:
        P.wait("vector", x_ready)
        P.wait("scalar", x_ready)
        sq_base = P.cnt["s"]
        for t in range(NT):
            xt = x_sb[:, t * D:(t + 1) * D]
            P.op("vector", lambda e, xt=xt, t=t: e.tensor_reduce(
                sums[:, t:t + 1], xt, AX.X, ALU.add),
                incs=([("v", 1)] if t == NT - 1 else []))
            w = [("s", sq_base + t - 1)] if t >= 2 else []
            P.op("scalar", lambda e, xt=xt, t=t: e.activation(
                trash[:, (t % 2) * D:(t % 2) * D + D], xt, AF.Square,
                accum_out=sq[:, t:t + 1]), waits=w, incs=[("s", 1)])
        s_sq = P.cnt["s"]
        def vedge():
            P.wait("vector", [("v", P.cnt["v"])])
        P.op("vector", lambda e: e.tensor_scalar_mul(mean[:], sums[:], 1.0 / D),
             waits=[("v", P.cnt["v"])], incs=[("v", 1)])
        vedge()
        P.op("vector", lambda e: e.tensor_scalar_mul(var[:], sq[:], 1.0 / D),
             waits=[("s", s_sq)], incs=[("v", 1)])
        vedge()
        P.op("vector", lambda e: e.tensor_mul(sums[:], mean[:], mean[:]), incs=[("v", 1)])
        vedge()
        v_var = P.op("vector", lambda e: e.tensor_sub(var[:], var[:], sums[:]),
                     incs=[("v", 1)])[0]
        s_lnop = P.op("scalar", lambda e: e.activation(rstd[:], var[:], AF.Ln, bias=cst(P, "eps")),
             waits=[("v", v_var)], incs=[("s", 1)])[0]
        P.wait("scalar", [("s", s_lnop)])
        s_ln = P.op("scalar", lambda e: e.activation(rstd[:], rstd[:], AF.Exp, scale=-0.5),
                    incs=[("s", 1)])[0]
        P.op("vector", lambda e: e.scalar_tensor_tensor(
            nmr[:], mean[:], -1.0, rstd[:], ALU.mult, ALU.mult),
            waits=[("s", s_ln), ("v", P.cnt["v"])], incs=[("v", 1)])
        v_nmr = P.cnt["v"]
        rstd_a, nmr_a = rstd, nmr
        if mamba_double:
            rc = st[:, 6 * NT:7 * NT]
            nc2 = st[:, 7 * NT:8 * NT]

            def ved():
                P.wait("vector", [("v", P.cnt["v"])])
            P.op("vector", lambda e: e.tensor_mul(rc[:], rstd[:], rstd[:]),
                 waits=[("v", P.cnt["v"])], incs=[("v", 1)])
            ved()
            P.op("vector", lambda e: e.tensor_mul(rc[:], rc[:], var[:]), incs=[("v", 1)])
            v_t = P.cnt["v"]
            s_a = P.op("scalar", lambda e: e.activation(rc[:], rc[:], AF.Ln, bias=cst(P, "eps")),
                       waits=[("v", v_t)], incs=[("s", 1)])[0]
            P.wait("scalar", [("s", s_a)])
            s_b = P.op("scalar", lambda e: e.activation(rc[:], rc[:], AF.Exp, scale=-0.5),
                       incs=[("s", 1)])[0]
            P.op("vector", lambda e: e.tensor_mul(rc[:], rc[:], rstd[:]),
                 waits=[("s", s_b), ("v", P.cnt["v"])], incs=[("v", 1)])
            ved()
            P.op("vector", lambda e: e.scalar_tensor_tensor(
                nc2[:], mean[:], -1.0, rc[:], ALU.mult, ALU.mult), incs=[("v", 1)])
            v_nmr = P.cnt["v"]
            rstd_a, nmr_a = rc, nc2
        P.wait("scalar", [("v", v_nmr)])
        if inplace:
            for t in range(NT):
                P.op("scalar", lambda e, t=t: e.activation(
                    x_sb[:, t * D:(t + 1) * D], x_sb[:, t * D:(t + 1) * D],
                    AF.Identity, scale=rstd[:, t:t + 1], bias=nmr[:, t:t + 1]),
                    incs=[("s", 1)])
            return P.cnt["v"], P.cnt["s"]
        for t in range(NT):
            ic = [("s", 1)] if t == NT - 1 else []
            P.op("scalar", lambda e, t=t, rstd_a=rstd_a, nmr_a=nmr_a: e.activation(
                xln[:, t * D:(t + 1) * D], x_sb[:, t * D:(t + 1) * D], AF.Identity,
                scale=rstd_a[:, t:t + 1], bias=nmr_a[:, t:t + 1]), incs=ic)
        if mamba_double:
            s_apply = P.cnt["s"]
            u_t = P.tiles["mscr"][:, 0:D]
            P.wait("vector", [("s", s_apply)])
            for t in range(NT):
                P.op("scalar", lambda e, t=t, u_t=u_t: e.activation(
                    u_t, x_sb[:, t * D:(t + 1) * D], AF.Identity,
                    scale=rstd[:, t:t + 1], bias=nmr[:, t:t + 1]),
                    waits=[("v", P.cnt["v"])], incs=[("s", 1)])
                P.op("vector", lambda e, t=t, u_t=u_t: e.tensor_add(
                    x_sb[:, t * D:(t + 1) * D], x_sb[:, t * D:(t + 1) * D], u_t),
                    waits=[("s", P.cnt["s"]), ("v", P.cnt["v"])], incs=[("v", 1)])
    else:
        P.wait("scalar", x_ready)
        for t in range(NT):
            ic = [("s", 1)] if t == NT - 1 else []
            P.op("scalar", lambda e, t=t: e.activation(
                xln[:, t * D:(t + 1) * D], x_sb[:, t * D:(t + 1) * D], AF.Copy), incs=ic)
    s_xln = P.cnt["s"]

    ident = P.tiles["ident_sb"]
    pt = [P.tiles["pt0"], P.tiles["pt1"]]
    P.wait("tensor", [("s", s_xln)])
    evac_tot = []
    g = 0
    for dk in range(NDK):
        for t0 in range(0, NT, 4):
            par = g % 2
            if g >= 2:
                P.wait("tensor", [("v", evac_tot[g - 2])])
            for j in range(4):
                t = t0 + j
                ic = [("mm", 1)] if j == 3 else []
                P.op("tensor", lambda e, dk=dk, t=t, par=par, j=j: e.transpose(
                    pt[par][:, j * 128:(j + 1) * 128],
                    xln[:, t * D + dk * 128: t * D + dk * 128 + 128],
                    ident[:]), incs=ic)
            mm_tot = P.cnt["mm"]
            P.op("vector", lambda e, dk=dk, t0=t0, par=par: e.tensor_copy(
                xf[:, dk * T + t0 * 128: dk * T + t0 * 128 + 512],
                pt[par][:, 0:512]),
                waits=[("mm", mm_tot)], incs=[("v", 1)])
            evac_tot.append(P.cnt["v"])
            g += 1
    v_done = P.cnt["v"]
    s_done = P.cnt["s"]
    if gain is not None:
        w_sb, b_sb = gain
        P.wait("scalar", [("v", v_done)] + list(gain_ready))
        for dk in range(NDK):
            ic = [("s", 1)] if dk == NDK - 1 else []
            P.op("scalar", lambda e, dk=dk: e.activation(
                xf[:, dk * T:(dk + 1) * T], xf[:, dk * T:(dk + 1) * T], AF.Identity,
                scale=w_sb[:, dk:dk + 1], bias=b_sb[:, dk:dk + 1]), incs=ic)
        s_done = P.cnt["s"]
    return v_done, s_done



def alloc_l1ffn(P):
    P.sbuf("cm_sb", (128, NDK * D), BF16)
    P.sbuf("sm_sb", (128, NDK * D), BF16)
    if "scr" not in P.tiles:
        P.sbuf("scr", (128, 4096), F32)


def emit_l1ffn_prep(P, theta, theta_noise, phi, phi_noise):
    scr = P.tiles["scr"]
    for (base, noise, out_name, bias) in ((theta, theta_noise, "cm_sb", PI / 2),
                                          (phi, phi_noise, "sm_sb", 0.0)):
        for half in range(2):
            hb = base.rearrange("(k p) o -> p k o", p=128)[:, 4 * half:4 * half + 4, :]
            hn = noise.rearrange("(k p) o -> p k o", p=128)[:, 4 * half:4 * half + 4, :]
            s_prev = P.cnt["s"]
            d0 = P.op("sync", lambda e, hb=hb: e.dma_start(
                out=scr[:].rearrange("p (k o) -> p k o", k=4), in_=hb),
                waits=[("s", s_prev)], incs=[("dprep", 16)])[0]
            d1 = P.op("gpsimd", lambda e, hn=hn: e.dma_start(
                out=scr[:].rearrange("p (k o) -> p k o", k=4), in_=hn,
                accum_op=ALU.add),
                waits=[("dprep", d0)], incs=[("g", 16)])[0]
            P.op("scalar", lambda e, out_name=out_name, bias=bias, half=half: e.activation(
                P.tiles[out_name][:, half * 4096:(half + 1) * 4096], scr[:],
                AF.Sin, bias=(cst(P, "halfpi") if bias else 0.0)),
                waits=[("g", d1)], incs=[("s", 1)])


def emit_l1ffn_main(P, xf_ready):
    xf, x_sb = P.tiles["xf_sb"], P.tiles["x_sb"]
    cm, sm = P.tiles["cm_sb"], P.tiles["sm_sb"]
    scr = P.tiles["scr"]
    ps = [P.tiles["ps0"], P.tiles["ps1"], P.tiles["ps2"], P.tiles["ps3"]]
    P.wait("tensor", list(xf_ready))
    u = 0
    sq_done = []
    add_done = []
    stt_done = []
    for tc in range(NT):
        for oh in range(2):
            par = u % 2
            pr, pi_ = ps[par], ps[2 + par]
            if u >= 2:
                P.wait("tensor", [("s", sq_done[u - 2])])
            for ik in range(NDK):
                ic = [("mm", 1)] if ik == NDK - 1 else []
                P.op("tensor", lambda e, ik=ik, tc=tc, oh=oh, pr=pr: e.matmul(
                    pr[:], xf[:, ik * T + tc * 128: ik * T + tc * 128 + 128],
                    cm[:, ik * D + oh * 512: ik * D + oh * 512 + 512],
                    start=(ik == 0), stop=(ik == NDK - 1)), incs=ic)
            for ik in range(NDK):
                ic = [("mm", 1)] if ik == NDK - 1 else []
                P.op("tensor", lambda e, ik=ik, tc=tc, oh=oh, pi_=pi_: e.matmul(
                    pi_[:], xf[:, ik * T + tc * 128: ik * T + tc * 128 + 128],
                    sm[:, ik * D + oh * 512: ik * D + oh * 512 + 512],
                    start=(ik == 0), stop=(ik == NDK - 1)), incs=ic)
            mm_tot = P.cnt["mm"]
            r2 = scr[:, par * 2048 + 0:par * 2048 + 512]
            i2 = scr[:, par * 2048 + 512:par * 2048 + 1024]
            mag = scr[:, par * 2048 + 1024:par * 2048 + 1536]
            sqw = [("mm", mm_tot - 1)]
            if u >= 2:
                sqw.append(("v", add_done[u - 2]))
            P.op("scalar", lambda e, pr=pr, r2=r2: e.activation(r2, pr[:], AF.Square),
                 waits=sqw, incs=[("s", 1)])
            P.op("scalar", lambda e, pi_=pi_, i2=i2: e.activation(i2, pi_[:], AF.Square),
                 waits=[("mm", mm_tot)], incs=[("s", 1)])
            sq_done.append(P.cnt["s"])
            P.op("vector", lambda e, r2=r2, i2=i2: e.tensor_add(r2, r2, i2),
                 waits=[("s", sq_done[u])], incs=[("v", 1)])
            add_done.append(P.cnt["v"])
            sqrw = [("v", add_done[u])]
            if u >= 2:
                sqrw.append(("v", stt_done[u - 2]))
            P.op("scalar", lambda e, mag=mag, r2=r2: e.activation(mag, r2, AF.Sqrt),
                 waits=sqrw, incs=[("s", 1)])
            s_sqrt = P.cnt["s"]
            xs = x_sb[:, tc * D + oh * 512: tc * D + oh * 512 + 512]
            P.op("vector", lambda e, mag=mag, xs=xs: e.scalar_tensor_tensor(
                xs, mag, 0.5, xs, ALU.mult, ALU.add),
                waits=[("s", s_sqrt)], incs=[("v", 1)])
            stt_done.append(P.cnt["v"])
            u += 1



NV = 32
FV = 1024
NE = 8


def alloc_moe(P):
    P.sbuf("rw_sb", (8, D), BF16)
    P.sbuf("rwT_sb", (128, NDK * NE), BF16)
    P.sbuf("b1stg", (8, D), F32)
    P.sbuf("b1f", (128, 32 * NE), F32)
    P.sbuf("b2_sb", (8, D), BF16)
    P.sbuf("w_tm", (128, NT * NE), F32)
    P.sbuf("w_fm", (8, T), BF16)
    P.sbuf("exps_sb", (128, 2 * NE), F32)
    P.sbuf("msc", (128, 16), F32)
    for p in range(2):
        P.sbuf(f"w1_buf{p}", (128, NDK * FV), BF16)
        P.sbuf(f"w2_buf{p}", (128, NDK * FV), BF16)


def emit_moe_prep(P, rw_ap, b1_ap, b2_ap):
    rw, rwT = P.tiles["rw_sb"], P.tiles["rwT_sb"]
    b1stg = P.tiles["b1stg"]
    b1f, b2 = P.tiles["b1f"], P.tiles["b2_sb"]
    pt0, psB = P.tiles["pt0"], P.tiles["psB"]
    ident, id32 = P.tiles["ident_sb"], P.tiles["id32_sb"]
    d_rw = P.op("gpsimd", lambda e: e.dma_start(out=rw[:], in_=rw_ap),
                incs=[("dmoe_g", 16)])[0]
    d_b2 = P.op("gpsimd", lambda e: e.dma_start(out=b2[:], in_=b2_ap),
                incs=[("dmoe_g", 16)])[0]

    P.wait("tensor", [("dmoe_g", d_rw + 16), ("dma", P.cnt["dma"])])
    for dk in range(NDK):
        ic = [("mm", 1)] if dk == NDK - 1 else []
        P.op("tensor", lambda e, dk=dk: e.transpose(
            pt0[:, dk * 8:dk * 8 + 8], rw[:, dk * 128:(dk + 1) * 128], ident[0:8, 0:8]),
            incs=ic)
    mm_rw = P.cnt["mm"]
    v_rw = P.op("vector", lambda e: e.tensor_copy(rwT[:], pt0[:, 0:NDK * 8]),
                waits=[("mm", mm_rw)], incs=[("v", 1)])[0]
    b1r = b1_ap.rearrange("e (bat c) -> bat e c", bat=4)
    for bat in range(4):
        d_b1 = P.op("sync", lambda e, bat=bat: e.dma_start(out=b1stg[:], in_=b1r[bat]),
                    waits=[("mm", P.cnt["mm"])], incs=[("dmoe", 16)])[0]
        P.wait("tensor", [("dmoe", d_b1), ("v", P.cnt["v"])])
        for j8 in range(8):
            j = bat * 8 + j8
            ic = [("mm", 1)] if j8 == 7 else []
            P.op("tensor", lambda e, j=j, j8=j8: e.transpose(
                psB[:, j8 * 8:j8 * 8 + 8], b1stg[:, j8 * 128:(j8 + 1) * 128], id32[0:8, 0:8]),
                incs=ic)
        mm_b = P.cnt["mm"]
        P.op("vector", lambda e, bat=bat: e.tensor_copy(
            b1f[:, bat * 64:(bat + 1) * 64], psB[:, 0:64]),
            waits=[("mm", mm_b)], incs=[("v", 1)])
    return P.cnt["v"], P.cnt["s"]


def emit_moe_router(P, xf_ready):
    xf, rwT = P.tiles["xf_sb"], P.tiles["rwT_sb"]
    w_tm, w_fm = P.tiles["w_tm"], P.tiles["w_fm"]
    exps2, msc = P.tiles["exps_sb"], P.tiles["msc"]
    psA, psB = P.tiles["psA"], P.tiles["psB"]
    ps_l = [psA, psB]

    def vedge():
        P.wait("vector", [("v", P.cnt["v"])])

    P.wait("tensor", list(xf_ready))
    exp_done = []
    wsel_done = []
    for t in range(NT):
        par = t % 2
        pl = ps_l[par]
        if t >= 2:
            P.wait("tensor", [("s", exp_done[t - 2])])
        for dk in range(NDK):
            ic = [("mm", 1)] if dk == NDK - 1 else []
            P.op("tensor", lambda e, t=t, dk=dk, pl=pl: e.matmul(
                pl[:, 0:8], xf[:, dk * T + t * 128: dk * T + t * 128 + 128],
                rwT[:, dk * 8:(dk + 1) * 8],
                start=(dk == 0), stop=(dk == NDK - 1)), incs=ic)
        mm_t = P.cnt["mm"]
        exps = exps2[:, par * 8:par * 8 + 8]
        sw = [("mm", mm_t)]
        if t >= 2:
            sw.append(("v", wsel_done[t - 2]))
        P.op("scalar", lambda e, pl=pl, exps=exps: e.activation(exps, pl[:, 0:8], AF.Exp),
             waits=sw, incs=[("s", 1)])
        exp_done.append(P.cnt["s"])
        rsum, m1, m2, dn = (msc[:, 0:1], msc[:, 1:2], msc[:, 2:3], msc[:, 3:4])
        mask = msc[:, 8:16]
        P.op("vector", lambda e, exps=exps: e.tensor_reduce(rsum, exps, AX.X, ALU.add),
             waits=[("s", exp_done[t]), ("v", P.cnt["v"])], incs=[("v", 1)])
        vedge()
        P.op("vector", lambda e, exps=exps: e.tensor_reduce(m1, exps, AX.X, ALU.max),
             incs=[("v", 1)])
        vedge()
        P.op("vector", lambda e, exps=exps: e.tensor_single_scalar(
            mask, exps, m1, ALU.is_ge), incs=[("v", 1)])
        vedge()
        P.op("vector", lambda e, exps=exps: e.scalar_tensor_tensor(
            mask, mask, -1e30, exps, ALU.mult, ALU.add), incs=[("v", 1)])
        vedge()
        P.op("vector", lambda e: e.tensor_reduce(m2, mask, AX.X, ALU.max),
             incs=[("v", 1)])
        vedge()
        P.op("vector", lambda e: e.tensor_add(dn, m1, m2), incs=[("v", 1)])
        vedge()
        P.op("vector", lambda e: e.scalar_tensor_tensor(
            dn, rsum, 1e-8, dn, ALU.mult, ALU.add), incs=[("v", 1)])
        vedge()
        P.op("vector", lambda e: e.reciprocal(dn, dn), incs=[("v", 1)])
        vedge()
        P.op("vector", lambda e, exps=exps: e.tensor_single_scalar(
            mask, exps, m2, ALU.is_ge), incs=[("v", 1)])
        vedge()
        P.op("vector", lambda e: e.tensor_scalar_mul(dn, dn, 0.5), incs=[("v", 1)])
        vedge()
        P.op("vector", lambda e, exps=exps, t=t: e.tensor_single_scalar(
            exps, exps, dn, ALU.mult), incs=[("v", 1)])
        vedge()
        P.op("vector", lambda e, exps=exps, t=t: e.tensor_mul(
            w_tm[:, t * 8:(t + 1) * 8], exps, mask), incs=[("v", 1)])
        wsel_done.append(P.cnt["v"])
    v_wtm = P.cnt["v"]
    id32 = P.tiles["id32_sb"]
    P.wait("tensor", [("v", v_wtm)])
    tr_evac = []
    for t in range(NT):
        par = t % 2
        if t >= 2:
            P.wait("tensor", [("v", tr_evac[t - 2])])
        mm_t = P.op("tensor", lambda e, t=t, par=par: e.transpose(
            psB[0:8, par * 128:par * 128 + 128],
            w_tm[:, t * 8:(t + 1) * 8], id32[:]), incs=[("mm", 1)])[0]
        P.op("vector", lambda e, t=t, par=par: e.tensor_copy(
            w_fm[:, t * 128:(t + 1) * 128], psB[0:8, par * 128:par * 128 + 128]),
            waits=[("mm", mm_t)], incs=[("v", 1)])
        tr_evac.append(P.cnt["v"])
    return P.cnt["v"]



def alloc_l1attn(P):
    for nm in ("wbufA", "wbufB"):
        if nm not in P.tiles:
            P.sbuf(nm, (128, NDK * D), BF16)
    for nm in ("wbufC", "wbufD"):
        if nm not in P.tiles:
            P.sbuf(nm, (128, NDK * D), BF16)
    xln = P.tiles["xln_sb"]
    P.tiles["ekv_sb"] = xln[:, D:3 * D]
    P.tiles["qf_sb"] = xln[:, 3 * D:4 * D]
    P.tiles["af_sb"] = xln[:, 4 * D:5 * D]
    P.sbuf("ctx_sb", (128, 8 * 264), F32)
    P.sbuf("ctxn_sb", (128, 8 * 256), BF16)
    P.tiles["qt_sb"] = P.tiles["ctx_sb"][:, 0:D]
    P.tiles["arow_sb"] = xln[0:1, 11 * D:13 * D]
    P.tiles["bdrow_sb"] = xln[0:1, 13 * D:14 * D]
    P.sbuf("g1_sb", (128, 2 * NDK), F32)
    P.sbuf("g2_sb", (128, 2 * NDK), F32)
    P.sbuf("bf1_sb", (128, 32), F32)
    P.sbuf("asc_sb", (128, 8), F32)
    P.sbuf("onecol_sb", (128, 1), BF16)


ATTN_STOP = None


def emit_l1attn(P, aps, x_ready):
    xf, x_sb, xln = P.tiles["xf_sb"], P.tiles["x_sb"], P.tiles["xln_sb"]
    wA, wB = P.tiles["wbufA"], P.tiles["wbufB"]
    wC, wD = P.tiles["wbufC"], P.tiles["wbufD"]
    ekv, ctx, ctxn = P.tiles["ekv_sb"], P.tiles["ctx_sb"], P.tiles["ctxn_sb"]
    qt, qf, af = P.tiles["qt_sb"], P.tiles["qf_sb"], P.tiles["af_sb"]
    arow, bdrow = P.tiles["arow_sb"], P.tiles["bdrow_sb"]
    g1, g2, bf1, asc = (P.tiles["g1_sb"], P.tiles["g2_sb"], P.tiles["bf1_sb"],
                        P.tiles["asc_sb"])
    ps0, ps1, ps2, ps3 = (P.tiles["ps0"], P.tiles["ps1"], P.tiles["ps2"],
                          P.tiles["ps3"])
    psA, psB = P.tiles["psA"], P.tiles["psB"]
    onesbf = P.tiles["onesbf_sb"]
    onecol = P.tiles["onecol_sb"]
    ident = P.tiles["ident_sb"]
    SC4 = 0.25
    P.op("vector", lambda e: e.memset(onecol[:], 1.0),
         waits=[("v", P.cnt["v"])], incs=[("v", 1)])
    P.op("vector", lambda e: e.memset(ctx[:], 0.0), incs=[("v", 1)])
    v_one = P.cnt["v"]

    d_xo = P.op("sync", lambda e: e.dma_start(
        out=aps["x_old"].rearrange("(t p) d -> p t d", p=128),
        in_=x_sb[:].rearrange("p (t d) -> p t d", t=NT)),
        waits=list(x_ready), incs=[("dxold", 16)])[0]
    v_ou, s_ou = emit_ln(P, x_ready=list(x_ready) + [("dxold", d_xo)], inplace=True)
    x_ready = [("v", v_ou), ("s", s_ou)]

    def wload(dst, key, sem):
        return P.op("gpsimd", lambda e, dst=dst, key=key: e.dma_start(
            out=dst[:].rearrange("p (k c) -> p k c", k=NDK),
            in_=aps[key].rearrange("(k p) c -> p k c", p=128)),
            incs=[(sem, 16)])[0]

    d_wk = wload(wA, "L1_attn_wk", "dat_a")
    d_wv = wload(wB, "L1_attn_wv", "dat_a")
    d_small = P.cnt["dat_a"]
    for (dst, key) in ((g1[:, 0:NDK], "a_ln1g"), (g1[:, NDK:2 * NDK], "a_ln1b"),
                       (g2[:, 0:NDK], "a_ln2g"), (g2[:, NDK:2 * NDK], "a_ln2b"),
                       (bf1[:], "a_bff1")):
        P.op("sync", lambda e, dst=dst, key=key: e.dma_start(
            out=dst, in_=aps[key]), incs=[("dat_f", 16)])
    d_f32 = P.cnt["dat_f"]

    def load_bias_rows(sem, wait):
        P.op("gpsimd", lambda e: e.dma_start(
            out=arow[:, 0:D], in_=aps["L1_attn_bo"].rearrange("(a c) -> a c", a=1)),
            waits=wait, incs=[(sem, 16)])
        P.op("gpsimd", lambda e: e.dma_start(
            out=arow[:, D:2 * D], in_=aps["L1_attn_b_ff2"].rearrange("(a c) -> a c", a=1)),
            incs=[(sem, 16)])
        P.op("gpsimd", lambda e: e.dma_start(
            out=bdrow[:, :], in_=aps["L1_attn_b_d"].rearrange("(a c) -> a c", a=1)),
            incs=[(sem, 16)])
        return P.cnt[sem]

    v1, s1 = emit_ln(P, x_ready=x_ready, gain=(g1[:, 0:NDK], g1[:, NDK:2 * NDK]),
                     gain_ready=[("dat_f", d_f32)])
    d_rows = load_bias_rows("dat_r", [("v", v1), ("s", s1)])

    P.wait("tensor", [("v", v1), ("s", s1), ("dat_a", d_small), ("v", v_one), ("dat_r", d_rows)])
    kv_evac = []
    ctxadd_v = []
    last_k_evac = 0
    last_v_evac = 0
    for tc in range(NT):
        for dh in range(2):
            if last_k_evac:
                P.wait("tensor", [("s", last_k_evac)])
            for dk in range(NDK):
                ic = [("mm", 1)] if dk == NDK - 1 else []
                P.op("tensor", lambda e, tc=tc, dh=dh, dk=dk: e.matmul(
                    ps0[:], xf[:, dk * T + tc * 128: dk * T + tc * 128 + 128],
                    wA[:, dk * D + dh * 512: dk * D + dh * 512 + 512],
                    start=(dk == 0), stop=(dk == NDK - 1)), incs=ic)
            mmk = P.cnt["mm"]
            P.op("scalar", lambda e, dh=dh: e.activation(
                ekv[:, dh * 512:dh * 512 + 512], ps0[:], AF.Exp, scale=SC4),
                waits=[("mm", mmk)], incs=[("s", 1)])
            last_k_evac = P.cnt["s"]
            if last_v_evac:
                P.wait("tensor", [("s", last_v_evac)])
            for dk in range(NDK):
                ic = [("mm", 1)] if dk == NDK - 1 else []
                P.op("tensor", lambda e, tc=tc, dh=dh, dk=dk: e.matmul(
                    ps1[:], xf[:, dk * T + tc * 128: dk * T + tc * 128 + 128],
                    wB[:, dk * D + dh * 512: dk * D + dh * 512 + 512],
                    start=(dk == 0), stop=(dk == NDK - 1)), incs=ic)
            mmv = P.cnt["mm"]
            P.op("scalar", lambda e, dh=dh: e.activation(
                ekv[:, D + dh * 512:D + dh * 512 + 512], ps1[:], AF.Copy),
                waits=[("mm", mmv)], incs=[("s", 1)])
            last_v_evac = P.cnt["s"]
        kv_evac.append(P.cnt["s"])
        P.wait("tensor", [("s", kv_evac[tc])])
        for u in range(8):
            h, hh = u // 2, u % 2
            pp = [ps2, ps3][u % 2]
            if tc * 8 + u >= 2:
                P.wait("tensor", [("v", ctxadd_v[tc * 8 + u - 2])])
            P.op("tensor", lambda e, h=h, hh=hh, pp=pp: e.matmul(
                pp[:, 0:256], ekv[:, h * 256 + hh * 128: h * 256 + hh * 128 + 128],
                ekv[:, D + h * 256: D + h * 256 + 256], start=True, stop=True))
            P.op("tensor", lambda e, h=h, hh=hh, pp=pp: e.matmul(
                pp[:, 256:257], ekv[:, h * 256 + hh * 128: h * 256 + hh * 128 + 128],
                onecol[:], start=True, stop=True), incs=[("mm", 1)])
            mmc = P.cnt["mm"]
            P.op("vector", lambda e, u=u, pp=pp, tc=tc: e.tensor_tensor(
                ctx[:, u * 264:u * 264 + 257], ctx[:, u * 264:u * 264 + 257],
                pp[:, 0:257], ALU.add) if tc > 0 else e.tensor_copy(
                ctx[:, u * 264:u * 264 + 257], pp[:, 0:257]),
                waits=[("mm", mmc), ("v", P.cnt["v"])], incs=[("v", 1)])
            ctxadd_v.append(P.cnt["v"])
    v_ctx = P.cnt["v"]
    d_cp = P.op("sync", lambda e: e.dma_start(out=aps["ctx_part"], in_=ctx[:]),
                waits=[("v", v_ctx)], incs=[("dctx", 16)])[0]
    P.op("gpsimd", lambda e: e.collective_compute(
        "AllReduce", ALU.add, replica_groups=GROUPS,
        ins=[aps["ctx_part"].opt()], outs=[aps["ctx_sum"].opt()]),
        waits=[("dctx", d_cp)], incs=[("cc", 1)])
    cc_ctx = P.cnt["cc"]
    d_cl = P.op("sync", lambda e: e.dma_start(out=ctx[:], in_=aps["ctx_sum"]),
                waits=[("cc", cc_ctx)], incs=[("dctx", 16)])[0]
    P.wait("vector", [("dctx", d_cl)])
    for u in range(8):
        P.op("vector", lambda e, u=u: e.reciprocal(
            asc[:, u:u + 1], ctx[:, u * 264 + 256:u * 264 + 257]),
            waits=[("v", P.cnt["v"])], incs=[("v", 1)])
        P.wait("vector", [("v", P.cnt["v"])])
        P.op("vector", lambda e, u=u: e.tensor_single_scalar(
            ctxn[:, u * 256:(u + 1) * 256], ctx[:, u * 264:u * 264 + 256],
            asc[:, u:u + 1], ALU.mult), incs=[("v", 1)])
        P.wait("vector", [("v", P.cnt["v"])])
    v_ctxn = P.cnt["v"]

    d_wq = wload(wC, "L1_attn_wq", "dat_b")
    d_wo = wload(wD, "L1_attn_wo", "dat_b")
    d_b = P.cnt["dat_b"]

    P.wait("tensor", [("v", v_ctxn), ("dat_b", d_b)])
    q_evac = []
    pb_v = []
    for tc in range(NT):
        if tc >= 1:
            P.wait("tensor", [("v", pb_v[-1])])
        for dh in range(2):
            pp = [ps0, ps1][dh]
            for dk in range(NDK):
                ic = [("mm", 1)] if dk == NDK - 1 else []
                P.op("tensor", lambda e, tc=tc, dh=dh, dk=dk, pp=pp: e.matmul(
                    pp[:], xf[:, dk * T + tc * 128: dk * T + tc * 128 + 128],
                    wC[:, dk * D + dh * 512: dk * D + dh * 512 + 512],
                    start=(dk == 0), stop=(dk == NDK - 1)), incs=ic)
            mmq = P.cnt["mm"]
            sw = [("mm", mmq)]
            if tc >= 1:
                sw.append(("v", pb_v[-1]))
            P.op("scalar", lambda e, dh=dh, pp=pp: e.activation(
                qt[:, dh * 512:dh * 512 + 512], pp[:], AF.Copy),
                waits=sw, incs=[("s", 1)])
        s_q = P.cnt["s"]
        P.wait("vector", [("s", s_q)])
        eq = xln[:, 0:D]
        for h in range(4):
            P.op("vector", lambda e, h=h: e.tensor_reduce(
                asc[:, h:h + 1], qt[:, h * 256:(h + 1) * 256], AX.X, ALU.max),
                waits=[("v", P.cnt["v"])], incs=[("v", 1)])
            P.wait("vector", [("v", P.cnt["v"])])
            P.op("vector", lambda e, h=h: e.tensor_single_scalar(
                asc[:, 4 + h:5 + h], asc[:, h:h + 1], -SC4, ALU.mult), incs=[("v", 1)])
            P.wait("vector", [("v", P.cnt["v"])])
        v_m = P.cnt["v"]
        P.wait("scalar", [("v", v_m)])
        mmprev = P.cnt["mm"]
        for h in range(4):
            P.op("scalar", lambda e, h=h: e.activation(
                eq[:, h * 256:(h + 1) * 256], qt[:, h * 256:(h + 1) * 256], AF.Exp,
                scale=SC4, bias=asc[:, 4 + h:5 + h],
                accum_out=asc[:, h:h + 1]),
                waits=([("mm", mmprev)] if h == 0 else []), incs=[("s", 1)])
        s_eq = P.cnt["s"]
        P.wait("vector", [("s", s_eq)])
        for h in range(4):
            P.op("vector", lambda e, h=h: e.reciprocal(
                asc[:, 4 + h:5 + h], asc[:, h:h + 1]),
                waits=[("v", P.cnt["v"])], incs=[("v", 1)])
            P.wait("vector", [("v", P.cnt["v"])])
            P.op("vector", lambda e, h=h: e.tensor_single_scalar(
                eq[:, h * 256:(h + 1) * 256], eq[:, h * 256:(h + 1) * 256],
                asc[:, 4 + h:5 + h], ALU.mult), incs=[("v", 1)])
            P.wait("vector", [("v", P.cnt["v"])])
        v_eq = P.cnt["v"]
        P.wait("tensor", [("v", v_eq)])
        qf_ev = []
        for dk in range(NDK):
            par = dk % 2
            pt = [P.tiles["pt0"], P.tiles["pt1"]][par]
            if dk >= 2:
                P.wait("tensor", [("v", qf_ev[dk - 2])])
            mmt = P.op("tensor", lambda e, dk=dk, pt=pt: e.transpose(
                pt[:, 0:128], eq[:, dk * 128:(dk + 1) * 128], ident[:]),
                incs=[("mm", 1)])[0]
            P.op("vector", lambda e, dk=dk, pt=pt: e.tensor_copy(
                qf[:, dk * 128:(dk + 1) * 128], pt[:, 0:128]),
                waits=[("mm", mmt), ("v", P.cnt["v"])], incs=[("v", 1)])
            qf_ev.append(P.cnt["v"])
        v_qf = P.cnt["v"]
        P.wait("tensor", [("v", v_qf)])
        af_ev = []
        for half in range(2):
            pp = [ps2, ps3][half]
            if tc >= 1 and half == 0:
                P.wait("tensor", [("v", pb_v[-1])])
            for j in range(4):
                a = half * 4 + j
                h, eh = a // 2, a % 2
                for hh in range(2):
                    ic = [("mm", 1)] if hh == 1 else []
                    P.op("tensor", lambda e, h=h, eh=eh, hh=hh, pp=pp, j=j: e.matmul(
                        pp[:, j * 128:(j + 1) * 128],
                        ctxn[:, (h * 2 + hh) * 256 + eh * 128: (h * 2 + hh) * 256 + eh * 128 + 128],
                        qf[:, (h * 2 + hh) * 128: (h * 2 + hh) * 128 + 128],
                        start=(hh == 0), stop=(hh == 1)), incs=ic)
            mmaf = P.cnt["mm"]
            P.op("vector", lambda e, half=half, pp=pp: e.tensor_copy(
                af[:, half * 512:(half + 1) * 512], pp[:, 0:512]),
                waits=[("mm", mmaf), ("v", P.cnt["v"])], incs=[("v", 1)])
            af_ev.append(P.cnt["v"])
        P.wait("tensor", [("v", af_ev[-1])])
        for dh in range(2):
            pp = [ps0, ps1][dh]
            for a in range(NDK):
                P.op("tensor", lambda e, a=a, dh=dh, pp=pp: e.matmul(
                    pp[:], af[:, a * 128:(a + 1) * 128],
                    wD[:, a * D + dh * 512: a * D + dh * 512 + 512],
                    start=(a == 0), stop=False))
            P.op("tensor", lambda e, dh=dh, pp=pp: e.matmul(
                pp[:], onesbf[:], arow[:, dh * 512:dh * 512 + 512],
                start=False, stop=True), incs=[("mm", 1)])
            mmo = P.cnt["mm"]
            xs = x_sb[:, tc * D + dh * 512: tc * D + dh * 512 + 512]
            P.op("vector", lambda e, xs=xs, pp=pp: e.tensor_add(xs, xs, pp[:]),
                 waits=[("mm", mmo), ("v", P.cnt["v"])], incs=[("v", 1)])
        pb_v.append(P.cnt["v"])

    if ATTN_STOP == "wo":
        return
    x_ready2 = [("v", P.cnt["v"]), ("s", P.cnt["s"])]
    v2, s2 = emit_ln(P, x_ready=x_ready2, gain=(g2[:, 0:NDK], g2[:, NDK:2 * NDK]),
                     gain_ready=[("dat_f", d_f32)])
    d_rows2 = load_bias_rows("dat_r2", [("v", v2), ("s", s2)])
    h = xln
    wf_sem = ["dff0", "dff1"]
    W1r = aps["L1_attn_w_ff1"].rearrange("(dk p) (vf f) -> vf p dk f", p=128, f=512)
    W2r = aps["L1_attn_w_ff2"].rearrange("(vf fk p) d -> vf p fk d", p=128, fk=4)
    ff1_mm = []
    ff2_mm = []
    gelu_s = []
    yst_v = []
    hev = []
    P.wait("tensor", [("v", v2), ("s", s2)])
    for vf in range(8):
        par = vf % 2
        wts = [("mm", ff1_mm[vf - 2])] if vf >= 2 else [("mm", P.cnt["mm"])]
        P.op("gpsimd", lambda e, vf=vf, par=par: e.dma_start(
            out=[wA, wB][par][:, 0:NDK * 512].rearrange("p (dk f) -> p dk f", f=512),
            in_=W1r[vf]), waits=wts, incs=[(wf_sem[par], 16)])
        w1r = P.cnt[wf_sem[par]]
        wts2 = [("mm", ff2_mm[vf - 2])] if vf >= 2 else [("mm", P.cnt["mm"])]
        P.op("gpsimd", lambda e, vf=vf, par=par: e.dma_start(
            out=[wC, wD][par][:, 0:4 * 1024].rearrange("p (fk d) -> p fk d", d=1024),
            in_=W2r[vf]), waits=wts2, incs=[(wf_sem[par] + "b", 16)])
        w2r = P.cnt[wf_sem[par] + "b"]
        P.wait("tensor", [(wf_sem[par], w1r)])
        if vf >= 1:
            P.wait("scalar", [("mm", ff2_mm[vf - 1])])
        for fm in range(4):
            for tg in range(4):
                gg = len(gelu_s)
                hp = [ps0, ps1][gg % 2]
                if gg >= 2:
                    P.wait("tensor", [("s", gelu_s[gg - 2])])
                for dk in range(NDK):
                    ic = [("mm", 1)] if dk == NDK - 1 else []
                    P.op("tensor", lambda e, fm=fm, tg=tg, dk=dk, par=par, hp=hp: e.matmul(
                        hp[:], [wA, wB][par][:, dk * 512 + fm * 128: dk * 512 + fm * 128 + 128],
                        xf[:, dk * T + tg * 512: dk * T + tg * 512 + 512],
                        start=(dk == 0), stop=(dk == NDK - 1)), incs=ic)
                mmh = P.cnt["mm"]
                jcol = vf * 4 + fm
                P.op("scalar", lambda e, fm=fm, tg=tg, hp=hp, jcol=jcol: e.activation(
                    h[:, fm * T + tg * 512: fm * T + tg * 512 + 512], hp[:],
                    AF.Gelu_apprx_tanh, bias=bf1[:, jcol:jcol + 1]),
                    waits=[("mm", mmh)], incs=[("s", 1)])
                gelu_s.append(P.cnt["s"])
        ff1_mm.append(P.cnt["mm"])
        s_h = P.cnt["s"]
        P.wait("tensor", [(wf_sem[par] + "b", w2r), ("s", s_h)])
        if vf == 0:
            P.wait("tensor", [("dat_r2", d_rows2)])
        for tc in range(NT):
            for dh in range(2):
                uu = len(yst_v)
                yp = [ps2, ps3][uu % 2]
                if uu >= 2:
                    P.wait("tensor", [("v", yst_v[uu - 2])])
                for fk in range(4):
                    last = (fk == 3 and vf > 0)
                    ic = [("mm", 1)] if last else []
                    P.op("tensor", lambda e, tc=tc, dh=dh, fk=fk, par=par, yp=yp, last=last: e.matmul(
                        yp[:], h[:, fk * T + tc * 128: fk * T + tc * 128 + 128],
                        [wC, wD][par][:, fk * 1024 + dh * 512: fk * 1024 + dh * 512 + 512],
                        start=(fk == 0), stop=last), incs=ic)
                if vf == 0:
                    P.op("tensor", lambda e, dh=dh, yp=yp: e.matmul(
                        yp[:], onesbf[:], arow[:, D + dh * 512: D + dh * 512 + 512],
                        start=False, stop=True), incs=[("mm", 1)])
                mmy = P.cnt["mm"]
                xs = x_sb[:, tc * D + dh * 512: tc * D + dh * 512 + 512]
                P.op("vector", lambda e, xs=xs, yp=yp: e.tensor_add(xs, xs, yp[:]),
                     waits=[("mm", mmy), ("v", P.cnt["v"])], incs=[("v", 1)])
                yst_v.append(P.cnt["v"])
        ff2_mm.append(P.cnt["mm"])

    if ATTN_STOP == "ff":
        return
    x_ready3 = [("v", P.cnt["v"]), ("s", P.cnt["s"])]
    v3, s3 = emit_ln(P, x_ready=x_ready3, identity=True)
    d_rows3 = load_bias_rows("dat_r3", [("v", v3), ("s", s3)])
    d_wd = P.op("gpsimd", lambda e: e.dma_start(
        out=wA[:].rearrange("p (k c) -> p k c", k=NDK),
        in_=aps["L1_attn_w_d"].rearrange("(k p) c -> p k c", p=128)),
        waits=[("mm", P.cnt["mm"])], incs=[("dat_d", 16)])[0]
    P.wait("tensor", [("v", v3), ("s", s3), ("dat_d", d_wd), ("dat_r3", d_rows3)])
    wd_s = []
    xob = P.tiles["ctx_sb"][:, 0:1024]
    xor_ap = aps["x_old"].rearrange("(t p) d -> p t d", p=128)
    ox_sem = ["dox0", "dox1"]
    for tc in range(NT):
        for dh in range(2):
            uu = tc * 2 + dh
            pp = [ps0, ps1][uu % 2]
            par = uu % 2
            wts = [("v", wd_s[uu - 2])] if uu >= 2 else [("v", P.cnt["v"])]
            P.op("sync", lambda e, tc=tc, dh=dh, par=par: e.dma_start(
                out=xob[:, par * 512:par * 512 + 512],
                in_=xor_ap[:, tc, dh * 512:dh * 512 + 512]),
                waits=wts, incs=[(ox_sem[par], 16)])
            d_ox = P.cnt[ox_sem[par]]
            if uu >= 2:
                P.wait("tensor", [("v", wd_s[uu - 2])])
            for dk in range(NDK):
                P.op("tensor", lambda e, tc=tc, dh=dh, dk=dk, pp=pp: e.matmul(
                    pp[:], xf[:, dk * T + tc * 128: dk * T + tc * 128 + 128],
                    wA[:, dk * D + dh * 512: dk * D + dh * 512 + 512],
                    start=(dk == 0), stop=False))
            P.op("tensor", lambda e, dh=dh, pp=pp: e.matmul(
                pp[:], onesbf[:], bdrow[:, dh * 512:dh * 512 + 512],
                start=False, stop=True), incs=[("mm", 1)])
            mmw = P.cnt["mm"]
            xs = x_sb[:, tc * D + dh * 512: tc * D + dh * 512 + 512]
            P.op("vector", lambda e, xs=xs, pp=pp, par=par: e.tensor_tensor(
                xs, xob[:, par * 512:par * 512 + 512], pp[:], ALU.add),
                waits=[("mm", mmw), (ox_sem[par], d_ox), ("v", P.cnt["v"])],
                incs=[("v", 1)])
            wd_s.append(P.cnt["v"])


def emit_moe_bias(P, router_v_done):
    w_fm, b2, x_sb = P.tiles["w_fm"], P.tiles["b2_sb"], P.tiles["x_sb"]
    psA, psB = P.tiles["psA"], P.tiles["psB"]
    ps = [psA, psB]
    P.wait("tensor", [("v", router_v_done)])
    add_done = []
    u = 0
    for tc in range(NT):
        for dh in range(2):
            par = u % 2
            if u >= 2:
                P.wait("tensor", [("v", add_done[u - 2])])
            mm_u = P.op("tensor", lambda e, tc=tc, dh=dh, par=par: e.matmul(
                ps[par][:], w_fm[:, tc * 128:(tc + 1) * 128],
                b2[:, dh * 512:(dh + 1) * 512], start=True, stop=True),
                incs=[("mm", 1)])[0]
            xs = x_sb[:, tc * D + dh * 512: tc * D + dh * 512 + 512]
            P.op("vector", lambda e, xs=xs, par=par: e.tensor_add(xs, xs, ps[par][:]),
                 waits=[("mm", mm_u), ("v", P.cnt["v"])], incs=[("v", 1)])
            add_done.append(P.cnt["v"])
            u += 1


def emit_moe_experts(P, W1_ap, W2_ap, xf_ready):
    xf, x_sb = P.tiles["xf_sb"], P.tiles["x_sb"]
    h = P.tiles["xln_sb"]
    b1f, w_tm = P.tiles["b1f"], P.tiles["w_tm"]
    ps_h = [P.tiles["ps0"], P.tiles["ps1"]]
    ps_y = [P.tiles["ps2"], P.tiles["ps3"]]
    w1b = [P.tiles["w1_buf0"], P.tiles["w1_buf1"]]
    w2b = [P.tiles["w2_buf0"], P.tiles["w2_buf1"]]
    W1r = W1_ap.rearrange("e (dk p) (vf f) -> e vf p dk f", p=128, f=FV)
    W2r = W2_ap.rearrange("e (vf fk p) d -> e vf p fk d", p=128, fk=NDK)

    P.wait("tensor", list(xf_ready))
    w1_sem = ["dw1a", "dw1b"]
    w2_sem = ["dw2a", "dw2b"]
    w1_mm_done = []
    w2_mm_done = []
    hevac_done = []
    gelu_done = []
    ystt_done = []
    ghp = 0
    gyp = 0
    for v in range(NV):
        e_idx, vf = v // 4, v % 4
        par = v % 2
        wts = [("mm", w1_mm_done[v - 2])] if v >= 2 else []
        P.op("gpsimd", lambda e, e_idx=e_idx, vf=vf, par=par: e.dma_start(
            out=w1b[par][:].rearrange("p (dk f) -> p dk f", f=FV),
            in_=W1r[e_idx, vf]),
            waits=wts, incs=[(w1_sem[par], 16)])
        w1_rdy = P.cnt[w1_sem[par]]
        wts2 = [("mm", w2_mm_done[v - 2])] if v >= 2 else []
        P.op("gpsimd", lambda e, e_idx=e_idx, vf=vf, par=par: e.dma_start(
            out=w2b[par][:].rearrange("p (fk d) -> p fk d", d=D),
            in_=W2r[e_idx, vf]),
            waits=wts2, incs=[(w2_sem[par], 16)])
        w2_rdy = P.cnt[w2_sem[par]]

        P.wait("tensor", [(w1_sem[par], w1_rdy)])
        if v >= 1:
            P.wait("scalar", [("mm", w2_mm_done[v - 1])])
        for fm in range(NDK):
            for tg in range(4):
                hpar = ghp % 2
                if ghp >= 2:
                    P.wait("tensor", [("s", gelu_done[ghp - 2])])
                for dk in range(NDK):
                    ic = [("mm", 1)] if dk == NDK - 1 else []
                    P.op("tensor", lambda e, fm=fm, tg=tg, dk=dk, par=par, hpar=hpar: e.matmul(
                        ps_h[hpar][:],
                        w1b[par][:, dk * FV + fm * 128: dk * FV + fm * 128 + 128],
                        xf[:, dk * T + tg * 512: dk * T + tg * 512 + 512],
                        start=(dk == 0), stop=(dk == NDK - 1)), incs=ic)
                mm_g = P.cnt["mm"]
                j8 = (vf * 8 + fm) * 8 + e_idx
                b1col = b1f[:, j8:j8 + 1]
                P.op("scalar", lambda e, fm=fm, tg=tg, hpar=hpar, b1col=b1col: e.activation(
                    h[:, fm * T + tg * 512: fm * T + tg * 512 + 512],
                    ps_h[hpar][:], AF.Gelu_apprx_tanh, bias=b1col),
                    waits=[("mm", mm_g)], incs=[("s", 1)])
                gelu_done.append(P.cnt["s"])
                ghp += 1
        w1_mm_done.append(P.cnt["mm"])
        hevac_done.append(P.cnt["s"])

        P.wait("tensor", [(w2_sem[par], w2_rdy), ("s", hevac_done[v])])
        for tc in range(NT):
            for dh in range(2):
                ypar = gyp % 2
                if gyp >= 2:
                    P.wait("tensor", [("v", ystt_done[gyp - 2])])
                for fk in range(NDK):
                    ic = [("mm", 1)] if fk == NDK - 1 else []
                    P.op("tensor", lambda e, tc=tc, dh=dh, fk=fk, par=par, ypar=ypar: e.matmul(
                        ps_y[ypar][:],
                        h[:, fk * T + tc * 128: fk * T + tc * 128 + 128],
                        w2b[par][:, fk * FV + dh * 512: fk * FV + dh * 512 + 512],
                        start=(fk == 0), stop=(fk == NDK - 1)), incs=ic)
                mm_g = P.cnt["mm"]
                wcol = w_tm[:, tc * 8 + e_idx: tc * 8 + e_idx + 1]
                xs = x_sb[:, tc * D + dh * 512: tc * D + dh * 512 + 512]
                P.op("vector", lambda e, xs=xs, ypar=ypar, wcol=wcol: e.scalar_tensor_tensor(
                    xs, ps_y[ypar][:], wcol, xs, ALU.mult, ALU.add),
                    waits=[("mm", mm_g), ("v", P.cnt["v"])], incs=[("v", 1)])
                ystt_done.append(P.cnt["v"])
                gyp += 1
        w2_mm_done.append(P.cnt["mm"])



DI = 512
NDC = DI // 128
SEQ = 8192
NSC = SEQ // 512
DST = 16


def alloc_mamba(P):
    P.sbuf("winT_sb", (128, NDK * 1024), BF16)
    P.sbuf("opT_sb", (128, NDC * D), BF16)
    P.sbuf("xpwT_sb", (128, NDC * 96), BF16)
    P.sbuf("dtwT_sb", (64, DI), BF16)
    P.sbuf("basis_sb", (16, 16 * 128), BF16)
    P.sbuf("A_sb", (128, NDC * DST), F32)
    P.sbuf("cw_sb", (128, NDC * 4), F32)
    P.sbuf("cb_sb", (128, NDC), F32)
    P.sbuf("D_sb", (128, NDC), F32)
    P.sbuf("dtb_sb", (128, NDC), F32)
    P.sbuf("lnw_sb", (128, NDK), F32)
    P.sbuf("lnb_sb", (128, NDK), F32)
    for p in range(2):
        P.sbuf(f"xin_buf{p}", (128, NDK * 512), BF16)
    xf = P.tiles["xf_sb"]
    off = 0
    for (nm, cols) in (("xcin_sb", NDC * 515), ("zc_sb", NDC * 512),
                       ("xcv_sb", NDC * 512), ("dts_sb", NDC * 512),
                       ("dtu_sb", NDC * 512), ("yg_sb", NDC * 512),
                       ("scanA", 512), ("scanB", 512), ("states", 512),
                       ("scanA1", 512), ("scanB1", 512), ("states1", 512),
                       ("ytbuf", 512)):
        P.tiles[nm] = xf[:, off:off + cols]
        off += cols
    assert off <= NDK * T
    P.sbuf("cacc_sb", (128, 512), F32)
    P.sbuf("xdbl_tm", (128, 4 * 96), F32)
    P.sbuf("dtraw_fm", (64, 512), BF16)
    P.sbuf("Bfm_sb", (16, 512), BF16)
    P.sbuf("Cfm_sb", (16, 512), BF16)
    P.sbuf("h_sb", (128, NDC * DST), F32)
    P.sbuf("yacc_sb", (128, NDC * 512), F32)
    P.sbuf("sz_sb", (128, 512), F32)
    P.sbuf("mscr", (128, 2 * 512), F32)
    P.sbuf("m4buf", (128, 2048), F32)


MAMBA_STOP = None


def emit_mamba(P, aps, x_ready):
    xf, x_sb = P.tiles["xf_sb"], P.tiles["x_sb"]
    winT, opT = P.tiles["winT_sb"], P.tiles["opT_sb"]
    xpwT, dtwT = P.tiles["xpwT_sb"], P.tiles["dtwT_sb"]
    basis = P.tiles["basis_sb"]
    A_sb, cw, cb, D_sb, dtb = (P.tiles["A_sb"], P.tiles["cw_sb"], P.tiles["cb_sb"],
                               P.tiles["D_sb"], P.tiles["dtb_sb"])
    xcin, zc, xcv = P.tiles["xcin_sb"], P.tiles["zc_sb"], P.tiles["xcv_sb"]
    cacc = P.tiles["cacc_sb"]
    xdbl_tm, dtraw = P.tiles["xdbl_tm"], P.tiles["dtraw_fm"]
    Bfm, Cfm = P.tiles["Bfm_sb"], P.tiles["Cfm_sb"]
    dts, dtu = P.tiles["dts_sb"], P.tiles["dtu_sb"]
    scanA2 = [P.tiles["scanA"], P.tiles["scanA1"]]
    scanB2 = [P.tiles["scanB"], P.tiles["scanB1"]]
    states2 = [P.tiles["states"], P.tiles["states1"]]
    ytbuf = P.tiles["ytbuf"]
    h_sb, yacc, sz, yg = (P.tiles["h_sb"], P.tiles["yacc_sb"], P.tiles["sz_sb"],
                          P.tiles["yg_sb"])
    mscr = P.tiles["mscr"]
    xinb = [P.tiles["xin_buf0"], P.tiles["xin_buf1"]]
    ps0, ps1 = P.tiles["ps0"], P.tiles["ps1"]
    ps2, ps3 = P.tiles["ps2"], P.tiles["ps3"]
    psA, psB = P.tiles["psA"], P.tiles["psB"]
    id32 = P.tiles["id32_sb"]

    def vedge():
        P.wait("vector", [("v", P.cnt["v"])])

    if MAMBA_STOP == "now":
        return
    def gload(out_ap, in_ap):
        P.op("gpsimd", lambda e, out_ap=out_ap, in_ap=in_ap: e.dma_start(
            out=out_ap, in_=in_ap), incs=[("dmam_g", 16)])

    gload(winT[:].rearrange("p (k c) -> p k c", k=NDK),
          aps["m_winT"].rearrange("(k p) c -> p k c", p=128))
    gload(opT[:].rearrange("p (k c) -> p k c", k=NDC),
          aps["m_opT"].rearrange("(k p) c -> p k c", p=128))
    gload(xpwT[:].rearrange("p (k c) -> p k c", k=NDC),
          aps["m_xpwT"].rearrange("(k p) c -> p k c", p=128))
    gload(dtwT[:], aps["m_dtwT"])
    gload(basis[:], aps["basis16"])
    w_g = P.cnt["dmam_g"]
    for (dst, key) in ((A_sb, "m_Alog"), (cw, "m_convw"), (cb, "m_convb"),
                       (D_sb, "m_D"), (dtb, "m_dtb"),
                       (P.tiles["lnw_sb"], "m_lnw"), (P.tiles["lnb_sb"], "m_lnb")):
        P.op("sync", lambda e, dst=dst, key=key: e.dma_start(
            out=dst[:], in_=aps[key]), incs=[("dmam", 16)])
    w_f = P.cnt["dmam"]
    if MAMBA_STOP == "w":
        P.wait("tensor", [("dmam", w_f), ("dmam_g", w_g)])
        P.wait("vector", [("dmam", w_f), ("dmam_g", w_g)])
        P.wait("scalar", [("dmam", w_f), ("dmam_g", w_g)])
        return
    s_A = P.op("scalar", lambda e: e.activation(A_sb[:], A_sb[:], AF.Exp),
               waits=[("dmam", w_f)], incs=[("s", 1)])[0]
    P.op("vector", lambda e: e.tensor_scalar_mul(A_sb[:], A_sb[:], -1.0),
         waits=[("s", s_A), ("v", P.cnt["v"])], incs=[("v", 1)])
    if MAMBA_STOP == "A":
        return

    v_ln, s_ln = emit_ln(P, x_ready=x_ready, mamba_double=True,
                         gain=(P.tiles["lnw_sb"], P.tiles["lnb_sb"]),
                         gain_ready=[("dmam", w_f)])

    if MAMBA_STOP == "ln":
        return
    d_ag = P.op("sync", lambda e: e.dma_start(
        out=aps["ag_in"].rearrange("(k p) c -> p k c", p=128),
        in_=xf[:].rearrange("p (k c) -> p k c", k=NDK)),
        waits=[("v", v_ln), ("s", s_ln)], incs=[("dag", 16)])[0]
    P.op("gpsimd", lambda e: e.collective_compute(
        "AllGather", ALU.bypass, replica_groups=GROUPS,
        ins=[aps["ag_in"].opt()], outs=[aps["ag_out"].opt()]),
        waits=[("dag", d_ag), ("dmam_g", w_g)], incs=[("cc", 1)])
    cc_ag = P.cnt["cc"]

    if MAMBA_STOP == "ag":
        return
    xin_sem = ["dxi0", "dxi1"]
    evac_done = []
    conv_v = []
    xp_evac_v = []
    oc_evac = []
    xp_mm = []
    agr = aps["ag_out"].rearrange("(r k p) c -> r k p c", k=NDK, p=128)
    P.wait("sync", [("cc", cc_ag)])
    P.wait("tensor", [("dmam_g", w_g), ("dma", P.cnt["dma"])])
    for sc in range(NSC):
        par = sc % 2
        r0, off = (sc * 512) // 2048, (sc * 512) % 2048
        wts = [("s", evac_done[sc - 2])] if sc >= 2 else []
        P.op("sync", lambda e, par=par, r0=r0, off=off: e.dma_start(
            out=xinb[par][:].rearrange("p (k c) -> p k c", k=NDK),
            in_=agr[r0, :, :, off:off + 512].transpose([1, 0, 2])),
            waits=wts, incs=[(xin_sem[par], 16)])
        x_rdy = P.cnt[xin_sem[par]]
        P.wait("tensor", [(xin_sem[par], x_rdy)])
        xc3 = xcin[:].rearrange("p (dc c) -> p dc c", dc=NDC)
        if sc == 0:
            P.op("vector", lambda e, xc3=xc3: e.memset(xc3[:, :, 0:3], 0.0),
                 waits=[("v", P.cnt["v"]), ("dag", d_ag)], incs=[("v", 1)])
        else:
            P.op("vector", lambda e, xc3=xc3: e.tensor_copy(
                xc3[:, :, 0:3], xc3[:, :, 512:515]),
                waits=[("v", P.cnt["v"]), ("s", evac_done[sc - 1])], incs=[("v", 1)])
        v_halo = P.cnt["v"]
        for oc in range(8):
            g = sc * 8 + oc
            pp = [ps0, ps1][g % 2]
            if g >= 2:
                P.wait("tensor", [("s", oc_evac[g - 2])])
            for dmk in range(NDK):
                ic = [("mm", 1)] if dmk == NDK - 1 else []
                P.op("tensor", lambda e, oc=oc, dmk=dmk, par=par, pp=pp: e.matmul(
                    pp[:], winT[:, dmk * 1024 + oc * 128: dmk * 1024 + oc * 128 + 128],
                    xinb[par][:, dmk * 512:(dmk + 1) * 512],
                    start=(dmk == 0), stop=(dmk == NDK - 1)), incs=ic)
            mm_oc = P.cnt["mm"]
            if oc < 4:
                dst = xcin[:, oc * 515 + 3: oc * 515 + 515]
            else:
                dst = zc[:, (oc - 4) * 512: (oc - 4) * 512 + 512]
            sw = [("mm", mm_oc)]
            if oc < 4:
                sw.append(("v", v_halo))
            if sc >= 1 and oc >= 4:
                sw.append(("dzs", 32 * sc))
            P.op("scalar", lambda e, dst=dst, pp=pp: e.activation(dst, pp[:], AF.Copy),
                 waits=sw, incs=[("s", 1)])
            oc_evac.append(P.cnt["s"])
        evac_done.append(P.cnt["s"])
        P.op("scalar", lambda e, sc=sc: e.dma_start(
            out=aps["z_sp"][:, :, sc * 512:(sc + 1) * 512].transpose([1, 0, 2]),
            in_=zc[:].rearrange("p (dc c) -> p dc c", dc=NDC)),
            waits=[("s", P.cnt["s"])], incs=[("dzs", 16)])
        for dc in range(NDC):
            P.op("vector", lambda e, dc=dc: e.tensor_single_scalar(
                cacc[:], xcin[:, dc * 515 + 0: dc * 515 + 512],
                cw[:, dc * 4:dc * 4 + 1], ALU.mult),
                waits=[("s", P.cnt["s"]), ("v", P.cnt["v"])], incs=[("v", 1)])
            for j in range(1, 4):
                P.op("vector", lambda e, dc=dc, j=j: e.scalar_tensor_tensor(
                    cacc[:], xcin[:, dc * 515 + j: dc * 515 + j + 512],
                    cw[:, dc * 4 + j:dc * 4 + j + 1], cacc[:], ALU.mult, ALU.add),
                    waits=[("v", P.cnt["v"])], incs=[("v", 1)])
            v_c = P.cnt["v"]
            P.op("scalar", lambda e, dc=dc: e.activation(
                xcv[:, dc * 512:(dc + 1) * 512], cacc[:], AF.Silu,
                bias=cb[:, dc:dc + 1]),
                waits=[("v", v_c), ("dzs", 32 * sc + 16) if sc >= 1 else ("v", 0)],
                incs=[("s", 1)])
        s_conv = P.cnt["s"]
        conv_v.append(P.cnt["v"])
        P.op("scalar", lambda e, sc=sc: e.dma_start(
            out=aps["xcv_sp"][:, :, sc * 512:(sc + 1) * 512].transpose([1, 0, 2]),
            in_=xcv[:].rearrange("p (dc c) -> p dc c", dc=NDC)),
            waits=[("s", P.cnt["s"])], incs=[("dzs", 16)])
        P.wait("tensor", [("s", s_conv)])
        for tt in range(4):
            pp = [ps2, ps3][tt % 2]
            if sc * 4 + tt >= 2:
                P.wait("tensor", [("v", xp_evac_v[sc * 4 + tt - 2])])
            for dc in range(NDC):
                ic = [("mm", 1)] if dc == NDC - 1 else []
                P.op("tensor", lambda e, tt=tt, dc=dc, pp=pp: e.matmul(
                    pp[:, 0:96], xcv[:, dc * 512 + tt * 128: dc * 512 + tt * 128 + 128],
                    xpwT[:, dc * 96:(dc + 1) * 96],
                    start=(dc == 0), stop=(dc == NDC - 1)), incs=ic)
            mm_t = P.cnt["mm"]
            sl = mscr[:, (tt % 2) * 512:(tt % 2) * 512 + 96]
            P.op("vector", lambda e, sl=sl, pp=pp: e.tensor_copy(sl, pp[:, 0:96]),
                 waits=[("mm", mm_t), ("v", P.cnt["v"]),
                        ("dxdb", 16 * (sc * 4 + tt))],
                 incs=[("v", 1)])
            xp_evac_v.append(P.cnt["v"])
            P.op("sync", lambda e, sc=sc, tt=tt, sl=sl: e.dma_start(
                out=aps["xdbl_part"][sc * 512 + tt * 128: sc * 512 + tt * 128 + 128, :],
                in_=sl),
                waits=[("v", xp_evac_v[-1])], incs=[("dxdb", 16)])
    P.op("gpsimd", lambda e: e.collective_compute(
        "AllReduce", ALU.add, replica_groups=GROUPS,
        ins=[aps["xdbl_part"].opt()], outs=[aps["xdbl_sum"].opt()]),
        waits=[("dxdb", P.cnt["dxdb"])], incs=[("cc", 1)])
    cc_ar = P.cnt["cc"]
    P.wait("sync", [("cc", cc_ar)])
    P.wait("scalar", [("cc", cc_ar), ("dzs", P.cnt["dzs"])])
    if MAMBA_STOP == "ar":
        return

    P.op("vector", lambda e: e.memset(h_sb[:], 0.0),
         waits=[("v", P.cnt["v"])], incs=[("v", 1)])
    op_evac_v = []
    dt_s = []
    last_bc_mm = 0
    gate_v = []
    gate_s = []
    xdsum_r = aps["xdbl_sum"].rearrange("(t p) c -> p t c", p=128)
    for sc in range(NSC):
        wts = [("mm", last_bc_mm)] if sc >= 1 else [("cc", cc_ar)]
        d_xd = P.op("sync", lambda e, sc=sc: e.dma_start(
            out=xdbl_tm[:].rearrange("p (t c) -> p t c", t=4),
            in_=xdsum_r[:, sc * 4:sc * 4 + 4, :]),
            waits=wts, incs=[("dxd", 16)])[0]
        wts2 = [("v", gate_v[sc - 1]), ("s", gate_s[sc - 1])] if sc >= 1 else []
        d_xc = P.op("scalar", lambda e, sc=sc: e.dma_start(
            out=xcv[:].rearrange("p (dc c) -> p dc c", dc=NDC),
            in_=aps["xcv_sp"][:, :, sc * 512:(sc + 1) * 512].transpose([1, 0, 2])),
            waits=wts2, incs=[("dxc", 16)])[0]
        d_z = P.op("scalar", lambda e, sc=sc: e.dma_start(
            out=zc[:].rearrange("p (dc c) -> p dc c", dc=NDC),
            in_=aps["z_sp"][:, :, sc * 512:(sc + 1) * 512].transpose([1, 0, 2])),
            incs=[("dzr", 16)])[0]
        P.wait("tensor", [("dxd", d_xd), ("v", P.cnt["v"])])
        tr_mm = []
        tr_ev = []
        for tt in range(4):
            pp = [ps2, ps3][tt % 2]
            if tt >= 2:
                P.wait("tensor", [("v", tr_ev[tt - 2])])
            P.op("tensor", lambda e, tt=tt, pp=pp: e.transpose(
                pp[0:80, 0:128], xdbl_tm[:, tt * 96: tt * 96 + 80], id32[:]))
            mm1 = P.op("tensor", lambda e, tt=tt, pp=pp: e.transpose(
                pp[0:16, 128:256], xdbl_tm[:, tt * 96 + 80: tt * 96 + 96], id32[:]),
                incs=[("mm", 1)])[0]
            tr_mm.append(mm1)
            P.op("vector", lambda e, tt=tt, pp=pp: e.tensor_copy(
                dtraw[:, tt * 128:(tt + 1) * 128], pp[0:64, 0:128]),
                waits=[("mm", mm1), ("v", P.cnt["v"])], incs=[("v", 1)])
            P.op("vector", lambda e, tt=tt, pp=pp: e.tensor_copy(
                Bfm[:, tt * 128:(tt + 1) * 128], pp[64:80, 0:128]),
                waits=[("v", P.cnt["v"])], incs=[("v", 1)])
            P.op("vector", lambda e, tt=tt, pp=pp: e.tensor_copy(
                Cfm[:, tt * 128:(tt + 1) * 128], pp[0:16, 128:256]),
                waits=[("v", P.cnt["v"])], incs=[("v", 1)])
            tr_ev.append(P.cnt["v"])
        v_tr = P.cnt["v"]
        P.wait("tensor", [("v", v_tr)])
        for dc in range(NDC):
            pp = [ps0, ps1][dc % 2]
            if dc >= 2:
                P.wait("tensor", [("s", dt_s[-2])])
            mm_d = P.op("tensor", lambda e, dc=dc, pp=pp: e.matmul(
                pp[:], dtwT[:, dc * 128:(dc + 1) * 128], dtraw[:],
                start=True, stop=True), incs=[("mm", 1)])[0]
            t0 = mscr[:, 0:512]
            t1 = mscr[:, 512:1024]
            P.op("vector", lambda e, pp=pp, t0=t0, dc=dc: e.tensor_single_scalar(
                t0, pp[:], dtb[:, dc:dc + 1], ALU.add),
                waits=[("mm", mm_d), ("v", P.cnt["v"]), ("s", P.cnt["s"]),
                       ("dyp", P.cnt.get("dyp", 0))],
                incs=[("v", 1)])
            vedge()
            s_abs = P.op("scalar", lambda e, t0=t0, t1=t1: e.activation(t1, t0, AF.Abs),
                         waits=[("v", P.cnt["v"]), ("s", P.cnt["s"])], incs=[("s", 1)])[0]
            P.wait("scalar", [("s", s_abs)])
            s0 = P.op("scalar", lambda e, t1=t1: e.activation(t1, t1, AF.Exp, scale=-1.0),
                      incs=[("s", 1)])[0]
            P.wait("scalar", [("s", s0)])
            s1 = P.op("scalar", lambda e, t1=t1: e.activation(t1, t1, AF.Ln, bias=cst(P, "one")),
                      incs=[("s", 1)])[0]
            P.op("vector", lambda e, t0=t0: e.tensor_relu(t0, t0),
                 waits=[("v", P.cnt["v"]), ("s", s_abs)], incs=[("v", 1)])
            vedge()
            P.op("vector", lambda e, t0=t0, t1=t1, dc=dc: e.tensor_add(
                dts[:, dc * 512:(dc + 1) * 512], t0, t1),
                waits=[("s", s1)], incs=[("v", 1)])
            dt_s.append(P.cnt["s"])
        v_dt = P.cnt["v"]
        P.wait("vector", [("dxc", d_xc)])
        for dc in range(NDC):
            P.op("vector", lambda e, dc=dc: e.tensor_mul(
                dtu[:, dc * 512:(dc + 1) * 512], dts[:, dc * 512:(dc + 1) * 512],
                xcv[:, dc * 512:(dc + 1) * 512]),
                waits=[("v", P.cnt["v"])], incs=[("v", 1)])
            P.op("vector", lambda e, dc=dc: e.tensor_single_scalar(
                yacc[:, dc * 512:(dc + 1) * 512], xcv[:, dc * 512:(dc + 1) * 512],
                D_sb[:, dc:dc + 1], ALU.mult),
                waits=[("v", P.cnt["v"])], incs=[("v", 1)])
        v_du = P.cnt["v"]
        P.wait("tensor", [("v", v_du)])
        uu2 = 0
        exp_done = []
        scan_done = []
        ymul_done = []
        yadd_done = []
        for s_i in range(DST):
            P.wait("tensor", [("v", P.cnt["v"])])
            mmB = P.op("tensor", lambda e, s_i=s_i: e.matmul(
                psA[:], basis[:, s_i * 128:(s_i + 1) * 128], Bfm[:],
                start=True, stop=True), incs=[("mm", 1)])[0]
            mmC = P.op("tensor", lambda e, s_i=s_i: e.matmul(
                psB[:], basis[:, s_i * 128:(s_i + 1) * 128], Cfm[:],
                start=True, stop=True), incs=[("mm", 1)])[0]
            last_bc_mm = P.cnt["mm"]
            for dc in range(NDC):
                col = dc * DST + s_i
                par = uu2 % 2
                sA, sB, sS = scanA2[par], scanB2[par], states2[par]
                ew = [("s", P.cnt["s"])]
                if uu2 >= 2:
                    ew.append(("v", scan_done[uu2 - 2]))
                P.op("scalar", lambda e, dc=dc, col=col, sA=sA: e.activation(
                    sA[:], dts[:, dc * 512:(dc + 1) * 512], AF.Exp,
                    scale=A_sb[:, col:col + 1]), waits=ew, incs=[("s", 1)])
                exp_done.append(P.cnt["s"])
                bw = [("mm", mmB), ("v", P.cnt["v"])]
                P.op("vector", lambda e, dc=dc, sB=sB: e.tensor_mul(
                    sB[:], dtu[:, dc * 512:(dc + 1) * 512], psA[:]),
                    waits=bw, incs=[("v", 1)])
                vedge()
                sw = [("s", exp_done[uu2])]
                if uu2 >= 2:
                    sw.append(("v", ymul_done[uu2 - 2]))
                P.op("vector", lambda e, col=col, sA=sA, sB=sB, sS=sS: e.tensor_tensor_scan(
                    sS[:], sA[:], sB[:], h_sb[:, col:col + 1],
                    ALU.mult, ALU.add), waits=sw, incs=[("v", 1)])
                scan_done.append(P.cnt["v"])
                vedge()
                P.op("vector", lambda e, col=col, sS=sS: e.tensor_copy(
                    h_sb[:, col:col + 1], sS[:, 511:512]), incs=[("v", 1)])
                P.op("vector", lambda e, sS=sS: e.tensor_mul(ytbuf[:], sS[:], psB[:]),
                     waits=[("mm", mmC)], incs=[("v", 1)])
                ymul_done.append(P.cnt["v"])
                vedge()
                P.op("vector", lambda e, dc=dc: e.tensor_add(
                    yacc[:, dc * 512:(dc + 1) * 512],
                    yacc[:, dc * 512:(dc + 1) * 512], ytbuf[:]), incs=[("v", 1)])
                yadd_done.append(P.cnt["v"])
                uu2 += 1
        for dc in range(NDC):
            P.op("scalar", lambda e, dc=dc: e.activation(
                sz[:], zc[:, dc * 512:(dc + 1) * 512], AF.Silu),
                waits=[("v", P.cnt["v"]), ("s", P.cnt["s"]), ("dzr", d_z)],
                incs=[("s", 1)])
            P.op("vector", lambda e, dc=dc: e.tensor_mul(
                yg[:, dc * 512:(dc + 1) * 512], yacc[:, dc * 512:(dc + 1) * 512],
                sz[:]), waits=[("s", P.cnt["s"]), ("v", P.cnt["v"])], incs=[("v", 1)])
        gate_v.append(P.cnt["v"])
        gate_s.append(P.cnt["s"])
        P.wait("tensor", [("v", gate_v[-1])])
        for tt in range(4):
            for dh in range(2):
                u = len(op_evac_v)
                pp = [ps2, ps3][u % 2]
                if u >= 2:
                    P.wait("tensor", [("v", op_evac_v[u - 2])])
                for dc in range(NDC):
                    ic = [("mm", 1)] if dc == NDC - 1 else []
                    P.op("tensor", lambda e, tt=tt, dh=dh, dc=dc, pp=pp: e.matmul(
                        pp[:], yg[:, dc * 512 + tt * 128: dc * 512 + tt * 128 + 128],
                        opT[:, dc * D + dh * 512: dc * D + dh * 512 + 512],
                        start=(dc == 0), stop=(dc == NDC - 1)), incs=ic)
                mm_o = P.cnt["mm"]
                sl = mscr[:, (u % 2) * 512:(u % 2) * 512 + 512]
                P.op("vector", lambda e, sl=sl, pp=pp: e.tensor_copy(sl, pp[:]),
                     waits=[("mm", mm_o), ("v", P.cnt["v"]),
                            ("dyp", 16 * u)],
                     incs=[("v", 1)])
                op_evac_v.append(P.cnt["v"])
                P.op("sync", lambda e, sc=sc, tt=tt, dh=dh, sl=sl: e.dma_start(
                    out=aps["yprt"][sc * 512 + tt * 128: sc * 512 + tt * 128 + 128,
                                    dh * 512:(dh + 1) * 512],
                    in_=sl), waits=[("v", op_evac_v[-1])], incs=[("dyp", 16)])
    if MAMBA_STOP == "rs":
        return
    P.op("gpsimd", lambda e: e.collective_compute(
        "ReduceScatter", ALU.add, replica_groups=GROUPS,
        ins=[aps["yprt"].opt()], outs=[aps["rs_out"].opt()]),
        waits=[("dyp", P.cnt["dyp"])], incs=[("cc", 1)])
    cc_rs = P.cnt["cc"]
    m4 = P.tiles["m4buf"]
    P.wait("sync", [("cc", cc_rs)])
    rsr = aps["rs_out"].rearrange("(t p) d -> p t d", p=128)
    for rnd in range(8):
        wts = [("v", P.cnt["v"])] if rnd else []
        d_r = P.op("sync", lambda e, rnd=rnd: e.dma_start(
            out=m4[:].rearrange("p (t d) -> p t d", t=2),
            in_=rsr[:, rnd * 2:rnd * 2 + 2, :]),
            waits=wts, incs=[("drs", 16)])[0]
        for j in range(2):
            t = rnd * 2 + j
            P.op("vector", lambda e, t=t, j=j: e.tensor_add(
                x_sb[:, t * D:(t + 1) * D], x_sb[:, t * D:(t + 1) * D],
                m4[:, j * D:(j + 1) * D]),
                waits=[("drs", d_r), ("v", P.cnt["v"])], incs=[("v", 1)])


def build_chain(parts, debug_dump=()):
    nc = bass.Bass("TRN2", target_bir_lowering=False)
    es = contextlib.ExitStack()
    inp = {}

    def di(name, shape, dtype=F32):
        if name not in inp:
            inp[name] = nc.dram_tensor(name, list(shape), dtype, kind="ExternalInput").ap()
        return inp[name]

    x_in = di("x", (T, D))
    ident = di("ident", (128, 128), BF16)
    onesbf = di("onesbf", (1, 128), BF16)
    ident32 = di("ident32", (128, 128), F32)
    xo = nc.dram_tensor("xo", [T, D], F32, kind="ExternalOutput").ap()

    P = Prog(nc, es)
    P.ident32_ap = ident32
    alloc_common(P)
    if "l1ffn" in parts:
        alloc_l1ffn(P)
    if "moe" in parts:
        alloc_moe(P)
    if "mamba" in parts:
        alloc_mamba(P)
    if "l1attn" in parts:
        alloc_l1attn(P)

    emit_load_consts(P, ident, onesbf)
    emit_load_x(P, x_in)
    x_ready = [("dma", P.cnt["dma"])]

    for part in parts:
        if part == "l1ffn":
            th = di("L1_ffn_theta", (D, D))
            tn = di("L1_ffn_theta_noise", (D, D))
            ph = di("L1_ffn_phi", (D, D))
            pn = di("L1_ffn_phi_noise", (D, D))
            emit_l1ffn_prep(P, th, tn, ph, pn)
            prep_s = P.cnt["s"]
            v_done, s_done = emit_ln(P, x_ready=x_ready)
            emit_l1ffn_main(P, xf_ready=[("v", v_done), ("s", max(s_done, prep_s))])
        elif part == "l1attn":
            aps = {}
            for nm in ("L1_attn_wq", "L1_attn_wk", "L1_attn_wv", "L1_attn_wo",
                       "L1_attn_w_d"):
                aps[nm] = di(nm, (D, D))
            aps["L1_attn_w_ff1"] = di("L1_attn_w_ff1", (D, 4 * D))
            aps["L1_attn_w_ff2"] = di("L1_attn_w_ff2", (4 * D, D))
            for nm in ("L1_attn_bo", "L1_attn_b_ff2", "L1_attn_b_d"):
                aps[nm] = di(nm, (D,))
            for nm in ("a_ln1g", "a_ln1b", "a_ln2g", "a_ln2b"):
                aps[nm] = di(nm, (128, NDK))
            aps["a_bff1"] = di("a_bff1", (128, 32))
            aps["x_old"] = nc.dram_tensor("x_old", [T, D], F32).ap()
            aps["ctx_part"] = nc.dram_tensor("ctx_part", [128, 8 * 264], F32).ap()
            aps["ctx_sum"] = nc.dram_tensor("ctx_sum", [128, 8 * 264], F32).ap()
            emit_l1attn(P, aps, x_ready)
        elif part == "mamba":
            aps = {}
            for (nm, shp, dt) in (
                    ("m_winT", (D, 1024), F32), ("m_opT", (DI, D), F32),
                    ("m_xpwT", (DI, 96), F32), ("m_dtwT", (64, DI), F32),
                    ("basis16", (16, 2048), BF16), ("m_Alog", (128, NDC * DST), F32),
                    ("m_convw", (128, NDC * 4), F32), ("m_convb", (128, NDC), F32),
                    ("m_D", (128, NDC), F32), ("m_dtb", (128, NDC), F32),
                    ("m_lnw", (128, NDK), F32), ("m_lnb", (128, NDK), F32)):
                aps[nm] = di(nm, shp, dt)
            for (nm, shp, dt) in (
                    ("ag_in", (D, T), BF16), ("ag_out", (4 * D, T), BF16),
                    ("xdbl_part", (SEQ, 96), F32), ("xdbl_sum", (SEQ, 96), F32),
                    ("z_sp", (NDC, 128, SEQ), BF16), ("xcv_sp", (NDC, 128, SEQ), BF16),
                    ("yprt", (SEQ, D), F32), ("rs_out", (T, D), F32)):
                aps[nm] = nc.dram_tensor(nm, list(shp), dt).ap()
            emit_mamba(P, aps, x_ready)
        elif part == "moe":
            rw = di("L0_ffn_router_w", (NE, D))
            W1 = di("L0_ffn_W1", (NE, D, 4096))
            b1 = di("L0_ffn_b1", (NE, 4096))
            W2 = di("L0_ffn_W2", (NE, 4096, D))
            b2 = di("L0_ffn_b2", (NE, D))
            emit_moe_prep(P, rw, b1, b2)
            v_done, s_done = emit_ln(P, x_ready=x_ready)
            xf_ready = [("v", v_done), ("s", s_done)]
            rv = emit_moe_router(P, xf_ready)
            emit_moe_bias(P, rv)
            emit_moe_experts(P, W1, W2, xf_ready)
        else:
            raise NotImplementedError(part)
        x_ready = [("v", P.cnt["v"]), ("s", P.cnt["s"])]

    emit_store_x(P, xo)
    if debug_dump:
        for name in debug_dump:
            tile = P.tiles[name]
            shp = [tile.shape[0], tile.shape[1]]
            dt = tile.dtype
            o = nc.dram_tensor("dbg_" + name, shp, dt, kind="ExternalOutput").ap()
            P.op("sync", lambda e, o=o, tile=tile: e.dma_start(out=o, in_=tile[:]),
                 incs=[("st", 16)])
        P.wait("sync", [("st", P.cnt["st"])])
    P.finalize()
    es.close()
    return nc


IDENT_BF = np.eye(128, dtype=np.float32).astype(BF)
ONES_BF = np.ones((1, 128), dtype=np.float32).astype(BF)
IDENT_F32 = np.eye(128, dtype=np.float32)
BASIS16 = np.zeros((16, 2048), np.float32)
for _s in range(16):
    BASIS16[_s, _s * 128:(_s + 1) * 128] = 1.0
BASIS16 = BASIS16.astype(BF)


def _col128(v):
    return np.ascontiguousarray(v.reshape(-1, 128).T)


def shard_inputs(inputs, parts=()):
    x = np.ascontiguousarray(np.asarray(inputs["L0_attn_x"]))
    maps = []
    if "mamba" in parts:
        winT = np.asarray(inputs["L0_attn_in_proj_w"]).T
        opT = np.asarray(inputs["L0_attn_out_proj_w"]).T
        xpwT = np.asarray(inputs["L0_attn_x_proj_w"]).T
        dtwT = np.asarray(inputs["L0_attn_dt_proj_w"]).T
        alog = np.asarray(inputs["L0_attn_A_log"])
        cwv = np.asarray(inputs["L0_attn_conv_w"])[:, 0, :]
        cbv = np.asarray(inputs["L0_attn_conv_b"])
        Dv = np.asarray(inputs["L0_attn_D"])
        dtbv = np.asarray(inputs["L0_attn_dt_proj_b"])
        lnw = np.asarray(inputs["L0_attn_ln_w"])
        lnb = np.asarray(inputs["L0_attn_ln_b"])
    for c in range(8):
        b, s = c // 4, c % 4
        m = {"x": np.ascontiguousarray(x[b, s * T:(s + 1) * T]),
             "ident": IDENT_BF, "onesbf": ONES_BF, "ident32": IDENT_F32}
        if "moe" in parts or "l1ffn" in parts or "l1attn" in parts:
            for kk in inputs:
                if kk.startswith("L0_ffn_") or kk.startswith("L1_"):
                    m[kk] = np.ascontiguousarray(np.asarray(inputs[kk]))
        if "l1attn" in parts:
            m["a_ln1g"] = _col128(np.asarray(inputs["L1_attn_ln1_g"]))
            m["a_ln1b"] = _col128(np.asarray(inputs["L1_attn_ln1_b"]))
            m["a_ln2g"] = _col128(np.asarray(inputs["L1_attn_ln2_g"]))
            m["a_ln2b"] = _col128(np.asarray(inputs["L1_attn_ln2_b"]))
            m["a_bff1"] = _col128(np.asarray(inputs["L1_attn_b_ff1"]))
        if "mamba" in parts:
            r = s
            sl = slice(r * DI, (r + 1) * DI)
            m["m_winT"] = np.ascontiguousarray(
                np.concatenate([winT[:, sl], winT[:, 2048 + r * DI:2048 + (r + 1) * DI]], axis=1))
            m["m_opT"] = np.ascontiguousarray(opT[sl])
            m["m_xpwT"] = np.ascontiguousarray(xpwT[sl])
            m["m_dtwT"] = np.ascontiguousarray(dtwT[:, sl])
            m["basis16"] = BASIS16
            m["m_Alog"] = np.ascontiguousarray(
                alog[sl].reshape(NDC, 128, DST).transpose(1, 0, 2).reshape(128, NDC * DST))
            m["m_convw"] = np.ascontiguousarray(
                cwv[sl].reshape(NDC, 128, 4).transpose(1, 0, 2).reshape(128, NDC * 4))
            m["m_convb"] = _col128(cbv[sl])
            m["m_D"] = _col128(Dv[sl])
            m["m_dtb"] = _col128(dtbv[sl])
            m["m_lnw"] = _col128(lnw)
            m["m_lnb"] = _col128(lnb)
        maps.append(m)
    return maps


_PROGS = {}


def _prog(part):
    if part not in _PROGS:
        _PROGS[part] = build_chain([part])
    return _PROGS[part]


def _input_names(nc):
    import concourse.mybir as mb
    names = set()
    for alloc in nc.m.functions[0].allocations:
        if isinstance(alloc, mb.MemoryLocationSet) and alloc.kind == "ExternalInput":
            names.add(alloc.memorylocations[0].name)
    return names


def kernel(**inputs):
    x = np.asarray(inputs["L0_attn_x"])
    chain = ["moe", "mamba", "moe", "l1ffn", "l1attn", "l1ffn"]
    cur = {"L0_attn_x": x}
    cur.update({kk: v for kk, v in inputs.items() if kk != "L0_attn_x"})
    for part in chain:
        nc = _prog(part)
        names = _input_names(nc)
        maps = shard_inputs(cur, parts=[part])
        maps = [{kk: v for kk, v in m.items() if kk in names} for m in maps]
        res = run_bass_kernel_spmd(nc, maps, core_ids=list(range(8)))
        xo = np.zeros((2, 8192, 1024), np.float32)
        for c in range(8):
            b, s = c // 4, c % 4
            xo[b, s * T:(s + 1) * T] = res.results[c]["xo"]
        cur["L0_attn_x"] = xo
    return cur["L0_attn_x"]



# version 06 (final)
# speedup vs baseline: 1.0366x; 1.0217x over previous
import contextlib
import numpy as np
import ml_dtypes
import concourse.bass as bass
import concourse.mybir as mybir
from concourse.bass_utils import run_bass_kernel_spmd

F32 = mybir.dt.float32
BF16 = mybir.dt.bfloat16
AF = mybir.ActivationFunctionType
ALU = mybir.AluOpType
AX = mybir.AxisListType

T = 2048
NT = T // 128
D = 1024
NDK = D // 128
PI = float(np.pi)
ENGINES = ("sync", "vector", "scalar", "tensor", "gpsimd")
GROUPS = [[0, 1, 2, 3], [4, 5, 6, 7]]
BF = ml_dtypes.bfloat16


class Prog:

    def __init__(self, nc, es):
        self.nc = nc
        self.es = es
        self.cmds = {e: [] for e in ENGINES}
        self.sem = {}
        self.cnt = {}
        self.tiles = {}
        for s in ("dma", "st", "mm", "v", "s", "g", "cc"):
            self.ensure_sem(s)

    def ensure_sem(self, name):
        if name not in self.sem:
            self.sem[name] = self.es.enter_context(self.nc.semaphore(name))
            self.cnt[name] = 0
        return self.sem[name]

    def bump(self, name, n):
        self.cnt[name] += n
        return self.cnt[name]

    def op(self, eng, fn, waits=(), incs=()):
        totals = [(self.ensure_sem(s), self.bump(s, n)) for (s, n) in incs]
        wl = [(self.ensure_sem(s), v) for (s, v) in waits if v > 0]

        def cmd(e):
            for (s, v) in wl:
                e.wait_ge(s, v)
            inst = fn(e)
            for i, (s, _tot) in enumerate(totals):
                inst = inst.then_inc(s, incs[i][1])
        self.cmds[eng].append(cmd)
        return [t for (_s, t) in totals]

    def wait(self, eng, waits):
        wl = [(self.ensure_sem(s), v) for (s, v) in waits if v > 0]

        def cmd(e):
            for (s, v) in wl:
                e.wait_ge(s, v)
        self.cmds[eng].append(cmd)

    def sbuf(self, name, shape, dtype):
        t = self.es.enter_context(self.nc.sbuf_tensor(name, list(shape), dtype))
        self.tiles[name] = t
        return t

    def psum(self, name, shape=(128, 512), dtype=F32):
        t = self.es.enter_context(self.nc.psum_tensor(name, list(shape), dtype))
        self.tiles[name] = t
        return t

    def finalize(self):
        with self.nc.Block() as block:
            for eng in ENGINES:
                cmds = self.cmds[eng]
                if not cmds:
                    continue
                getattr(block, eng)(lambda e, cmds=cmds: [c(e) for c in cmds])


CONSTS = {"eps": (0, 1e-5), "halfpi": (1, PI / 2), "neg1": (2, -1.0), "one": (3, 1.0)}


def alloc_common(P):
    P.sbuf("const_sb", (128, 8), F32)
    P.sbuf("x_sb", (128, NT * D), F32)
    P.sbuf("xln_sb", (128, NT * D), BF16)
    P.sbuf("xf_sb", (128, NDK * T), BF16)
    P.sbuf("stat_sb", (128, NT * 8), F32)
    P.sbuf("ident_sb", (128, 128), BF16)
    P.sbuf("id32_sb", (128, 128), F32)
    P.sbuf("onesbf_sb", (1, 128), BF16)
    for i in range(4):
        P.psum(f"ps{i}")
    P.psum("pt0", (128, 1024), BF16)
    P.psum("pt1", (128, 1024), BF16)
    P.psum("psA")
    P.psum("psB")


def cst(P, name):
    return P.tiles["const_sb"][:, CONSTS[name][0]:CONSTS[name][0] + 1]


def emit_load_consts(P, ident_ap, onesbf_ap):
    P.op("sync", lambda e: e.dma_start(out=P.tiles["ident_sb"][:], in_=ident_ap),
         incs=[("dma", 16)])
    P.op("sync", lambda e: e.dma_start(out=P.tiles["onesbf_sb"][:], in_=onesbf_ap),
         incs=[("dma", 16)])
    P.op("sync", lambda e: e.dma_start(out=P.tiles["id32_sb"][:], in_=P.ident32_ap),
         incs=[("dma", 16)])
    cs = P.tiles["const_sb"]
    for (_nm, (slot, val)) in CONSTS.items():
        P.op("vector", lambda e, slot=slot, val=val: e.memset(cs[:, slot:slot + 1], val))
    P.op("vector", lambda e: e.memset(cs[:, 7:8], 0.0), incs=[("v", 1)])
    P.wait("scalar", [("v", P.cnt["v"])])
    P.wait("gpsimd", [("v", P.cnt["v"])])


def emit_load_x(P, x_hbm):
    x_sb = P.tiles["x_sb"]
    P.op("sync", lambda e: e.dma_start(
        out=x_sb[:].rearrange("p (t d) -> p t d", t=NT),
        in_=x_hbm.rearrange("(t p) d -> p t d", p=128)),
        incs=[("dma", 16)])


def emit_store_x(P, x_hbm_out):
    x_sb = P.tiles["x_sb"]
    P.op("sync", lambda e: e.dma_start(
        out=x_hbm_out.rearrange("(t p) d -> p t d", p=128),
        in_=x_sb[:].rearrange("p (t d) -> p t d", t=NT)),
        waits=[("v", P.cnt["v"]), ("s", P.cnt["s"])],
        incs=[("st", 16)])
    P.wait("sync", [("st", P.cnt["st"])])


def emit_ln(P, x_ready=(), identity=False, gain=None, mamba_double=False,
            gain_ready=(), inplace=False):
    x_sb, xln, xf = P.tiles["x_sb"], P.tiles["xln_sb"], P.tiles["xf_sb"]
    st = P.tiles["stat_sb"]
    trash = xln
    sums = st[:, 0 * NT:1 * NT]
    sq = st[:, 1 * NT:2 * NT]
    mean = st[:, 2 * NT:3 * NT]
    var = st[:, 3 * NT:4 * NT]
    rstd = st[:, 4 * NT:5 * NT]
    nmr = st[:, 5 * NT:6 * NT]

    if not identity:
        P.wait("vector", x_ready)
        P.wait("scalar", x_ready)
        sq_base = P.cnt["s"]
        for t in range(NT):
            xt = x_sb[:, t * D:(t + 1) * D]
            P.op("vector", lambda e, xt=xt, t=t: e.tensor_reduce(
                sums[:, t:t + 1], xt, AX.X, ALU.add),
                incs=([("v", 1)] if t == NT - 1 else []))
            w = [("s", sq_base + t - 1)] if t >= 2 else []
            P.op("scalar", lambda e, xt=xt, t=t: e.activation(
                trash[:, (t % 2) * D:(t % 2) * D + D], xt, AF.Square,
                accum_out=sq[:, t:t + 1]), waits=w, incs=[("s", 1)])
        s_sq = P.cnt["s"]
        def vedge():
            P.wait("vector", [("v", P.cnt["v"])])
        P.op("vector", lambda e: e.tensor_scalar_mul(mean[:], sums[:], 1.0 / D),
             waits=[("v", P.cnt["v"])], incs=[("v", 1)])
        vedge()
        P.op("vector", lambda e: e.tensor_scalar_mul(var[:], sq[:], 1.0 / D),
             waits=[("s", s_sq)], incs=[("v", 1)])
        vedge()
        P.op("vector", lambda e: e.tensor_mul(sums[:], mean[:], mean[:]), incs=[("v", 1)])
        vedge()
        v_var = P.op("vector", lambda e: e.tensor_sub(var[:], var[:], sums[:]),
                     incs=[("v", 1)])[0]
        s_lnop = P.op("scalar", lambda e: e.activation(rstd[:], var[:], AF.Ln, bias=cst(P, "eps")),
             waits=[("v", v_var)], incs=[("s", 1)])[0]
        P.wait("scalar", [("s", s_lnop)])
        s_ln = P.op("scalar", lambda e: e.activation(rstd[:], rstd[:], AF.Exp, scale=-0.5),
                    incs=[("s", 1)])[0]
        P.op("vector", lambda e: e.scalar_tensor_tensor(
            nmr[:], mean[:], -1.0, rstd[:], ALU.mult, ALU.mult),
            waits=[("s", s_ln), ("v", P.cnt["v"])], incs=[("v", 1)])
        v_nmr = P.cnt["v"]
        rstd_a, nmr_a = rstd, nmr
        if mamba_double:
            rc = st[:, 6 * NT:7 * NT]
            nc2 = st[:, 7 * NT:8 * NT]

            def ved():
                P.wait("vector", [("v", P.cnt["v"])])
            P.op("vector", lambda e: e.tensor_mul(rc[:], rstd[:], rstd[:]),
                 waits=[("v", P.cnt["v"])], incs=[("v", 1)])
            ved()
            P.op("vector", lambda e: e.tensor_mul(rc[:], rc[:], var[:]), incs=[("v", 1)])
            v_t = P.cnt["v"]
            s_a = P.op("scalar", lambda e: e.activation(rc[:], rc[:], AF.Ln, bias=cst(P, "eps")),
                       waits=[("v", v_t)], incs=[("s", 1)])[0]
            P.wait("scalar", [("s", s_a)])
            s_b = P.op("scalar", lambda e: e.activation(rc[:], rc[:], AF.Exp, scale=-0.5),
                       incs=[("s", 1)])[0]
            P.op("vector", lambda e: e.tensor_mul(rc[:], rc[:], rstd[:]),
                 waits=[("s", s_b), ("v", P.cnt["v"])], incs=[("v", 1)])
            ved()
            P.op("vector", lambda e: e.scalar_tensor_tensor(
                nc2[:], mean[:], -1.0, rc[:], ALU.mult, ALU.mult), incs=[("v", 1)])
            v_nmr = P.cnt["v"]
            rstd_a, nmr_a = rc, nc2
        P.wait("scalar", [("v", v_nmr)])
        if inplace:
            for t in range(NT):
                P.op("scalar", lambda e, t=t: e.activation(
                    x_sb[:, t * D:(t + 1) * D], x_sb[:, t * D:(t + 1) * D],
                    AF.Identity, scale=rstd[:, t:t + 1], bias=nmr[:, t:t + 1]),
                    incs=[("s", 1)])
            return P.cnt["v"], P.cnt["s"]
        for t in range(NT):
            ic = [("s", 1)] if t == NT - 1 else []
            P.op("scalar", lambda e, t=t, rstd_a=rstd_a, nmr_a=nmr_a: e.activation(
                xln[:, t * D:(t + 1) * D], x_sb[:, t * D:(t + 1) * D], AF.Identity,
                scale=rstd_a[:, t:t + 1], bias=nmr_a[:, t:t + 1]), incs=ic)
        if mamba_double:
            s_apply = P.cnt["s"]
            u_t = P.tiles["mscr"][:, 0:D]
            P.wait("vector", [("s", s_apply)])
            for t in range(NT):
                P.op("scalar", lambda e, t=t, u_t=u_t: e.activation(
                    u_t, x_sb[:, t * D:(t + 1) * D], AF.Identity,
                    scale=rstd[:, t:t + 1], bias=nmr[:, t:t + 1]),
                    waits=[("v", P.cnt["v"])], incs=[("s", 1)])
                P.op("vector", lambda e, t=t, u_t=u_t: e.tensor_add(
                    x_sb[:, t * D:(t + 1) * D], x_sb[:, t * D:(t + 1) * D], u_t),
                    waits=[("s", P.cnt["s"]), ("v", P.cnt["v"])], incs=[("v", 1)])
    else:
        P.wait("scalar", x_ready)
        for t in range(NT):
            ic = [("s", 1)] if t == NT - 1 else []
            P.op("scalar", lambda e, t=t: e.activation(
                xln[:, t * D:(t + 1) * D], x_sb[:, t * D:(t + 1) * D], AF.Copy), incs=ic)
    s_xln = P.cnt["s"]

    ident = P.tiles["ident_sb"]
    pt = [P.tiles["pt0"], P.tiles["pt1"]]
    P.wait("tensor", [("s", s_xln)])
    evac_tot = []
    g = 0
    for dk in range(NDK):
        for t0 in range(0, NT, 4):
            par = g % 2
            if g >= 2:
                P.wait("tensor", [("v", evac_tot[g - 2])])
            for j in range(4):
                t = t0 + j
                ic = [("mm", 1)] if j == 3 else []
                P.op("tensor", lambda e, dk=dk, t=t, par=par, j=j: e.transpose(
                    pt[par][:, j * 128:(j + 1) * 128],
                    xln[:, t * D + dk * 128: t * D + dk * 128 + 128],
                    ident[:]), incs=ic)
            mm_tot = P.cnt["mm"]
            P.op("vector", lambda e, dk=dk, t0=t0, par=par: e.tensor_copy(
                xf[:, dk * T + t0 * 128: dk * T + t0 * 128 + 512],
                pt[par][:, 0:512]),
                waits=[("mm", mm_tot)], incs=[("v", 1)])
            evac_tot.append(P.cnt["v"])
            g += 1
    v_done = P.cnt["v"]
    s_done = P.cnt["s"]
    if gain is not None:
        w_sb, b_sb = gain
        P.wait("scalar", [("v", v_done)] + list(gain_ready))
        for dk in range(NDK):
            ic = [("s", 1)] if dk == NDK - 1 else []
            P.op("scalar", lambda e, dk=dk: e.activation(
                xf[:, dk * T:(dk + 1) * T], xf[:, dk * T:(dk + 1) * T], AF.Identity,
                scale=w_sb[:, dk:dk + 1], bias=b_sb[:, dk:dk + 1]), incs=ic)
        s_done = P.cnt["s"]
    return v_done, s_done



def alloc_l1ffn(P):
    P.sbuf("cm_sb", (128, NDK * D), BF16)
    P.sbuf("sm_sb", (128, NDK * D), BF16)
    if "scr" not in P.tiles:
        P.sbuf("scr", (128, 4096), F32)


def emit_l1ffn_prep(P, theta, theta_noise, phi, phi_noise):
    scr = P.tiles["scr"]
    for (base, noise, out_name, bias) in ((theta, theta_noise, "cm_sb", PI / 2),
                                          (phi, phi_noise, "sm_sb", 0.0)):
        for half in range(2):
            hb = base.rearrange("(k p) o -> p k o", p=128)[:, 4 * half:4 * half + 4, :]
            hn = noise.rearrange("(k p) o -> p k o", p=128)[:, 4 * half:4 * half + 4, :]
            s_prev = P.cnt["s"]
            d0 = P.op("sync", lambda e, hb=hb: e.dma_start(
                out=scr[:].rearrange("p (k o) -> p k o", k=4), in_=hb),
                waits=[("s", s_prev)], incs=[("dprep", 16)])[0]
            d1 = P.op("gpsimd", lambda e, hn=hn: e.dma_start(
                out=scr[:].rearrange("p (k o) -> p k o", k=4), in_=hn,
                accum_op=ALU.add),
                waits=[("dprep", d0)], incs=[("g", 16)])[0]
            P.op("scalar", lambda e, out_name=out_name, bias=bias, half=half: e.activation(
                P.tiles[out_name][:, half * 4096:(half + 1) * 4096], scr[:],
                AF.Sin, bias=(cst(P, "halfpi") if bias else 0.0)),
                waits=[("g", d1)], incs=[("s", 1)])


def emit_l1ffn_main(P, xf_ready):
    xf, x_sb = P.tiles["xf_sb"], P.tiles["x_sb"]
    cm, sm = P.tiles["cm_sb"], P.tiles["sm_sb"]
    scr = P.tiles["scr"]
    ps = [P.tiles["ps0"], P.tiles["ps1"], P.tiles["ps2"], P.tiles["ps3"]]
    P.wait("tensor", list(xf_ready))
    u = 0
    sq_done = []
    add_done = []
    stt_done = []
    for tc in range(NT):
        for oh in range(2):
            par = u % 2
            pr, pi_ = ps[par], ps[2 + par]
            if u >= 2:
                P.wait("tensor", [("s", sq_done[u - 2])])
            for ik in range(NDK):
                ic = [("mm", 1)] if ik == NDK - 1 else []
                P.op("tensor", lambda e, ik=ik, tc=tc, oh=oh, pr=pr: e.matmul(
                    pr[:], xf[:, ik * T + tc * 128: ik * T + tc * 128 + 128],
                    cm[:, ik * D + oh * 512: ik * D + oh * 512 + 512],
                    start=(ik == 0), stop=(ik == NDK - 1)), incs=ic)
            for ik in range(NDK):
                ic = [("mm", 1)] if ik == NDK - 1 else []
                P.op("tensor", lambda e, ik=ik, tc=tc, oh=oh, pi_=pi_: e.matmul(
                    pi_[:], xf[:, ik * T + tc * 128: ik * T + tc * 128 + 128],
                    sm[:, ik * D + oh * 512: ik * D + oh * 512 + 512],
                    start=(ik == 0), stop=(ik == NDK - 1)), incs=ic)
            mm_tot = P.cnt["mm"]
            r2 = scr[:, par * 2048 + 0:par * 2048 + 512]
            i2 = scr[:, par * 2048 + 512:par * 2048 + 1024]
            mag = scr[:, par * 2048 + 1024:par * 2048 + 1536]
            sqw = [("mm", mm_tot - 1)]
            if u >= 2:
                sqw.append(("v", add_done[u - 2]))
            P.op("scalar", lambda e, pr=pr, r2=r2: e.activation(r2, pr[:], AF.Square),
                 waits=sqw, incs=[("s", 1)])
            P.op("scalar", lambda e, pi_=pi_, i2=i2: e.activation(i2, pi_[:], AF.Square),
                 waits=[("mm", mm_tot)], incs=[("s", 1)])
            sq_done.append(P.cnt["s"])
            P.op("vector", lambda e, r2=r2, i2=i2: e.tensor_add(r2, r2, i2),
                 waits=[("s", sq_done[u])], incs=[("v", 1)])
            add_done.append(P.cnt["v"])
            sqrw = [("v", add_done[u])]
            if u >= 2:
                sqrw.append(("v", stt_done[u - 2]))
            P.op("scalar", lambda e, mag=mag, r2=r2: e.activation(mag, r2, AF.Sqrt),
                 waits=sqrw, incs=[("s", 1)])
            s_sqrt = P.cnt["s"]
            xs = x_sb[:, tc * D + oh * 512: tc * D + oh * 512 + 512]
            P.op("vector", lambda e, mag=mag, xs=xs: e.scalar_tensor_tensor(
                xs, mag, 0.5, xs, ALU.mult, ALU.add),
                waits=[("s", s_sqrt)], incs=[("v", 1)])
            stt_done.append(P.cnt["v"])
            u += 1



NV = 32
FV = 1024
NE = 8


def alloc_moe(P):
    P.sbuf("rw_sb", (8, D), BF16)
    P.sbuf("rwT_sb", (128, NDK * NE), BF16)
    P.sbuf("b1stg", (8, D), F32)
    P.sbuf("b1f", (128, 32 * NE), F32)
    P.sbuf("b2_sb", (8, D), BF16)
    P.sbuf("w_tm", (128, NT * NE), F32)
    P.sbuf("w_fm", (8, T), BF16)
    P.sbuf("exps_sb", (128, 2 * NE), F32)
    P.sbuf("msc", (128, 16), F32)
    for p in range(2):
        P.sbuf(f"w1_buf{p}", (128, NDK * FV), BF16)
        P.sbuf(f"w2_buf{p}", (128, NDK * FV), BF16)


def emit_moe_prep(P, rw_ap, b1_ap, b2_ap):
    rw, rwT = P.tiles["rw_sb"], P.tiles["rwT_sb"]
    b1stg = P.tiles["b1stg"]
    b1f, b2 = P.tiles["b1f"], P.tiles["b2_sb"]
    pt0, psB = P.tiles["pt0"], P.tiles["psB"]
    ident, id32 = P.tiles["ident_sb"], P.tiles["id32_sb"]
    d_rw = P.op("gpsimd", lambda e: e.dma_start(out=rw[:], in_=rw_ap),
                incs=[("dmoe_g", 16)])[0]
    d_b2 = P.op("gpsimd", lambda e: e.dma_start(out=b2[:], in_=b2_ap),
                incs=[("dmoe_g", 16)])[0]

    P.wait("tensor", [("dmoe_g", d_rw + 16), ("dma", P.cnt["dma"])])
    for dk in range(NDK):
        ic = [("mm", 1)] if dk == NDK - 1 else []
        P.op("tensor", lambda e, dk=dk: e.transpose(
            pt0[:, dk * 8:dk * 8 + 8], rw[:, dk * 128:(dk + 1) * 128], ident[0:8, 0:8]),
            incs=ic)
    mm_rw = P.cnt["mm"]
    v_rw = P.op("vector", lambda e: e.tensor_copy(rwT[:], pt0[:, 0:NDK * 8]),
                waits=[("mm", mm_rw)], incs=[("v", 1)])[0]
    b1r = b1_ap.rearrange("e (bat c) -> bat e c", bat=4)
    for bat in range(4):
        d_b1 = P.op("sync", lambda e, bat=bat: e.dma_start(out=b1stg[:], in_=b1r[bat]),
                    waits=[("mm", P.cnt["mm"])], incs=[("dmoe", 16)])[0]
        P.wait("tensor", [("dmoe", d_b1), ("v", P.cnt["v"])])
        for j8 in range(8):
            j = bat * 8 + j8
            ic = [("mm", 1)] if j8 == 7 else []
            P.op("tensor", lambda e, j=j, j8=j8: e.transpose(
                psB[:, j8 * 8:j8 * 8 + 8], b1stg[:, j8 * 128:(j8 + 1) * 128], id32[0:8, 0:8]),
                incs=ic)
        mm_b = P.cnt["mm"]
        P.op("vector", lambda e, bat=bat: e.tensor_copy(
            b1f[:, bat * 64:(bat + 1) * 64], psB[:, 0:64]),
            waits=[("mm", mm_b)], incs=[("v", 1)])
    return P.cnt["v"], P.cnt["s"]


def emit_moe_router(P, xf_ready):
    xf, rwT = P.tiles["xf_sb"], P.tiles["rwT_sb"]
    w_tm, w_fm = P.tiles["w_tm"], P.tiles["w_fm"]
    exps2, msc = P.tiles["exps_sb"], P.tiles["msc"]
    psA, psB = P.tiles["psA"], P.tiles["psB"]
    ps_l = [psA, psB]

    def vedge():
        P.wait("vector", [("v", P.cnt["v"])])

    P.wait("tensor", list(xf_ready))
    exp_done = []
    wsel_done = []
    for t in range(NT):
        par = t % 2
        pl = ps_l[par]
        if t >= 2:
            P.wait("tensor", [("s", exp_done[t - 2])])
        for dk in range(NDK):
            ic = [("mm", 1)] if dk == NDK - 1 else []
            P.op("tensor", lambda e, t=t, dk=dk, pl=pl: e.matmul(
                pl[:, 0:8], xf[:, dk * T + t * 128: dk * T + t * 128 + 128],
                rwT[:, dk * 8:(dk + 1) * 8],
                start=(dk == 0), stop=(dk == NDK - 1)), incs=ic)
        mm_t = P.cnt["mm"]
        exps = exps2[:, par * 8:par * 8 + 8]
        sw = [("mm", mm_t)]
        if t >= 2:
            sw.append(("v", wsel_done[t - 2]))
        P.op("scalar", lambda e, pl=pl, exps=exps: e.activation(exps, pl[:, 0:8], AF.Exp),
             waits=sw, incs=[("s", 1)])
        exp_done.append(P.cnt["s"])
        rsum, m1, m2, dn = (msc[:, 0:1], msc[:, 1:2], msc[:, 2:3], msc[:, 3:4])
        mask = msc[:, 8:16]
        P.op("vector", lambda e, exps=exps: e.tensor_reduce(rsum, exps, AX.X, ALU.add),
             waits=[("s", exp_done[t]), ("v", P.cnt["v"])], incs=[("v", 1)])
        vedge()
        P.op("vector", lambda e, exps=exps: e.tensor_reduce(m1, exps, AX.X, ALU.max),
             incs=[("v", 1)])
        vedge()
        P.op("vector", lambda e, exps=exps: e.tensor_single_scalar(
            mask, exps, m1, ALU.is_ge), incs=[("v", 1)])
        vedge()
        P.op("vector", lambda e, exps=exps: e.scalar_tensor_tensor(
            mask, mask, -1e30, exps, ALU.mult, ALU.add), incs=[("v", 1)])
        vedge()
        P.op("vector", lambda e: e.tensor_reduce(m2, mask, AX.X, ALU.max),
             incs=[("v", 1)])
        vedge()
        P.op("vector", lambda e: e.tensor_add(dn, m1, m2), incs=[("v", 1)])
        vedge()
        P.op("vector", lambda e: e.scalar_tensor_tensor(
            dn, rsum, 1e-8, dn, ALU.mult, ALU.add), incs=[("v", 1)])
        vedge()
        P.op("vector", lambda e: e.reciprocal(dn, dn), incs=[("v", 1)])
        vedge()
        P.op("vector", lambda e, exps=exps: e.tensor_single_scalar(
            mask, exps, m2, ALU.is_ge), incs=[("v", 1)])
        vedge()
        P.op("vector", lambda e: e.tensor_scalar_mul(dn, dn, 0.5), incs=[("v", 1)])
        vedge()
        P.op("vector", lambda e, exps=exps, t=t: e.tensor_single_scalar(
            exps, exps, dn, ALU.mult), incs=[("v", 1)])
        vedge()
        P.op("vector", lambda e, exps=exps, t=t: e.tensor_mul(
            w_tm[:, t * 8:(t + 1) * 8], exps, mask), incs=[("v", 1)])
        wsel_done.append(P.cnt["v"])
    v_wtm = P.cnt["v"]
    id32 = P.tiles["id32_sb"]
    P.wait("tensor", [("v", v_wtm)])
    tr_evac = []
    for t in range(NT):
        par = t % 2
        if t >= 2:
            P.wait("tensor", [("v", tr_evac[t - 2])])
        mm_t = P.op("tensor", lambda e, t=t, par=par: e.transpose(
            psB[0:8, par * 128:par * 128 + 128],
            w_tm[:, t * 8:(t + 1) * 8], id32[:]), incs=[("mm", 1)])[0]
        P.op("vector", lambda e, t=t, par=par: e.tensor_copy(
            w_fm[:, t * 128:(t + 1) * 128], psB[0:8, par * 128:par * 128 + 128]),
            waits=[("mm", mm_t)], incs=[("v", 1)])
        tr_evac.append(P.cnt["v"])
    return P.cnt["v"]



def alloc_l1attn(P):
    for nm in ("wbufA", "wbufB"):
        if nm not in P.tiles:
            P.sbuf(nm, (128, NDK * D), BF16)
    for nm in ("wbufC", "wbufD"):
        if nm not in P.tiles:
            P.sbuf(nm, (128, NDK * D), BF16)
    xln = P.tiles["xln_sb"]
    P.tiles["ekv_sb"] = xln[:, D:3 * D]
    P.tiles["qf_sb"] = xln[:, 3 * D:4 * D]
    P.tiles["af_sb"] = xln[:, 4 * D:5 * D]
    P.sbuf("ctx_sb", (128, 8 * 264), F32)
    P.sbuf("ctxn_sb", (128, 8 * 256), BF16)
    P.tiles["qt_sb"] = P.tiles["ctx_sb"][:, 0:D]
    P.tiles["arow_sb"] = xln[0:1, 11 * D:13 * D]
    P.tiles["bdrow_sb"] = xln[0:1, 13 * D:14 * D]
    P.sbuf("g1_sb", (128, 2 * NDK), F32)
    P.sbuf("g2_sb", (128, 2 * NDK), F32)
    P.sbuf("bf1_sb", (128, 32), F32)
    P.sbuf("asc_sb", (128, 8), F32)
    P.sbuf("onecol_sb", (128, 1), BF16)


ATTN_STOP = None


def emit_l1attn(P, aps, x_ready):
    xf, x_sb, xln = P.tiles["xf_sb"], P.tiles["x_sb"], P.tiles["xln_sb"]
    wA, wB = P.tiles["wbufA"], P.tiles["wbufB"]
    wC, wD = P.tiles["wbufC"], P.tiles["wbufD"]
    ekv, ctx, ctxn = P.tiles["ekv_sb"], P.tiles["ctx_sb"], P.tiles["ctxn_sb"]
    qt, qf, af = P.tiles["qt_sb"], P.tiles["qf_sb"], P.tiles["af_sb"]
    arow, bdrow = P.tiles["arow_sb"], P.tiles["bdrow_sb"]
    g1, g2, bf1, asc = (P.tiles["g1_sb"], P.tiles["g2_sb"], P.tiles["bf1_sb"],
                        P.tiles["asc_sb"])
    ps0, ps1, ps2, ps3 = (P.tiles["ps0"], P.tiles["ps1"], P.tiles["ps2"],
                          P.tiles["ps3"])
    psA, psB = P.tiles["psA"], P.tiles["psB"]
    onesbf = P.tiles["onesbf_sb"]
    onecol = P.tiles["onecol_sb"]
    ident = P.tiles["ident_sb"]
    SC4 = 0.25
    P.op("vector", lambda e: e.memset(onecol[:], 1.0),
         waits=[("v", P.cnt["v"])], incs=[("v", 1)])
    P.op("vector", lambda e: e.memset(ctx[:], 0.0), incs=[("v", 1)])
    v_one = P.cnt["v"]

    d_xo = P.op("sync", lambda e: e.dma_start(
        out=aps["x_old"].rearrange("(t p) d -> p t d", p=128),
        in_=x_sb[:].rearrange("p (t d) -> p t d", t=NT)),
        waits=list(x_ready), incs=[("dxold", 16)])[0]
    v_ou, s_ou = emit_ln(P, x_ready=list(x_ready) + [("dxold", d_xo)], inplace=True)
    x_ready = [("v", v_ou), ("s", s_ou)]

    def wload(dst, key, sem):
        return P.op("gpsimd", lambda e, dst=dst, key=key: e.dma_start(
            out=dst[:].rearrange("p (k c) -> p k c", k=NDK),
            in_=aps[key].rearrange("(k p) c -> p k c", p=128)),
            incs=[(sem, 16)])[0]

    d_wk = wload(wA, "L1_attn_wk", "dat_a")
    d_wv = wload(wB, "L1_attn_wv", "dat_a")
    d_small = P.cnt["dat_a"]
    for (dst, key) in ((g1[:, 0:NDK], "a_ln1g"), (g1[:, NDK:2 * NDK], "a_ln1b"),
                       (g2[:, 0:NDK], "a_ln2g"), (g2[:, NDK:2 * NDK], "a_ln2b"),
                       (bf1[:], "a_bff1")):
        P.op("sync", lambda e, dst=dst, key=key: e.dma_start(
            out=dst, in_=aps[key]), incs=[("dat_f", 16)])
    d_f32 = P.cnt["dat_f"]

    def load_bias_rows(sem, wait):
        P.op("gpsimd", lambda e: e.dma_start(
            out=arow[:, 0:D], in_=aps["L1_attn_bo"].rearrange("(a c) -> a c", a=1)),
            waits=wait, incs=[(sem, 16)])
        P.op("gpsimd", lambda e: e.dma_start(
            out=arow[:, D:2 * D], in_=aps["L1_attn_b_ff2"].rearrange("(a c) -> a c", a=1)),
            incs=[(sem, 16)])
        P.op("gpsimd", lambda e: e.dma_start(
            out=bdrow[:, :], in_=aps["L1_attn_b_d"].rearrange("(a c) -> a c", a=1)),
            incs=[(sem, 16)])
        return P.cnt[sem]

    v1, s1 = emit_ln(P, x_ready=x_ready, gain=(g1[:, 0:NDK], g1[:, NDK:2 * NDK]),
                     gain_ready=[("dat_f", d_f32)])
    d_rows = load_bias_rows("dat_r", [("v", v1), ("s", s1)])

    P.wait("tensor", [("v", v1), ("s", s1), ("dat_a", d_small), ("v", v_one), ("dat_r", d_rows)])
    kv_evac = []
    ctxadd_v = []
    last_k_evac = 0
    last_v_evac = 0
    for tc in range(NT):
        for dh in range(2):
            if last_k_evac:
                P.wait("tensor", [("s", last_k_evac)])
            for dk in range(NDK):
                ic = [("mm", 1)] if dk == NDK - 1 else []
                P.op("tensor", lambda e, tc=tc, dh=dh, dk=dk: e.matmul(
                    ps0[:], xf[:, dk * T + tc * 128: dk * T + tc * 128 + 128],
                    wA[:, dk * D + dh * 512: dk * D + dh * 512 + 512],
                    start=(dk == 0), stop=(dk == NDK - 1)), incs=ic)
            mmk = P.cnt["mm"]
            P.op("scalar", lambda e, dh=dh: e.activation(
                ekv[:, dh * 512:dh * 512 + 512], ps0[:], AF.Exp, scale=SC4),
                waits=[("mm", mmk)], incs=[("s", 1)])
            last_k_evac = P.cnt["s"]
            if last_v_evac:
                P.wait("tensor", [("s", last_v_evac)])
            for dk in range(NDK):
                ic = [("mm", 1)] if dk == NDK - 1 else []
                P.op("tensor", lambda e, tc=tc, dh=dh, dk=dk: e.matmul(
                    ps1[:], xf[:, dk * T + tc * 128: dk * T + tc * 128 + 128],
                    wB[:, dk * D + dh * 512: dk * D + dh * 512 + 512],
                    start=(dk == 0), stop=(dk == NDK - 1)), incs=ic)
            mmv = P.cnt["mm"]
            P.op("scalar", lambda e, dh=dh: e.activation(
                ekv[:, D + dh * 512:D + dh * 512 + 512], ps1[:], AF.Copy),
                waits=[("mm", mmv)], incs=[("s", 1)])
            last_v_evac = P.cnt["s"]
        kv_evac.append(P.cnt["s"])
        P.wait("tensor", [("s", kv_evac[tc])])
        for u in range(8):
            h, hh = u // 2, u % 2
            pp = [ps2, ps3][u % 2]
            if tc * 8 + u >= 2:
                P.wait("tensor", [("v", ctxadd_v[tc * 8 + u - 2])])
            P.op("tensor", lambda e, h=h, hh=hh, pp=pp: e.matmul(
                pp[:, 0:256], ekv[:, h * 256 + hh * 128: h * 256 + hh * 128 + 128],
                ekv[:, D + h * 256: D + h * 256 + 256], start=True, stop=True))
            P.op("tensor", lambda e, h=h, hh=hh, pp=pp: e.matmul(
                pp[:, 256:257], ekv[:, h * 256 + hh * 128: h * 256 + hh * 128 + 128],
                onecol[:], start=True, stop=True), incs=[("mm", 1)])
            mmc = P.cnt["mm"]
            P.op("vector", lambda e, u=u, pp=pp, tc=tc: e.tensor_tensor(
                ctx[:, u * 264:u * 264 + 257], ctx[:, u * 264:u * 264 + 257],
                pp[:, 0:257], ALU.add) if tc > 0 else e.tensor_copy(
                ctx[:, u * 264:u * 264 + 257], pp[:, 0:257]),
                waits=[("mm", mmc), ("v", P.cnt["v"])], incs=[("v", 1)])
            ctxadd_v.append(P.cnt["v"])
    v_ctx = P.cnt["v"]
    d_cp = P.op("sync", lambda e: e.dma_start(out=aps["ctx_part"], in_=ctx[:]),
                waits=[("v", v_ctx)], incs=[("dctx", 16)])[0]
    P.op("gpsimd", lambda e: e.collective_compute(
        "AllReduce", ALU.add, replica_groups=GROUPS,
        ins=[aps["ctx_part"].opt()], outs=[aps["ctx_sum"].opt()]),
        waits=[("dctx", d_cp)], incs=[("cc", 1)])
    cc_ctx = P.cnt["cc"]
    d_cl = P.op("sync", lambda e: e.dma_start(out=ctx[:], in_=aps["ctx_sum"]),
                waits=[("cc", cc_ctx)], incs=[("dctx", 16)])[0]
    P.wait("vector", [("dctx", d_cl)])
    for u in range(8):
        P.op("vector", lambda e, u=u: e.reciprocal(
            asc[:, u:u + 1], ctx[:, u * 264 + 256:u * 264 + 257]),
            waits=[("v", P.cnt["v"])], incs=[("v", 1)])
        P.wait("vector", [("v", P.cnt["v"])])
        P.op("vector", lambda e, u=u: e.tensor_single_scalar(
            ctxn[:, u * 256:(u + 1) * 256], ctx[:, u * 264:u * 264 + 256],
            asc[:, u:u + 1], ALU.mult), incs=[("v", 1)])
        P.wait("vector", [("v", P.cnt["v"])])
    v_ctxn = P.cnt["v"]

    d_wq = wload(wC, "L1_attn_wq", "dat_b")
    d_wo = wload(wD, "L1_attn_wo", "dat_b")
    d_b = P.cnt["dat_b"]

    P.wait("tensor", [("v", v_ctxn), ("dat_b", d_b)])
    q_evac = []
    pb_v = []
    for tc in range(NT):
        if tc >= 1:
            P.wait("tensor", [("v", pb_v[-1])])
        for dh in range(2):
            pp = [ps0, ps1][dh]
            for dk in range(NDK):
                ic = [("mm", 1)] if dk == NDK - 1 else []
                P.op("tensor", lambda e, tc=tc, dh=dh, dk=dk, pp=pp: e.matmul(
                    pp[:], xf[:, dk * T + tc * 128: dk * T + tc * 128 + 128],
                    wC[:, dk * D + dh * 512: dk * D + dh * 512 + 512],
                    start=(dk == 0), stop=(dk == NDK - 1)), incs=ic)
            mmq = P.cnt["mm"]
            sw = [("mm", mmq)]
            if tc >= 1:
                sw.append(("v", pb_v[-1]))
            P.op("scalar", lambda e, dh=dh, pp=pp: e.activation(
                qt[:, dh * 512:dh * 512 + 512], pp[:], AF.Copy),
                waits=sw, incs=[("s", 1)])
        s_q = P.cnt["s"]
        P.wait("vector", [("s", s_q)])
        eq = xln[:, 0:D]
        for h in range(4):
            P.op("vector", lambda e, h=h: e.tensor_reduce(
                asc[:, h:h + 1], qt[:, h * 256:(h + 1) * 256], AX.X, ALU.max),
                waits=[("v", P.cnt["v"])], incs=[("v", 1)])
            P.wait("vector", [("v", P.cnt["v"])])
            P.op("vector", lambda e, h=h: e.tensor_single_scalar(
                asc[:, 4 + h:5 + h], asc[:, h:h + 1], -SC4, ALU.mult), incs=[("v", 1)])
            P.wait("vector", [("v", P.cnt["v"])])
        v_m = P.cnt["v"]
        P.wait("scalar", [("v", v_m)])
        mmprev = P.cnt["mm"]
        for h in range(4):
            P.op("scalar", lambda e, h=h: e.activation(
                eq[:, h * 256:(h + 1) * 256], qt[:, h * 256:(h + 1) * 256], AF.Exp,
                scale=SC4, bias=asc[:, 4 + h:5 + h],
                accum_out=asc[:, h:h + 1]),
                waits=([("mm", mmprev)] if h == 0 else []), incs=[("s", 1)])
        s_eq = P.cnt["s"]
        P.wait("vector", [("s", s_eq)])
        for h in range(4):
            P.op("vector", lambda e, h=h: e.reciprocal(
                asc[:, 4 + h:5 + h], asc[:, h:h + 1]),
                waits=[("v", P.cnt["v"])], incs=[("v", 1)])
            P.wait("vector", [("v", P.cnt["v"])])
            P.op("vector", lambda e, h=h: e.tensor_single_scalar(
                eq[:, h * 256:(h + 1) * 256], eq[:, h * 256:(h + 1) * 256],
                asc[:, 4 + h:5 + h], ALU.mult), incs=[("v", 1)])
            P.wait("vector", [("v", P.cnt["v"])])
        v_eq = P.cnt["v"]
        P.wait("tensor", [("v", v_eq)])
        qf_ev = []
        for dk in range(NDK):
            par = dk % 2
            pt = [P.tiles["pt0"], P.tiles["pt1"]][par]
            if dk >= 2:
                P.wait("tensor", [("v", qf_ev[dk - 2])])
            mmt = P.op("tensor", lambda e, dk=dk, pt=pt: e.transpose(
                pt[:, 0:128], eq[:, dk * 128:(dk + 1) * 128], ident[:]),
                incs=[("mm", 1)])[0]
            P.op("vector", lambda e, dk=dk, pt=pt: e.tensor_copy(
                qf[:, dk * 128:(dk + 1) * 128], pt[:, 0:128]),
                waits=[("mm", mmt), ("v", P.cnt["v"])], incs=[("v", 1)])
            qf_ev.append(P.cnt["v"])
        v_qf = P.cnt["v"]
        P.wait("tensor", [("v", v_qf)])
        af_ev = []
        for half in range(2):
            pp = [ps2, ps3][half]
            if tc >= 1 and half == 0:
                P.wait("tensor", [("v", pb_v[-1])])
            for j in range(4):
                a = half * 4 + j
                h, eh = a // 2, a % 2
                for hh in range(2):
                    ic = [("mm", 1)] if hh == 1 else []
                    P.op("tensor", lambda e, h=h, eh=eh, hh=hh, pp=pp, j=j: e.matmul(
                        pp[:, j * 128:(j + 1) * 128],
                        ctxn[:, (h * 2 + hh) * 256 + eh * 128: (h * 2 + hh) * 256 + eh * 128 + 128],
                        qf[:, (h * 2 + hh) * 128: (h * 2 + hh) * 128 + 128],
                        start=(hh == 0), stop=(hh == 1)), incs=ic)
            mmaf = P.cnt["mm"]
            P.op("vector", lambda e, half=half, pp=pp: e.tensor_copy(
                af[:, half * 512:(half + 1) * 512], pp[:, 0:512]),
                waits=[("mm", mmaf), ("v", P.cnt["v"])], incs=[("v", 1)])
            af_ev.append(P.cnt["v"])
        P.wait("tensor", [("v", af_ev[-1])])
        for dh in range(2):
            pp = [ps0, ps1][dh]
            for a in range(NDK):
                P.op("tensor", lambda e, a=a, dh=dh, pp=pp: e.matmul(
                    pp[:], af[:, a * 128:(a + 1) * 128],
                    wD[:, a * D + dh * 512: a * D + dh * 512 + 512],
                    start=(a == 0), stop=False))
            P.op("tensor", lambda e, dh=dh, pp=pp: e.matmul(
                pp[:], onesbf[:], arow[:, dh * 512:dh * 512 + 512],
                start=False, stop=True), incs=[("mm", 1)])
            mmo = P.cnt["mm"]
            xs = x_sb[:, tc * D + dh * 512: tc * D + dh * 512 + 512]
            P.op("vector", lambda e, xs=xs, pp=pp: e.tensor_add(xs, xs, pp[:]),
                 waits=[("mm", mmo), ("v", P.cnt["v"])], incs=[("v", 1)])
        pb_v.append(P.cnt["v"])

    if ATTN_STOP == "wo":
        return
    x_ready2 = [("v", P.cnt["v"]), ("s", P.cnt["s"])]
    v2, s2 = emit_ln(P, x_ready=x_ready2, gain=(g2[:, 0:NDK], g2[:, NDK:2 * NDK]),
                     gain_ready=[("dat_f", d_f32)])
    d_rows2 = load_bias_rows("dat_r2", [("v", v2), ("s", s2)])
    h = xln
    wf_sem = ["dff0", "dff1"]
    W1r = aps["L1_attn_w_ff1"].rearrange("(dk p) (vf f) -> vf p dk f", p=128, f=512)
    W2r = aps["L1_attn_w_ff2"].rearrange("(vf fk p) d -> vf p fk d", p=128, fk=4)
    ff1_mm = []
    ff2_mm = []
    gelu_s = []
    yst_v = []
    hev = []
    P.wait("tensor", [("v", v2), ("s", s2)])
    for vf in range(8):
        par = vf % 2
        wts = [("mm", ff1_mm[vf - 2])] if vf >= 2 else [("mm", P.cnt["mm"])]
        P.op("gpsimd", lambda e, vf=vf, par=par: e.dma_start(
            out=[wA, wB][par][:, 0:NDK * 512].rearrange("p (dk f) -> p dk f", f=512),
            in_=W1r[vf]), waits=wts, incs=[(wf_sem[par], 16)])
        w1r = P.cnt[wf_sem[par]]
        wts2 = [("mm", ff2_mm[vf - 2])] if vf >= 2 else [("mm", P.cnt["mm"])]
        P.op("gpsimd", lambda e, vf=vf, par=par: e.dma_start(
            out=[wC, wD][par][:, 0:4 * 1024].rearrange("p (fk d) -> p fk d", d=1024),
            in_=W2r[vf]), waits=wts2, incs=[(wf_sem[par] + "b", 16)])
        w2r = P.cnt[wf_sem[par] + "b"]
        P.wait("tensor", [(wf_sem[par], w1r)])
        if vf >= 1:
            P.wait("scalar", [("mm", ff2_mm[vf - 1])])
        for fm in range(4):
            for tg in range(4):
                gg = len(gelu_s)
                hp = [ps0, ps1][gg % 2]
                if gg >= 2:
                    P.wait("tensor", [("s", gelu_s[gg - 2])])
                for dk in range(NDK):
                    ic = [("mm", 1)] if dk == NDK - 1 else []
                    P.op("tensor", lambda e, fm=fm, tg=tg, dk=dk, par=par, hp=hp: e.matmul(
                        hp[:], [wA, wB][par][:, dk * 512 + fm * 128: dk * 512 + fm * 128 + 128],
                        xf[:, dk * T + tg * 512: dk * T + tg * 512 + 512],
                        start=(dk == 0), stop=(dk == NDK - 1)), incs=ic)
                mmh = P.cnt["mm"]
                jcol = vf * 4 + fm
                P.op("scalar", lambda e, fm=fm, tg=tg, hp=hp, jcol=jcol: e.activation(
                    h[:, fm * T + tg * 512: fm * T + tg * 512 + 512], hp[:],
                    AF.Gelu_apprx_tanh, bias=bf1[:, jcol:jcol + 1]),
                    waits=[("mm", mmh)], incs=[("s", 1)])
                gelu_s.append(P.cnt["s"])
        ff1_mm.append(P.cnt["mm"])
        s_h = P.cnt["s"]
        P.wait("tensor", [(wf_sem[par] + "b", w2r), ("s", s_h)])
        if vf == 0:
            P.wait("tensor", [("dat_r2", d_rows2)])
        for tc in range(NT):
            for dh in range(2):
                uu = len(yst_v)
                yp = [ps2, ps3][uu % 2]
                if uu >= 2:
                    P.wait("tensor", [("v", yst_v[uu - 2])])
                for fk in range(4):
                    last = (fk == 3 and vf > 0)
                    ic = [("mm", 1)] if last else []
                    P.op("tensor", lambda e, tc=tc, dh=dh, fk=fk, par=par, yp=yp, last=last: e.matmul(
                        yp[:], h[:, fk * T + tc * 128: fk * T + tc * 128 + 128],
                        [wC, wD][par][:, fk * 1024 + dh * 512: fk * 1024 + dh * 512 + 512],
                        start=(fk == 0), stop=last), incs=ic)
                if vf == 0:
                    P.op("tensor", lambda e, dh=dh, yp=yp: e.matmul(
                        yp[:], onesbf[:], arow[:, D + dh * 512: D + dh * 512 + 512],
                        start=False, stop=True), incs=[("mm", 1)])
                mmy = P.cnt["mm"]
                xs = x_sb[:, tc * D + dh * 512: tc * D + dh * 512 + 512]
                P.op("vector", lambda e, xs=xs, yp=yp: e.tensor_add(xs, xs, yp[:]),
                     waits=[("mm", mmy), ("v", P.cnt["v"])], incs=[("v", 1)])
                yst_v.append(P.cnt["v"])
        ff2_mm.append(P.cnt["mm"])

    if ATTN_STOP == "ff":
        return
    x_ready3 = [("v", P.cnt["v"]), ("s", P.cnt["s"])]
    v3, s3 = emit_ln(P, x_ready=x_ready3, identity=True)
    d_rows3 = load_bias_rows("dat_r3", [("v", v3), ("s", s3)])
    d_wd = P.op("gpsimd", lambda e: e.dma_start(
        out=wA[:].rearrange("p (k c) -> p k c", k=NDK),
        in_=aps["L1_attn_w_d"].rearrange("(k p) c -> p k c", p=128)),
        waits=[("mm", P.cnt["mm"])], incs=[("dat_d", 16)])[0]
    P.wait("tensor", [("v", v3), ("s", s3), ("dat_d", d_wd), ("dat_r3", d_rows3)])
    wd_s = []
    xob = P.tiles["ctx_sb"][:, 0:1024]
    xor_ap = aps["x_old"].rearrange("(t p) d -> p t d", p=128)
    ox_sem = ["dox0", "dox1"]
    for tc in range(NT):
        for dh in range(2):
            uu = tc * 2 + dh
            pp = [ps0, ps1][uu % 2]
            par = uu % 2
            wts = [("v", wd_s[uu - 2])] if uu >= 2 else [("v", P.cnt["v"])]
            P.op("sync", lambda e, tc=tc, dh=dh, par=par: e.dma_start(
                out=xob[:, par * 512:par * 512 + 512],
                in_=xor_ap[:, tc, dh * 512:dh * 512 + 512]),
                waits=wts, incs=[(ox_sem[par], 16)])
            d_ox = P.cnt[ox_sem[par]]
            if uu >= 2:
                P.wait("tensor", [("v", wd_s[uu - 2])])
            for dk in range(NDK):
                P.op("tensor", lambda e, tc=tc, dh=dh, dk=dk, pp=pp: e.matmul(
                    pp[:], xf[:, dk * T + tc * 128: dk * T + tc * 128 + 128],
                    wA[:, dk * D + dh * 512: dk * D + dh * 512 + 512],
                    start=(dk == 0), stop=False))
            P.op("tensor", lambda e, dh=dh, pp=pp: e.matmul(
                pp[:], onesbf[:], bdrow[:, dh * 512:dh * 512 + 512],
                start=False, stop=True), incs=[("mm", 1)])
            mmw = P.cnt["mm"]
            xs = x_sb[:, tc * D + dh * 512: tc * D + dh * 512 + 512]
            P.op("vector", lambda e, xs=xs, pp=pp, par=par: e.tensor_tensor(
                xs, xob[:, par * 512:par * 512 + 512], pp[:], ALU.add),
                waits=[("mm", mmw), (ox_sem[par], d_ox), ("v", P.cnt["v"])],
                incs=[("v", 1)])
            wd_s.append(P.cnt["v"])


def emit_moe_bias(P, router_v_done):
    w_fm, b2, x_sb = P.tiles["w_fm"], P.tiles["b2_sb"], P.tiles["x_sb"]
    psA, psB = P.tiles["psA"], P.tiles["psB"]
    ps = [psA, psB]
    P.wait("tensor", [("v", router_v_done)])
    add_done = []
    u = 0
    for tc in range(NT):
        for dh in range(2):
            par = u % 2
            if u >= 2:
                P.wait("tensor", [("v", add_done[u - 2])])
            mm_u = P.op("tensor", lambda e, tc=tc, dh=dh, par=par: e.matmul(
                ps[par][:], w_fm[:, tc * 128:(tc + 1) * 128],
                b2[:, dh * 512:(dh + 1) * 512], start=True, stop=True),
                incs=[("mm", 1)])[0]
            xs = x_sb[:, tc * D + dh * 512: tc * D + dh * 512 + 512]
            P.op("vector", lambda e, xs=xs, par=par: e.tensor_add(xs, xs, ps[par][:]),
                 waits=[("mm", mm_u), ("v", P.cnt["v"])], incs=[("v", 1)])
            add_done.append(P.cnt["v"])
            u += 1


def emit_moe_experts(P, W1_ap, W2_ap, xf_ready):
    xf, x_sb = P.tiles["xf_sb"], P.tiles["x_sb"]
    h = P.tiles["xln_sb"]
    b1f, w_tm = P.tiles["b1f"], P.tiles["w_tm"]
    ps_h = [P.tiles["ps0"], P.tiles["ps1"]]
    ps_y = [P.tiles["ps2"], P.tiles["ps3"]]
    w1b = [P.tiles["w1_buf0"], P.tiles["w1_buf1"]]
    w2b = [P.tiles["w2_buf0"], P.tiles["w2_buf1"]]
    W1r = W1_ap.rearrange("e (dk p) (vf f) -> e vf p dk f", p=128, f=FV)
    W2r = W2_ap.rearrange("e (vf fk p) d -> e vf p fk d", p=128, fk=NDK)

    P.wait("tensor", list(xf_ready))
    w1_sem = ["dw1a", "dw1b"]
    w2_sem = ["dw2a", "dw2b"]
    w1_mm_done = []
    w2_mm_done = []
    hevac_done = []
    gelu_done = []
    ystt_done = []
    ghp = 0
    gyp = 0
    for v in range(NV):
        e_idx, vf = v // 4, v % 4
        par = v % 2
        wts = [("mm", w1_mm_done[v - 2])] if v >= 2 else []
        P.op("gpsimd", lambda e, e_idx=e_idx, vf=vf, par=par: e.dma_start(
            out=w1b[par][:].rearrange("p (dk f) -> p dk f", f=FV),
            in_=W1r[e_idx, vf]),
            waits=wts, incs=[(w1_sem[par], 16)])
        w1_rdy = P.cnt[w1_sem[par]]
        wts2 = [("mm", w2_mm_done[v - 2])] if v >= 2 else []
        P.op("gpsimd", lambda e, e_idx=e_idx, vf=vf, par=par: e.dma_start(
            out=w2b[par][:].rearrange("p (fk d) -> p fk d", d=D),
            in_=W2r[e_idx, vf]),
            waits=wts2, incs=[(w2_sem[par], 16)])
        w2_rdy = P.cnt[w2_sem[par]]

        P.wait("tensor", [(w1_sem[par], w1_rdy)])
        if v >= 1:
            P.wait("scalar", [("mm", w2_mm_done[v - 1])])
        for fm in range(NDK):
            for tg in range(4):
                hpar = ghp % 2
                if ghp >= 2:
                    P.wait("tensor", [("s", gelu_done[ghp - 2])])
                for dk in range(NDK):
                    ic = [("mm", 1)] if dk == NDK - 1 else []
                    P.op("tensor", lambda e, fm=fm, tg=tg, dk=dk, par=par, hpar=hpar: e.matmul(
                        ps_h[hpar][:],
                        w1b[par][:, dk * FV + fm * 128: dk * FV + fm * 128 + 128],
                        xf[:, dk * T + tg * 512: dk * T + tg * 512 + 512],
                        start=(dk == 0), stop=(dk == NDK - 1)), incs=ic)
                mm_g = P.cnt["mm"]
                j8 = (vf * 8 + fm) * 8 + e_idx
                b1col = b1f[:, j8:j8 + 1]
                P.op("scalar", lambda e, fm=fm, tg=tg, hpar=hpar, b1col=b1col: e.activation(
                    h[:, fm * T + tg * 512: fm * T + tg * 512 + 512],
                    ps_h[hpar][:], AF.Gelu_apprx_tanh, bias=b1col),
                    waits=[("mm", mm_g)], incs=[("s", 1)])
                gelu_done.append(P.cnt["s"])
                ghp += 1
        w1_mm_done.append(P.cnt["mm"])
        hevac_done.append(P.cnt["s"])

        P.wait("tensor", [(w2_sem[par], w2_rdy), ("s", hevac_done[v])])
        for tc in range(NT):
            for dh in range(2):
                ypar = gyp % 2
                if gyp >= 2:
                    P.wait("tensor", [("v", ystt_done[gyp - 2])])
                for fk in range(NDK):
                    ic = [("mm", 1)] if fk == NDK - 1 else []
                    P.op("tensor", lambda e, tc=tc, dh=dh, fk=fk, par=par, ypar=ypar: e.matmul(
                        ps_y[ypar][:],
                        h[:, fk * T + tc * 128: fk * T + tc * 128 + 128],
                        w2b[par][:, fk * FV + dh * 512: fk * FV + dh * 512 + 512],
                        start=(fk == 0), stop=(fk == NDK - 1)), incs=ic)
                mm_g = P.cnt["mm"]
                wcol = w_tm[:, tc * 8 + e_idx: tc * 8 + e_idx + 1]
                xs = x_sb[:, tc * D + dh * 512: tc * D + dh * 512 + 512]
                P.op("vector", lambda e, xs=xs, ypar=ypar, wcol=wcol: e.scalar_tensor_tensor(
                    xs, ps_y[ypar][:], wcol, xs, ALU.mult, ALU.add),
                    waits=[("mm", mm_g), ("v", P.cnt["v"])], incs=[("v", 1)])
                ystt_done.append(P.cnt["v"])
                gyp += 1
        w2_mm_done.append(P.cnt["mm"])



DI = 512
NDC = DI // 128
SEQ = 8192
NSC = SEQ // 512
DST = 16


def alloc_mamba(P):
    P.sbuf("winT_sb", (128, NDK * 1024), BF16)
    P.sbuf("opT_sb", (128, NDC * D), BF16)
    P.sbuf("xpwT_sb", (128, NDC * 96), BF16)
    P.sbuf("dtwT_sb", (64, DI), BF16)
    P.sbuf("basis_sb", (16, 16 * 128), BF16)
    P.sbuf("A_sb", (128, NDC * DST), F32)
    P.sbuf("cw_sb", (128, NDC * 4), F32)
    P.sbuf("cb_sb", (128, NDC), F32)
    P.sbuf("D_sb", (128, NDC), F32)
    P.sbuf("dtb_sb", (128, NDC), F32)
    P.sbuf("lnw_sb", (128, NDK), F32)
    P.sbuf("lnb_sb", (128, NDK), F32)
    for p in range(2):
        P.sbuf(f"xin_buf{p}", (128, NDK * 512), BF16)
    xf = P.tiles["xf_sb"]
    off = 0
    for (nm, cols) in (("xcin_sb", NDC * 515), ("zc_sb", NDC * 512),
                       ("xcv_sb", NDC * 512), ("dts_sb", NDC * 512),
                       ("dtu_sb", NDC * 512), ("yg_sb", NDC * 512),
                       ("scanA", 512), ("scanB", 512), ("states", 512),
                       ("scanA1", 512), ("scanB1", 512), ("states1", 512)):
        P.tiles[nm] = xf[:, off:off + cols]
        off += cols
    assert off <= NDK * T
    P.tiles["ytbuf"] = P.tiles["xcin_sb"][:, 0:512]
    P.tiles["ytbuf1"] = P.tiles["xcin_sb"][:, 515:1027]
    P.sbuf("cacc_sb", (128, 512), F32)
    P.sbuf("xdbl_tm", (128, 4 * 96), F32)
    P.sbuf("dtraw_fm", (64, 512), BF16)
    P.sbuf("Bfm_sb", (16, 512), BF16)
    P.sbuf("Cfm_sb", (16, 512), BF16)
    P.sbuf("h_sb", (128, NDC * DST), F32)
    P.sbuf("yacc_sb", (128, NDC * 512), F32)
    P.sbuf("sz_sb", (128, 512), F32)
    P.sbuf("mscr", (128, 2 * 512), F32)
    P.sbuf("m4buf", (128, 2048), F32)


MAMBA_STOP = None


def emit_mamba(P, aps, x_ready):
    xf, x_sb = P.tiles["xf_sb"], P.tiles["x_sb"]
    winT, opT = P.tiles["winT_sb"], P.tiles["opT_sb"]
    xpwT, dtwT = P.tiles["xpwT_sb"], P.tiles["dtwT_sb"]
    basis = P.tiles["basis_sb"]
    A_sb, cw, cb, D_sb, dtb = (P.tiles["A_sb"], P.tiles["cw_sb"], P.tiles["cb_sb"],
                               P.tiles["D_sb"], P.tiles["dtb_sb"])
    xcin, zc, xcv = P.tiles["xcin_sb"], P.tiles["zc_sb"], P.tiles["xcv_sb"]
    cacc = P.tiles["cacc_sb"]
    xdbl_tm, dtraw = P.tiles["xdbl_tm"], P.tiles["dtraw_fm"]
    Bfm, Cfm = P.tiles["Bfm_sb"], P.tiles["Cfm_sb"]
    dts, dtu = P.tiles["dts_sb"], P.tiles["dtu_sb"]
    scanA2 = [P.tiles["scanA"], P.tiles["scanA1"]]
    scanB2 = [P.tiles["scanB"], P.tiles["scanB1"]]
    states2 = [P.tiles["states"], P.tiles["states1"]]
    ytbuf2 = [P.tiles["ytbuf"], P.tiles["ytbuf1"]]
    h_sb, yacc, sz, yg = (P.tiles["h_sb"], P.tiles["yacc_sb"], P.tiles["sz_sb"],
                          P.tiles["yg_sb"])
    mscr = P.tiles["mscr"]
    xinb = [P.tiles["xin_buf0"], P.tiles["xin_buf1"]]
    ps0, ps1 = P.tiles["ps0"], P.tiles["ps1"]
    ps2, ps3 = P.tiles["ps2"], P.tiles["ps3"]
    psA, psB = P.tiles["psA"], P.tiles["psB"]
    id32 = P.tiles["id32_sb"]

    def vedge():
        P.wait("vector", [("v", P.cnt["v"])])

    if MAMBA_STOP == "now":
        return
    def gload(out_ap, in_ap):
        P.op("gpsimd", lambda e, out_ap=out_ap, in_ap=in_ap: e.dma_start(
            out=out_ap, in_=in_ap), incs=[("dmam_g", 16)])

    gload(winT[:].rearrange("p (k c) -> p k c", k=NDK),
          aps["m_winT"].rearrange("(k p) c -> p k c", p=128))
    gload(opT[:].rearrange("p (k c) -> p k c", k=NDC),
          aps["m_opT"].rearrange("(k p) c -> p k c", p=128))
    gload(xpwT[:].rearrange("p (k c) -> p k c", k=NDC),
          aps["m_xpwT"].rearrange("(k p) c -> p k c", p=128))
    gload(dtwT[:], aps["m_dtwT"])
    gload(basis[:], aps["basis16"])
    w_g = P.cnt["dmam_g"]
    for (dst, key) in ((A_sb, "m_Alog"), (cw, "m_convw"), (cb, "m_convb"),
                       (D_sb, "m_D"), (dtb, "m_dtb"),
                       (P.tiles["lnw_sb"], "m_lnw"), (P.tiles["lnb_sb"], "m_lnb")):
        P.op("sync", lambda e, dst=dst, key=key: e.dma_start(
            out=dst[:], in_=aps[key]), incs=[("dmam", 16)])
    w_f = P.cnt["dmam"]
    if MAMBA_STOP == "w":
        P.wait("tensor", [("dmam", w_f), ("dmam_g", w_g)])
        P.wait("vector", [("dmam", w_f), ("dmam_g", w_g)])
        P.wait("scalar", [("dmam", w_f), ("dmam_g", w_g)])
        return
    s_A = P.op("scalar", lambda e: e.activation(A_sb[:], A_sb[:], AF.Exp),
               waits=[("dmam", w_f)], incs=[("s", 1)])[0]
    P.op("vector", lambda e: e.tensor_scalar_mul(A_sb[:], A_sb[:], -1.0),
         waits=[("s", s_A), ("v", P.cnt["v"])], incs=[("v", 1)])
    if MAMBA_STOP == "A":
        return

    v_ln, s_ln = emit_ln(P, x_ready=x_ready, mamba_double=True,
                         gain=(P.tiles["lnw_sb"], P.tiles["lnb_sb"]),
                         gain_ready=[("dmam", w_f)])

    if MAMBA_STOP == "ln":
        return
    d_ag = P.op("sync", lambda e: e.dma_start(
        out=aps["ag_in"].rearrange("(k p) c -> p k c", p=128),
        in_=xf[:].rearrange("p (k c) -> p k c", k=NDK)),
        waits=[("v", v_ln), ("s", s_ln)], incs=[("dag", 16)])[0]
    P.op("gpsimd", lambda e: e.collective_compute(
        "AllGather", ALU.bypass, replica_groups=GROUPS,
        ins=[aps["ag_in"].opt()], outs=[aps["ag_out"].opt()]),
        waits=[("dag", d_ag), ("dmam_g", w_g)], incs=[("cc", 1)])
    cc_ag = P.cnt["cc"]

    if MAMBA_STOP == "ag":
        return
    xin_sem = ["dxi0", "dxi1"]
    evac_done = []
    conv_v = []
    xp_evac_v = []
    oc_evac = []
    xp_mm = []
    agr = aps["ag_out"].rearrange("(r k p) c -> r k p c", k=NDK, p=128)
    P.wait("sync", [("cc", cc_ag)])
    P.wait("tensor", [("dmam_g", w_g), ("dma", P.cnt["dma"])])
    for sc in range(NSC):
        par = sc % 2
        r0, off = (sc * 512) // 2048, (sc * 512) % 2048
        wts = [("s", evac_done[sc - 2])] if sc >= 2 else []
        P.op("sync", lambda e, par=par, r0=r0, off=off: e.dma_start(
            out=xinb[par][:].rearrange("p (k c) -> p k c", k=NDK),
            in_=agr[r0, :, :, off:off + 512].transpose([1, 0, 2])),
            waits=wts, incs=[(xin_sem[par], 16)])
        x_rdy = P.cnt[xin_sem[par]]
        P.wait("tensor", [(xin_sem[par], x_rdy)])
        xc3 = xcin[:].rearrange("p (dc c) -> p dc c", dc=NDC)
        if sc == 0:
            P.op("vector", lambda e, xc3=xc3: e.memset(xc3[:, :, 0:3], 0.0),
                 waits=[("v", P.cnt["v"]), ("dag", d_ag)], incs=[("v", 1)])
        else:
            P.op("vector", lambda e, xc3=xc3: e.tensor_copy(
                xc3[:, :, 0:3], xc3[:, :, 512:515]),
                waits=[("v", P.cnt["v"]), ("s", evac_done[sc - 1])], incs=[("v", 1)])
        v_halo = P.cnt["v"]
        for oc in range(8):
            g = sc * 8 + oc
            pp = [ps0, ps1][g % 2]
            if g >= 2:
                P.wait("tensor", [("s", oc_evac[g - 2])])
            for dmk in range(NDK):
                ic = [("mm", 1)] if dmk == NDK - 1 else []
                P.op("tensor", lambda e, oc=oc, dmk=dmk, par=par, pp=pp: e.matmul(
                    pp[:], winT[:, dmk * 1024 + oc * 128: dmk * 1024 + oc * 128 + 128],
                    xinb[par][:, dmk * 512:(dmk + 1) * 512],
                    start=(dmk == 0), stop=(dmk == NDK - 1)), incs=ic)
            mm_oc = P.cnt["mm"]
            if oc < 4:
                dst = xcin[:, oc * 515 + 3: oc * 515 + 515]
            else:
                dst = zc[:, (oc - 4) * 512: (oc - 4) * 512 + 512]
            sw = [("mm", mm_oc)]
            if oc < 4:
                sw.append(("v", v_halo))
            if sc >= 1 and oc >= 4:
                sw.append(("dzs", 32 * sc))
            P.op("scalar", lambda e, dst=dst, pp=pp: e.activation(dst, pp[:], AF.Copy),
                 waits=sw, incs=[("s", 1)])
            oc_evac.append(P.cnt["s"])
        evac_done.append(P.cnt["s"])
        P.op("scalar", lambda e, sc=sc: e.dma_start(
            out=aps["z_sp"][:, :, sc * 512:(sc + 1) * 512].transpose([1, 0, 2]),
            in_=zc[:].rearrange("p (dc c) -> p dc c", dc=NDC)),
            waits=[("s", P.cnt["s"])], incs=[("dzs", 16)])
        for dc in range(NDC):
            P.op("vector", lambda e, dc=dc: e.tensor_single_scalar(
                cacc[:], xcin[:, dc * 515 + 0: dc * 515 + 512],
                cw[:, dc * 4:dc * 4 + 1], ALU.mult),
                waits=[("s", P.cnt["s"]), ("v", P.cnt["v"])], incs=[("v", 1)])
            for j in range(1, 4):
                P.op("vector", lambda e, dc=dc, j=j: e.scalar_tensor_tensor(
                    cacc[:], xcin[:, dc * 515 + j: dc * 515 + j + 512],
                    cw[:, dc * 4 + j:dc * 4 + j + 1], cacc[:], ALU.mult, ALU.add),
                    waits=[("v", P.cnt["v"])], incs=[("v", 1)])
            v_c = P.cnt["v"]
            P.op("scalar", lambda e, dc=dc: e.activation(
                xcv[:, dc * 512:(dc + 1) * 512], cacc[:], AF.Silu,
                bias=cb[:, dc:dc + 1]),
                waits=[("v", v_c), ("dzs", 32 * sc + 16) if sc >= 1 else ("v", 0)],
                incs=[("s", 1)])
        s_conv = P.cnt["s"]
        conv_v.append(P.cnt["v"])
        P.op("scalar", lambda e, sc=sc: e.dma_start(
            out=aps["xcv_sp"][:, :, sc * 512:(sc + 1) * 512].transpose([1, 0, 2]),
            in_=xcv[:].rearrange("p (dc c) -> p dc c", dc=NDC)),
            waits=[("s", P.cnt["s"])], incs=[("dzs", 16)])
        P.wait("tensor", [("s", s_conv)])
        for tt in range(4):
            pp = [ps2, ps3][tt % 2]
            if sc * 4 + tt >= 2:
                P.wait("tensor", [("v", xp_evac_v[sc * 4 + tt - 2])])
            for dc in range(NDC):
                ic = [("mm", 1)] if dc == NDC - 1 else []
                P.op("tensor", lambda e, tt=tt, dc=dc, pp=pp: e.matmul(
                    pp[:, 0:96], xcv[:, dc * 512 + tt * 128: dc * 512 + tt * 128 + 128],
                    xpwT[:, dc * 96:(dc + 1) * 96],
                    start=(dc == 0), stop=(dc == NDC - 1)), incs=ic)
            mm_t = P.cnt["mm"]
            sl = mscr[:, (tt % 2) * 512:(tt % 2) * 512 + 96]
            P.op("vector", lambda e, sl=sl, pp=pp: e.tensor_copy(sl, pp[:, 0:96]),
                 waits=[("mm", mm_t), ("v", P.cnt["v"]),
                        ("dxdb", 16 * (sc * 4 + tt))],
                 incs=[("v", 1)])
            xp_evac_v.append(P.cnt["v"])
            P.op("sync", lambda e, sc=sc, tt=tt, sl=sl: e.dma_start(
                out=aps["xdbl_part"][sc * 512 + tt * 128: sc * 512 + tt * 128 + 128, :],
                in_=sl),
                waits=[("v", xp_evac_v[-1])], incs=[("dxdb", 16)])
    P.op("gpsimd", lambda e: e.collective_compute(
        "AllReduce", ALU.add, replica_groups=GROUPS,
        ins=[aps["xdbl_part"].opt()], outs=[aps["xdbl_sum"].opt()]),
        waits=[("dxdb", P.cnt["dxdb"])], incs=[("cc", 1)])
    cc_ar = P.cnt["cc"]
    P.wait("sync", [("cc", cc_ar)])
    P.wait("scalar", [("cc", cc_ar), ("dzs", P.cnt["dzs"])])
    if MAMBA_STOP == "ar":
        return

    P.op("vector", lambda e: e.memset(h_sb[:], 0.0),
         waits=[("v", P.cnt["v"])], incs=[("v", 1)])
    op_evac_v = []
    dt_s = []
    last_bc_mm = 0
    gate_v = []
    gate_s = []
    xdsum_r = aps["xdbl_sum"].rearrange("(t p) c -> p t c", p=128)
    for sc in range(NSC):
        wts = [("mm", last_bc_mm)] if sc >= 1 else [("cc", cc_ar)]
        d_xd = P.op("sync", lambda e, sc=sc: e.dma_start(
            out=xdbl_tm[:].rearrange("p (t c) -> p t c", t=4),
            in_=xdsum_r[:, sc * 4:sc * 4 + 4, :]),
            waits=wts, incs=[("dxd", 16)])[0]
        wts2 = [("v", gate_v[sc - 1]), ("s", gate_s[sc - 1])] if sc >= 1 else []
        d_xc = P.op("scalar", lambda e, sc=sc: e.dma_start(
            out=xcv[:].rearrange("p (dc c) -> p dc c", dc=NDC),
            in_=aps["xcv_sp"][:, :, sc * 512:(sc + 1) * 512].transpose([1, 0, 2])),
            waits=wts2, incs=[("dxc", 16)])[0]
        d_z = P.op("scalar", lambda e, sc=sc: e.dma_start(
            out=zc[:].rearrange("p (dc c) -> p dc c", dc=NDC),
            in_=aps["z_sp"][:, :, sc * 512:(sc + 1) * 512].transpose([1, 0, 2])),
            incs=[("dzr", 16)])[0]
        P.wait("tensor", [("dxd", d_xd), ("v", P.cnt["v"])])
        tr_mm = []
        tr_ev = []
        for tt in range(4):
            pp = [ps2, ps3][tt % 2]
            if tt >= 2:
                P.wait("tensor", [("v", tr_ev[tt - 2])])
            P.op("tensor", lambda e, tt=tt, pp=pp: e.transpose(
                pp[0:80, 0:128], xdbl_tm[:, tt * 96: tt * 96 + 80], id32[:]))
            mm1 = P.op("tensor", lambda e, tt=tt, pp=pp: e.transpose(
                pp[0:16, 128:256], xdbl_tm[:, tt * 96 + 80: tt * 96 + 96], id32[:]),
                incs=[("mm", 1)])[0]
            tr_mm.append(mm1)
            P.op("vector", lambda e, tt=tt, pp=pp: e.tensor_copy(
                dtraw[:, tt * 128:(tt + 1) * 128], pp[0:64, 0:128]),
                waits=[("mm", mm1), ("v", P.cnt["v"])], incs=[("v", 1)])
            P.op("vector", lambda e, tt=tt, pp=pp: e.tensor_copy(
                Bfm[:, tt * 128:(tt + 1) * 128], pp[64:80, 0:128]),
                waits=[("v", P.cnt["v"])], incs=[("v", 1)])
            P.op("vector", lambda e, tt=tt, pp=pp: e.tensor_copy(
                Cfm[:, tt * 128:(tt + 1) * 128], pp[0:16, 128:256]),
                waits=[("v", P.cnt["v"])], incs=[("v", 1)])
            tr_ev.append(P.cnt["v"])
        v_tr = P.cnt["v"]
        P.wait("tensor", [("v", v_tr)])
        for dc in range(NDC):
            pp = [ps0, ps1][dc % 2]
            if dc >= 2:
                P.wait("tensor", [("s", dt_s[-2])])
            mm_d = P.op("tensor", lambda e, dc=dc, pp=pp: e.matmul(
                pp[:], dtwT[:, dc * 128:(dc + 1) * 128], dtraw[:],
                start=True, stop=True), incs=[("mm", 1)])[0]
            t0 = mscr[:, 0:512]
            t1 = mscr[:, 512:1024]
            P.op("vector", lambda e, pp=pp, t0=t0, dc=dc: e.tensor_single_scalar(
                t0, pp[:], dtb[:, dc:dc + 1], ALU.add),
                waits=[("mm", mm_d), ("v", P.cnt["v"]), ("s", P.cnt["s"]),
                       ("dyp", P.cnt.get("dyp", 0))],
                incs=[("v", 1)])
            vedge()
            s_abs = P.op("scalar", lambda e, t0=t0, t1=t1: e.activation(t1, t0, AF.Abs),
                         waits=[("v", P.cnt["v"]), ("s", P.cnt["s"])], incs=[("s", 1)])[0]
            P.wait("scalar", [("s", s_abs)])
            s0 = P.op("scalar", lambda e, t1=t1: e.activation(t1, t1, AF.Exp, scale=-1.0),
                      incs=[("s", 1)])[0]
            P.wait("scalar", [("s", s0)])
            s1 = P.op("scalar", lambda e, t1=t1: e.activation(t1, t1, AF.Ln, bias=cst(P, "one")),
                      incs=[("s", 1)])[0]
            P.op("vector", lambda e, t0=t0: e.tensor_relu(t0, t0),
                 waits=[("v", P.cnt["v"]), ("s", s_abs)], incs=[("v", 1)])
            vedge()
            P.op("vector", lambda e, t0=t0, t1=t1, dc=dc: e.tensor_add(
                dts[:, dc * 512:(dc + 1) * 512], t0, t1),
                waits=[("s", s1)], incs=[("v", 1)])
            dt_s.append(P.cnt["s"])
        v_dt = P.cnt["v"]
        P.wait("vector", [("dxc", d_xc)])
        for dc in range(NDC):
            P.op("vector", lambda e, dc=dc: e.tensor_mul(
                dtu[:, dc * 512:(dc + 1) * 512], dts[:, dc * 512:(dc + 1) * 512],
                xcv[:, dc * 512:(dc + 1) * 512]),
                waits=[("v", P.cnt["v"])], incs=[("v", 1)])
            P.op("vector", lambda e, dc=dc: e.tensor_single_scalar(
                yacc[:, dc * 512:(dc + 1) * 512], xcv[:, dc * 512:(dc + 1) * 512],
                D_sb[:, dc:dc + 1], ALU.mult),
                waits=[("v", P.cnt["v"]), ("g", P.cnt["g"])], incs=[("v", 1)])
        v_du = P.cnt["v"]
        P.wait("tensor", [("v", v_du)])
        uu2 = 0
        exp_done = []
        scan_done = []
        ymul_done = []
        yadd_done = []
        for s_i in range(DST):
            P.wait("tensor", [("v", P.cnt["v"])])
            mmB = P.op("tensor", lambda e, s_i=s_i: e.matmul(
                psA[:], basis[:, s_i * 128:(s_i + 1) * 128], Bfm[:],
                start=True, stop=True), incs=[("mm", 1)])[0]
            mmC = P.op("tensor", lambda e, s_i=s_i: e.matmul(
                psB[:], basis[:, s_i * 128:(s_i + 1) * 128], Cfm[:],
                start=True, stop=True), incs=[("mm", 1)])[0]
            last_bc_mm = P.cnt["mm"]
            for dc in range(NDC):
                col = dc * DST + s_i
                par = uu2 % 2
                sA, sB, sS = scanA2[par], scanB2[par], states2[par]
                ew = [("s", P.cnt["s"])]
                if uu2 >= 2:
                    ew.append(("v", scan_done[uu2 - 2]))
                P.op("scalar", lambda e, dc=dc, col=col, sA=sA: e.activation(
                    sA[:], dts[:, dc * 512:(dc + 1) * 512], AF.Exp,
                    scale=A_sb[:, col:col + 1]), waits=ew, incs=[("s", 1)])
                exp_done.append(P.cnt["s"])
                bw = [("mm", mmB), ("v", P.cnt["v"])]
                P.op("vector", lambda e, dc=dc, sB=sB: e.tensor_mul(
                    sB[:], dtu[:, dc * 512:(dc + 1) * 512], psA[:]),
                    waits=bw, incs=[("v", 1)])
                vedge()
                sw = [("s", exp_done[uu2])]
                if uu2 >= 2:
                    sw.append(("v", ymul_done[uu2 - 2]))
                P.op("vector", lambda e, col=col, sA=sA, sB=sB, sS=sS: e.tensor_tensor_scan(
                    sS[:], sA[:], sB[:], h_sb[:, col:col + 1],
                    ALU.mult, ALU.add), waits=sw, incs=[("v", 1)])
                scan_done.append(P.cnt["v"])
                vedge()
                P.op("vector", lambda e, col=col, sS=sS: e.tensor_copy(
                    h_sb[:, col:col + 1], sS[:, 511:512]), incs=[("v", 1)])
                yt = ytbuf2[par]
                mw = [("mm", mmC)]
                if uu2 >= 2:
                    mw.append(("g", yadd_done[uu2 - 2]))
                P.op("vector", lambda e, sS=sS, yt=yt: e.tensor_mul(yt[:], sS[:], psB[:]),
                     waits=mw, incs=[("v", 1)])
                ymul_done.append(P.cnt["v"])
                P.op("gpsimd", lambda e, dc=dc, yt=yt: e.tensor_add(
                    yacc[:, dc * 512:(dc + 1) * 512],
                    yacc[:, dc * 512:(dc + 1) * 512], yt[:]),
                    waits=[("v", ymul_done[uu2]), ("g", P.cnt["g"])], incs=[("g", 1)])
                yadd_done.append(P.cnt["g"])
                uu2 += 1
        P.wait("vector", [("g", P.cnt["g"])])
        P.wait("scalar", [("g", P.cnt["g"])])
        for dc in range(NDC):
            P.op("scalar", lambda e, dc=dc: e.activation(
                sz[:], zc[:, dc * 512:(dc + 1) * 512], AF.Silu),
                waits=[("v", P.cnt["v"]), ("s", P.cnt["s"]), ("dzr", d_z)],
                incs=[("s", 1)])
            P.op("vector", lambda e, dc=dc: e.tensor_mul(
                yg[:, dc * 512:(dc + 1) * 512], yacc[:, dc * 512:(dc + 1) * 512],
                sz[:]), waits=[("s", P.cnt["s"]), ("v", P.cnt["v"])], incs=[("v", 1)])
        gate_v.append(P.cnt["v"])
        gate_s.append(P.cnt["s"])
        P.wait("tensor", [("v", gate_v[-1])])
        for tt in range(4):
            for dh in range(2):
                u = len(op_evac_v)
                pp = [ps2, ps3][u % 2]
                if u >= 2:
                    P.wait("tensor", [("v", op_evac_v[u - 2])])
                for dc in range(NDC):
                    ic = [("mm", 1)] if dc == NDC - 1 else []
                    P.op("tensor", lambda e, tt=tt, dh=dh, dc=dc, pp=pp: e.matmul(
                        pp[:], yg[:, dc * 512 + tt * 128: dc * 512 + tt * 128 + 128],
                        opT[:, dc * D + dh * 512: dc * D + dh * 512 + 512],
                        start=(dc == 0), stop=(dc == NDC - 1)), incs=ic)
                mm_o = P.cnt["mm"]
                sl = mscr[:, (u % 2) * 512:(u % 2) * 512 + 512]
                P.op("vector", lambda e, sl=sl, pp=pp: e.tensor_copy(sl, pp[:]),
                     waits=[("mm", mm_o), ("v", P.cnt["v"]),
                            ("dyp", 16 * u)],
                     incs=[("v", 1)])
                op_evac_v.append(P.cnt["v"])
                P.op("sync", lambda e, sc=sc, tt=tt, dh=dh, sl=sl: e.dma_start(
                    out=aps["yprt"][sc * 512 + tt * 128: sc * 512 + tt * 128 + 128,
                                    dh * 512:(dh + 1) * 512],
                    in_=sl), waits=[("v", op_evac_v[-1])], incs=[("dyp", 16)])
    if MAMBA_STOP == "rs":
        return
    P.op("gpsimd", lambda e: e.collective_compute(
        "ReduceScatter", ALU.add, replica_groups=GROUPS,
        ins=[aps["yprt"].opt()], outs=[aps["rs_out"].opt()]),
        waits=[("dyp", P.cnt["dyp"])], incs=[("cc", 1)])
    cc_rs = P.cnt["cc"]
    m4 = P.tiles["m4buf"]
    P.wait("sync", [("cc", cc_rs)])
    rsr = aps["rs_out"].rearrange("(t p) d -> p t d", p=128)
    for rnd in range(8):
        wts = [("v", P.cnt["v"])] if rnd else []
        d_r = P.op("sync", lambda e, rnd=rnd: e.dma_start(
            out=m4[:].rearrange("p (t d) -> p t d", t=2),
            in_=rsr[:, rnd * 2:rnd * 2 + 2, :]),
            waits=wts, incs=[("drs", 16)])[0]
        for j in range(2):
            t = rnd * 2 + j
            P.op("vector", lambda e, t=t, j=j: e.tensor_add(
                x_sb[:, t * D:(t + 1) * D], x_sb[:, t * D:(t + 1) * D],
                m4[:, j * D:(j + 1) * D]),
                waits=[("drs", d_r), ("v", P.cnt["v"])], incs=[("v", 1)])


def build_chain(parts, debug_dump=()):
    nc = bass.Bass("TRN2", target_bir_lowering=False)
    es = contextlib.ExitStack()
    inp = {}

    def di(name, shape, dtype=F32):
        if name not in inp:
            inp[name] = nc.dram_tensor(name, list(shape), dtype, kind="ExternalInput").ap()
        return inp[name]

    x_in = di("x", (T, D))
    ident = di("ident", (128, 128), BF16)
    onesbf = di("onesbf", (1, 128), BF16)
    ident32 = di("ident32", (128, 128), F32)
    xo = nc.dram_tensor("xo", [T, D], F32, kind="ExternalOutput").ap()

    P = Prog(nc, es)
    P.ident32_ap = ident32
    alloc_common(P)
    if "l1ffn" in parts:
        alloc_l1ffn(P)
    if "moe" in parts:
        alloc_moe(P)
    if "mamba" in parts:
        alloc_mamba(P)
    if "l1attn" in parts:
        alloc_l1attn(P)

    emit_load_consts(P, ident, onesbf)
    emit_load_x(P, x_in)
    x_ready = [("dma", P.cnt["dma"])]

    for part in parts:
        if part == "l1ffn":
            th = di("L1_ffn_theta", (D, D))
            tn = di("L1_ffn_theta_noise", (D, D))
            ph = di("L1_ffn_phi", (D, D))
            pn = di("L1_ffn_phi_noise", (D, D))
            emit_l1ffn_prep(P, th, tn, ph, pn)
            prep_s = P.cnt["s"]
            v_done, s_done = emit_ln(P, x_ready=x_ready)
            emit_l1ffn_main(P, xf_ready=[("v", v_done), ("s", max(s_done, prep_s))])
        elif part == "l1attn":
            aps = {}
            for nm in ("L1_attn_wq", "L1_attn_wk", "L1_attn_wv", "L1_attn_wo",
                       "L1_attn_w_d"):
                aps[nm] = di(nm, (D, D))
            aps["L1_attn_w_ff1"] = di("L1_attn_w_ff1", (D, 4 * D))
            aps["L1_attn_w_ff2"] = di("L1_attn_w_ff2", (4 * D, D))
            for nm in ("L1_attn_bo", "L1_attn_b_ff2", "L1_attn_b_d"):
                aps[nm] = di(nm, (D,))
            for nm in ("a_ln1g", "a_ln1b", "a_ln2g", "a_ln2b"):
                aps[nm] = di(nm, (128, NDK))
            aps["a_bff1"] = di("a_bff1", (128, 32))
            aps["x_old"] = nc.dram_tensor("x_old", [T, D], F32).ap()
            aps["ctx_part"] = nc.dram_tensor("ctx_part", [128, 8 * 264], F32).ap()
            aps["ctx_sum"] = nc.dram_tensor("ctx_sum", [128, 8 * 264], F32).ap()
            emit_l1attn(P, aps, x_ready)
        elif part == "mamba":
            aps = {}
            for (nm, shp, dt) in (
                    ("m_winT", (D, 1024), F32), ("m_opT", (DI, D), F32),
                    ("m_xpwT", (DI, 96), F32), ("m_dtwT", (64, DI), F32),
                    ("basis16", (16, 2048), BF16), ("m_Alog", (128, NDC * DST), F32),
                    ("m_convw", (128, NDC * 4), F32), ("m_convb", (128, NDC), F32),
                    ("m_D", (128, NDC), F32), ("m_dtb", (128, NDC), F32),
                    ("m_lnw", (128, NDK), F32), ("m_lnb", (128, NDK), F32)):
                aps[nm] = di(nm, shp, dt)
            for (nm, shp, dt) in (
                    ("ag_in", (D, T), BF16), ("ag_out", (4 * D, T), BF16),
                    ("xdbl_part", (SEQ, 96), F32), ("xdbl_sum", (SEQ, 96), F32),
                    ("z_sp", (NDC, 128, SEQ), BF16), ("xcv_sp", (NDC, 128, SEQ), BF16),
                    ("yprt", (SEQ, D), F32), ("rs_out", (T, D), F32)):
                aps[nm] = nc.dram_tensor(nm, list(shp), dt).ap()
            emit_mamba(P, aps, x_ready)
        elif part == "moe":
            rw = di("L0_ffn_router_w", (NE, D))
            W1 = di("L0_ffn_W1", (NE, D, 4096))
            b1 = di("L0_ffn_b1", (NE, 4096))
            W2 = di("L0_ffn_W2", (NE, 4096, D))
            b2 = di("L0_ffn_b2", (NE, D))
            emit_moe_prep(P, rw, b1, b2)
            v_done, s_done = emit_ln(P, x_ready=x_ready)
            xf_ready = [("v", v_done), ("s", s_done)]
            rv = emit_moe_router(P, xf_ready)
            emit_moe_bias(P, rv)
            emit_moe_experts(P, W1, W2, xf_ready)
        else:
            raise NotImplementedError(part)
        x_ready = [("v", P.cnt["v"]), ("s", P.cnt["s"])]

    emit_store_x(P, xo)
    if debug_dump:
        for name in debug_dump:
            tile = P.tiles[name]
            shp = [tile.shape[0], tile.shape[1]]
            dt = tile.dtype
            o = nc.dram_tensor("dbg_" + name, shp, dt, kind="ExternalOutput").ap()
            P.op("sync", lambda e, o=o, tile=tile: e.dma_start(out=o, in_=tile[:]),
                 incs=[("st", 16)])
        P.wait("sync", [("st", P.cnt["st"])])
    P.finalize()
    es.close()
    return nc


IDENT_BF = np.eye(128, dtype=np.float32).astype(BF)
ONES_BF = np.ones((1, 128), dtype=np.float32).astype(BF)
IDENT_F32 = np.eye(128, dtype=np.float32)
BASIS16 = np.zeros((16, 2048), np.float32)
for _s in range(16):
    BASIS16[_s, _s * 128:(_s + 1) * 128] = 1.0
BASIS16 = BASIS16.astype(BF)


def _col128(v):
    return np.ascontiguousarray(v.reshape(-1, 128).T)


def shard_inputs(inputs, parts=()):
    x = np.ascontiguousarray(np.asarray(inputs["L0_attn_x"]))
    maps = []
    if "mamba" in parts:
        winT = np.asarray(inputs["L0_attn_in_proj_w"]).T
        opT = np.asarray(inputs["L0_attn_out_proj_w"]).T
        xpwT = np.asarray(inputs["L0_attn_x_proj_w"]).T
        dtwT = np.asarray(inputs["L0_attn_dt_proj_w"]).T
        alog = np.asarray(inputs["L0_attn_A_log"])
        cwv = np.asarray(inputs["L0_attn_conv_w"])[:, 0, :]
        cbv = np.asarray(inputs["L0_attn_conv_b"])
        Dv = np.asarray(inputs["L0_attn_D"])
        dtbv = np.asarray(inputs["L0_attn_dt_proj_b"])
        lnw = np.asarray(inputs["L0_attn_ln_w"])
        lnb = np.asarray(inputs["L0_attn_ln_b"])
    for c in range(8):
        b, s = c // 4, c % 4
        m = {"x": np.ascontiguousarray(x[b, s * T:(s + 1) * T]),
             "ident": IDENT_BF, "onesbf": ONES_BF, "ident32": IDENT_F32}
        if "moe" in parts or "l1ffn" in parts or "l1attn" in parts:
            for kk in inputs:
                if kk.startswith("L0_ffn_") or kk.startswith("L1_"):
                    m[kk] = np.ascontiguousarray(np.asarray(inputs[kk]))
        if "l1attn" in parts:
            m["a_ln1g"] = _col128(np.asarray(inputs["L1_attn_ln1_g"]))
            m["a_ln1b"] = _col128(np.asarray(inputs["L1_attn_ln1_b"]))
            m["a_ln2g"] = _col128(np.asarray(inputs["L1_attn_ln2_g"]))
            m["a_ln2b"] = _col128(np.asarray(inputs["L1_attn_ln2_b"]))
            m["a_bff1"] = _col128(np.asarray(inputs["L1_attn_b_ff1"]))
        if "mamba" in parts:
            r = s
            sl = slice(r * DI, (r + 1) * DI)
            m["m_winT"] = np.ascontiguousarray(
                np.concatenate([winT[:, sl], winT[:, 2048 + r * DI:2048 + (r + 1) * DI]], axis=1))
            m["m_opT"] = np.ascontiguousarray(opT[sl])
            m["m_xpwT"] = np.ascontiguousarray(xpwT[sl])
            m["m_dtwT"] = np.ascontiguousarray(dtwT[:, sl])
            m["basis16"] = BASIS16
            m["m_Alog"] = np.ascontiguousarray(
                alog[sl].reshape(NDC, 128, DST).transpose(1, 0, 2).reshape(128, NDC * DST))
            m["m_convw"] = np.ascontiguousarray(
                cwv[sl].reshape(NDC, 128, 4).transpose(1, 0, 2).reshape(128, NDC * 4))
            m["m_convb"] = _col128(cbv[sl])
            m["m_D"] = _col128(Dv[sl])
            m["m_dtb"] = _col128(dtbv[sl])
            m["m_lnw"] = _col128(lnw)
            m["m_lnb"] = _col128(lnb)
        maps.append(m)
    return maps


_PROGS = {}


def _prog(part):
    if part not in _PROGS:
        _PROGS[part] = build_chain([part])
    return _PROGS[part]


def _input_names(nc):
    import concourse.mybir as mb
    names = set()
    for alloc in nc.m.functions[0].allocations:
        if isinstance(alloc, mb.MemoryLocationSet) and alloc.kind == "ExternalInput":
            names.add(alloc.memorylocations[0].name)
    return names


def kernel(**inputs):
    x = np.asarray(inputs["L0_attn_x"])
    chain = ["moe", "mamba", "moe", "l1ffn", "l1attn", "l1ffn"]
    cur = {"L0_attn_x": x}
    cur.update({kk: v for kk, v in inputs.items() if kk != "L0_attn_x"})
    for part in chain:
        nc = _prog(part)
        names = _input_names(nc)
        maps = shard_inputs(cur, parts=[part])
        maps = [{kk: v for kk, v in m.items() if kk in names} for m in maps]
        res = run_bass_kernel_spmd(nc, maps, core_ids=list(range(8)))
        xo = np.zeros((2, 8192, 1024), np.float32)
        for c in range(8):
            b, s = c // 4, c % 4
            xo[b, s * T:(s + 1) * T] = res.results[c]["xo"]
        cur["L0_attn_x"] = xo
    return cur["L0_attn_x"]

